# version 24
# baseline (speedup 1.0000x reference)
import math
import numpy as np

import concourse.bass as bass
import concourse.mybir as mybir
from concourse.bass_utils import run_bass_kernel_spmd

F32 = mybir.dt.float32
BF16 = mybir.dt.bfloat16
FP8 = mybir.dt.float8e4
AX = mybir.AxisListType
ALU = mybir.AluOpType
ACTF = mybir.ActivationFunctionType

NEG_MASK = -1.0e33
EPS = 1.0e-5


class Cfg:
    def __init__(self, D=2048, S=2048, H=None):
        self.D = D
        self.S = S
        self.H = H if H is not None else 4 * D
        self.SO = S // 2
        self.DC = D // 128
        self.QT = self.SO // 128
        self.KT = S // 128
        self.HC = self.H // 128
        self.SLAB = min(512, self.H, D, self.SO)
        self.DSL = D // self.SLAB
        self.HSL = self.H // self.SLAB
        self.SSL = S // self.SLAB
        self.OSL = self.SO // self.SLAB
        self.QG = min(2, self.QT)
        self.NQG = self.QT // self.QG
        self.NH4 = self.H // 512
        self.W6 = 512
        self.NHW = self.H // self.W6
        self.scale = 1.0 / math.sqrt(float(D))
        self.debug = False
        assert self.SLAB == 512, "psum accumulation slabs must be bank-aligned"
        assert D % 128 == 0 and self.SO % 128 == 0 and self.H % self.SLAB == 0
        assert D % self.SLAB == 0 and self.SO % self.SLAB == 0
        assert self.NH4 % 2 == 0 and self.QT % self.QG == 0
        assert self.NHW % 4 == 0 and self.W6 % 512 == 0


class Sem:
    def __init__(self, h):
        self.h = h
        self.n = 0

    def inc(self, instr, k=1):
        instr.then_inc(self.h, k)
        self.n += k
        return self.n


def build_program(c: Cfg):
    nc = bass.Bass("TRN2", target_bir_lowering=False)
    D, S, H, SO = c.D, c.S, c.H, c.SO
    DC, QT, KT, HC = c.DC, c.QT, c.KT, c.HC
    SLAB, DSL, HSL, OSL = c.SLAB, c.DSL, c.HSL, c.OSL

    din = lambda n, sh, dt=F32: nc.dram_tensor(n, sh, dt, kind="ExternalInput").ap()
    x_hbm = din("x", [S, D])
    qkv_hbm = din("qkv", [D, 3 * D])
    qkvc_hbm = din("qkvc", [128, (3 * D // 128) * DC * 128])
    wob_hbm = din("wob", [128, DSL * DC * SLAB])
    gamma_hbm = din("gamma", [D])
    beta_hbm = din("beta", [D])
    gwT_hbm = din("gwT", [D, H])
    vwT_hbm = din("vwT", [D, H])
    owT_hbm = din("owT", [H, D])
    gb_hbm = din("gate_b", [H])
    vb_hbm = din("val_b", [H])
    ob_hbm = din("out_b", [D])
    gbbt_hbm = din("gbbt", [128, H])
    vbbt_hbm = din("vbbt", [128, H])
    obbt_hbm = din("obbt", [128, D])
    mko_hbm = din("mask_own", [128, 128])
    mkp_hbm = din("mask_peer", [128, 128])
    eye32_hbm = din("eye32", [128, 128])
    eye16_hbm = din("eye16", [128, 128], BF16)
    out_hbm = nc.dram_tensor("out", [SO, D], F32, kind="ExternalOutput").ap()

    dint = lambda n, sh, dt=F32: nc.dram_tensor(n, sh, dt, kind="Internal").ap()
    hn_sp = dint("hn_sp", [KT, 128, DC * 128])
    k_sp = dint("k_sp", [128, KT, DC * 128])
    v_sp = dint("v_sp", [128, DC, KT * 128])
    q_sp = dint("q_sp", [DC, 128, SO])
    h_sp = dint("h_sp", [SO, D])
    x1_sp = dint("x1_sp", [SO, D])
    wgi_sp = dint("wgi_sp", [HSL, 128, DC * SLAB], FP8)
    wvi_sp = dint("wvi_sp", [HSL, 128, DC * SLAB], FP8)
    woi_sp = dint("woi_sp", [128, HC * D], FP8)
    x2i_sp = dint("x2i_sp", [QT, 128, H], BF16)
    if c.debug:
        qa_l = [SO - 128 * (m % QT) for m in range(KT)]
        dbg_ptp = dint("dbg_ptp", [128, int(sum(qa_l))])
        dbg_l = dint("dbg_l", [128, QT])
        dbg_negM = dint("dbg_negM", [128, SO])
        dbg_ofm = dint("dbg_ofm", [128, DC * SO])
        dbg_ao = dint("dbg_ao", [128, QT * D])

    te, ve, ac, gp, sy = nc.tensor, nc.vector, nc.scalar, nc.gpsimd, nc.sync

    sem_cms = []

    def mksem(name):
        cm = nc.semaphore(name)
        h = cm.__enter__()
        sem_cms.append(cm)
        return Sem(h)

    sL = mksem("sL")
    sX = [mksem("sX0"), mksem("sX1")]
    sW = [mksem("sW0"), mksem("sW1")]
    sH = [mksem("sH0"), mksem("sH1")]
    sU = [mksem("sU0"), mksem("sU1")]
    sK = mksem("sK")
    sWS = mksem("sWS")
    sQW = [mksem("sQW0"), mksem("sQW1"), mksem("sQW2"), mksem("sQW3")]
    sA5 = mksem("sA5")
    sV5 = mksem("sV5")
    sS = mksem("sS")
    sP = mksem("sP")
    sP6 = [mksem("sP6a"), mksem("sP6b")]
    sKV = mksem("sKV")
    sT = mksem("sT")
    sV = mksem("sV")
    sA = mksem("sA")
    sG = mksem("sG")
    sO = mksem("sO")
    sB = mksem("sB")

    def barrier(full=True):
        drains = [sS, sL, sO, sK, sWS, sKV, sX[0], sX[1], sW[0], sW[1],
                  sH[0], sH[1], sU[0], sU[1]]
        if full:
            drains += [sP] + sP6 + sQW
        for s_ in drains:
            sy.wait_ge(s_.h, s_.n)
        gp.wait_ge(sG.h, sG.n)
        engs = [te, ve, ac, gp, sy]
        for e in engs:
            sB.inc(e.drain() if e is not sy else e.nop())
        for e in engs:
            e.wait_ge(sB.h, sB.n)

    live = {}

    def alloc(name, sh, dt=F32):
        cm = nc.sbuf_tensor(name, sh, dt)
        h = cm.__enter__()
        live[name] = cm
        return h

    def alloc_r(name, sh, dt=F32):
        cm = nc.sbuf_tensor(name, sh, dt, side="right")
        h = cm.__enter__()
        live[name] = cm
        return h

    def free(*names):
        for n in names:
            live.pop(n).__exit__(None, None, None)

    psum_live = {}

    def palloc(name, sh, dt=F32):
        cm = nc.psum_tensor(name, sh, dt)
        h = cm.__enter__()
        psum_live[name] = cm
        return h

    def pfree(*names):
        for n in names:
            psum_live.pop(n).__exit__(None, None, None)

    ones = alloc("ones", [128, 1])
    ones16 = alloc("ones16", [128, 1], BF16)
    ones_row = alloc("ones_row", [1, 128])
    c127 = alloc("c127", [128, 1])
    eye32 = alloc("eye32s", [128, 128])
    eye16 = alloc("eye16s", [128, 128], BF16)
    mask_own = alloc("mask_own_s", [128, 128])
    mask_peer = alloc("mask_peer_s", [128, 128])
    nstat = 6 * ((D + 511) // 512)
    stat = alloc("stat", [128, nstat])
    agg = alloc("agg", [128, 2])
    vep = alloc("vep", [128, 1])
    rr0 = alloc("rr0", [128, 1])
    tt0 = alloc("tt0", [128, 1])
    tt1 = alloc("tt1", [128, 1])
    recip_l = alloc("recip_l", [128, QT])
    gxcol = alloc("gxcol", [128, QT])
    ggvcol = alloc("ggvcol", [128, QT])
    sc_x = alloc("sc_x", [128, QT])
    sc_gv = alloc("sc_gv", [128, QT])
    scg_t = alloc("scg_t", [128, QT])
    scv_t = alloc("scv_t", [128, QT])
    sc3 = alloc("sc3", [128, QT])
    gwpart = alloc("gwpart", [128, 1])
    chalf = alloc("chalf", [128, 2])
    gwsum = alloc("gwsum", [128, 3])
    gwtot_sb = alloc("gwtot_sb", [1, 3])
    gwrec_sb = alloc("gwrec_sb", [1, 3])
    rgam = alloc("rgam", [128, 3])
    gamw = alloc("gamw", [128, 3])
    qn_row = alloc("qn_row", [1, SO])
    maxkn = alloc("maxkn", [1, 1])
    negM = alloc("negM", [128, SO])
    negMr = alloc("negMr", [1, SO])

    def ln_stats(x_ap, nfree):
        nchunks = (nfree + 511) // 512
        for i in range(nchunks):
            lo = i * 512
            hi = min(nfree, lo + 512)
            ve.bn_stats(stat[:, 6 * i:6 * (i + 1)], x_ap[:, lo:hi])
        ve.drain()
        ve.bn_aggr(agg[:], stat[:, :6 * nchunks])
        ve.drain()

    def ln_rstd():
        ve.tensor_scalar(vep[:], agg[:, 1:2], EPS, None, op0=ALU.add)
        sV.inc(ve.drain())
        ac.wait_ge(sV.h, sV.n)
        sA.inc(ac.activation(tt0[:], vep[:], ACTF.Sqrt))
        ve.wait_ge(sA.h, sA.n)
        ve.reciprocal(rr0[:], tt0[:])
        for _ in range(2):
            ve.drain()
            ve.tensor_tensor(tt0[:], rr0[:], rr0[:], op=ALU.mult)
            ve.drain()
            ve.tensor_tensor(tt0[:], tt0[:], vep[:], op=ALU.mult)
            ve.drain()
            ve.tensor_scalar(tt0[:], tt0[:], -0.5, 1.5, op0=ALU.mult, op1=ALU.add)
            ve.drain()
            ve.tensor_tensor(rr0[:], rr0[:], tt0[:], op=ALU.mult)
        ve.drain()
        ve.tensor_scalar(tt1[:], agg[:, 0:1], -1.0, rr0[:], op0=ALU.mult, op1=ALU.mult)
        sV.inc(ve.drain())

    def ln_apply(dst, src):
        ac.wait_ge(sV.h, sV.n)
        ac.drain()
        sA.inc(ac.activation(dst, src, ACTF.Identity, bias=tt1[:], scale=rr0[:]))

    def bcast(dst, src_row, n, ps, war_cell=None):
        for lo in range(0, n, 512):
            w = min(512, n - lo)
            te.wait_ge(sA.h, sA.n if war_cell is None else war_cell[0])
            sT.inc(te.matmul(ps[:, :w], ones_row[:1, :], src_row[:1, lo:lo + w],
                             start=True, stop=True))
            ac.wait_ge(sT.h, sT.n)
            sA.inc(ac.copy(dst[:, lo:lo + w], ps[:, :w]))
            if war_cell is not None:
                war_cell[0] = sA.n
        return sT.n

    def round_ladder(f_ap, sc_ap, out_ap, w_scr, u_scr, p_scr):
        ve.tensor_scalar(w_scr, f_ap, sc_ap, 128.5, op0=ALU.mult, op1=ALU.add)
        ve.drain()
        ve.tensor_copy(u_scr, w_scr)
        ve.drain()
        for j in range(7, -1, -1):
            k = float(1 << j)
            ve.tensor_scalar(p_scr, u_scr, k, None, op0=ALU.is_ge)
            ve.drain()
            ve.scalar_tensor_tensor(u_scr, p_scr, -k, u_scr, op0=ALU.mult, op1=ALU.add)
            ve.drain()
        ve.scalar_tensor_tensor(p_scr, u_scr, -1.0, w_scr, op0=ALU.mult, op1=ALU.add)
        ve.drain()
        ve.tensor_scalar(out_ap, p_scr, -128.0, None, op0=ALU.add)
        ve.drain()

    WCW = 512
    wqb = alloc_r("wqb", [128, 4 * WCW])
    wpb = alloc_r("wpb", [128, 2 * WCW])
    wmb = alloc_r("wmb", [128, 2 * WCW])
    wib = alloc_r("wib", [128, WCW], FP8)
    NCG = H // WCW

    def wchunks(src_hbm, mat_sp):
        out = []
        for dc in range(DC):
            for cg in range(NCG):
                def store(wib_, dc=dc, cg=cg):
                    s_lo = (cg * WCW) // SLAB
                    s_hi = ((cg + 1) * WCW) // SLAB
                    for s in range(s_lo, s_hi):
                        off = s * SLAB - cg * WCW
                        sP.inc(gp.dma_start(
                            out=mat_sp[s][:, dc * SLAB:(dc + 1) * SLAB],
                            in_=wib_[:, off:off + SLAB]), 16)
                out.append((src_hbm[dc * 128:(dc + 1) * 128, cg * WCW:(cg + 1) * WCW],
                            WCW, store))
        return out

    def owt_chunks():
        out = []
        for ch in range(HC):
            for hf in range(D // WCW):
                def store(wib_, ch=ch, hf=hf):
                    sP.inc(gp.dma_start(
                        out=woi_sp[:, ch * D + hf * WCW: ch * D + (hf + 1) * WCW],
                        in_=wib_[:, :WCW]), 16)
                out.append((owT_hbm[ch * 128:(ch + 1) * 128,
                                    hf * WCW:(hf + 1) * WCW], WCW, store))
        return out

    wq_us = [0, 0, 0, 0]

    class Pump:
        DEPTH = 4

        def __init__(self, stages):
            self.items = []
            self.nslot = 0
            for compute, idx, chunks in stages:
                if callable(chunks):
                    self.items.append((compute, idx, False, None, None))
                    continue
                for ci, ck in enumerate(chunks):
                    self.items.append((compute, idx, ci == 0, ck, self.nslot))
                    self.nslot += 1
            self.n = len(self.items)
            self.cursor = 0
            self.loaded = 0
            self.lds = {}

        def _load(self):
            i = self.loaded
            item = self.items[i]
            if item[3] is not None:
                compute, idx, first, (src_, ncols, store_fn), sl = item
                b = sl % 4
                gp.wait_ge(sA5.h, wq_us[b])
                self.lds[i] = sQW[b].inc(
                    gp.dma_start(out=wqb[:, b * WCW:b * WCW + ncols], in_=src_), 16)
            self.loaded += 1

        def pump(self, k):
            while k > 0 and self.cursor < self.n:
                while self.loaded < min(self.cursor + self.DEPTH, self.n):
                    self._load()
                compute, idx, first, ck, sl = self.items[self.cursor]
                if ck is None:
                    compute(idx)
                else:
                    self.compute_chunk(compute, self.cursor, idx, first, ck, sl)
                self.cursor += 1
                k -= 1

        def compute_chunk(self, compute, i, idx, first, ck, sl):
            src_, ncols, store_fn = ck
            compute(self, i, idx, first, ncols, store_fn, sl)

        def drain(self):
            self.pump(self.n)

    def gamma_compute(pp, i, idx, first, ncols, store_fn, sl):
        b = sl % 4
        wb_ = wqb[:, b * WCW:b * WCW + ncols]
        if first:
            ve.wait_ge(sA5.h, sA5.n)
            ve.memset(gwsum[:, idx:idx + 1], 0.0)
            sV5.inc(ve.drain())
            ac.wait_ge(sV5.h, sV5.n)
        ac.wait_ge(sQW[b].h, pp.lds[i])
        ac.wait_ge(sV5.h, sV5.n)
        ac.drain()
        sA5.inc(ac.activation(wpb[:, (sl % 2) * WCW:(sl % 2) * WCW + ncols], wb_,
                              ACTF.Abs, accum_out=gwpart[:]))
        wq_us[b] = sA5.n
        ve.wait_ge(sA5.h, sA5.n)
        ve.tensor_tensor(gwsum[:, idx:idx + 1], gwsum[:, idx:idx + 1],
                         gwpart[:], op=ALU.add)
        sV5.inc(ve.drain())

    def quant_compute(pp, i, idx, first, ncols, store_fn, sl):
        b = sl % 4
        wb_ = wqb[:, b * WCW:b * WCW + ncols]
        wpb_ = wpb[:, (sl % 2) * WCW:(sl % 2) * WCW + ncols]
        wmb_ = wmb[:, (sl % 2) * WCW:(sl % 2) * WCW + ncols]
        if not hasattr(pp, "sv2_tok"):
            pp.sv2_tok = [0, 0]
        ac.wait_ge(sQW[b].h, pp.lds[i])
        ac.wait_ge(sV5.h, pp.sv2_tok[sl % 2])
        sA5.inc(ac.activation(wpb_, wb_, ACTF.Sign,
                              bias=chalf[:, 0:1], scale=rgam[:, idx:idx + 1]))
        sA5.inc(ac.activation(wmb_, wb_, ACTF.Sign,
                              bias=chalf[:, 1:2], scale=rgam[:, idx:idx + 1]))
        wq_us[b] = sA5.n
        ve.wait_ge(sA5.h, sA5.n)
        ve.wait_ge(sP.h, sP.n)
        ve.tensor_tensor(wib[:, :ncols], wpb_, wmb_, op=ALU.add)
        pp.sv2_tok[sl % 2] = sV5.inc(ve.drain())
        gp.wait_ge(sV5.h, sV5.n)
        store_fn(wib)

    def wquant_bcast(idx):
        n_elems = D * H if idx < 2 else H * D
        te.wait_ge(sV5.h, sV5.n)
        gps = palloc("gwps" + str(idx), [1, 1])
        mm_tok = sT.inc(te.matmul(gps[:1, :1], gwsum[:, idx:idx + 1], ones[:],
                                  start=True, stop=True))
        ve.wait_ge(sT.h, mm_tok)
        ve.tensor_scalar(gwtot_sb[:1, idx:idx + 1], gps[:1, :1], 1.0 / float(n_elems),
                         1e-5, op0=ALU.mult, op1=ALU.max)
        ve.drain()
        ve.reciprocal(gwrec_sb[:1, idx:idx + 1], gwtot_sb[:1, idx:idx + 1])
        sV5.inc(ve.drain())
        pfree("gwps" + str(idx))
        bc_psw = palloc("bc_psw" + str(idx), [128, 512])
        te.wait_ge(sV5.h, sV5.n)
        cell = [0]
        bcast(rgam[:, idx:idx + 1], gwrec_sb[:1, idx:idx + 1], 1, bc_psw,
              war_cell=cell)
        bcast(gamw[:, idx:idx + 1], gwtot_sb[:1, idx:idx + 1], 1, bc_psw,
              war_cell=cell)
        ac.drain()
        pfree("bc_psw" + str(idx))

    wpump = Pump([
        (gamma_compute, 0, wchunks(gwT_hbm, wgi_sp)),
        (wquant_bcast, 0, wquant_bcast),
        (gamma_compute, 1, wchunks(vwT_hbm, wvi_sp)),
        (wquant_bcast, 1, wquant_bcast),
        (quant_compute, 0, wchunks(gwT_hbm, wgi_sp)),
        (gamma_compute, 2, owt_chunks()),
        (wquant_bcast, 2, wquant_bcast),
        (quant_compute, 1, wchunks(vwT_hbm, wvi_sp)),
        (quant_compute, 2, owt_chunks()),
    ])

    gam_b = alloc("gam_b", [128, D])
    bet_b = alloc("bet_b", [128, D])
    sL.inc(sy.dma_start(out=eye32[:], in_=eye32_hbm), 16)
    sL.inc(sy.dma_start(out=eye16[:], in_=eye16_hbm), 16)
    sL.inc(sy.dma_start(out=mask_own[:], in_=mko_hbm), 16)
    sL.inc(sy.dma_start(out=mask_peer[:], in_=mkp_hbm), 16)
    sL.inc(sy.dma_start(out=gam_b[:1, :], in_=gamma_hbm[None, :]), 16)
    sL.inc(sy.dma_start(out=bet_b[:1, :], in_=beta_hbm[None, :]), 16)
    ve.memset(ones[:], 1.0)
    ve.memset(ones16[:], 1.0)
    ve.memset(ones_row[:], 1.0)
    ve.memset(c127[:], 127.0)
    ve.memset(chalf[:, 0:1], -0.5)
    ve.memset(chalf[:, 1:2], 0.5)
    sV.inc(ve.drain())
    bc_ps0 = palloc("bc_ps0", [128, 512])
    te.wait_ge(sL.h, sL.n)
    te.wait_ge(sV.h, sV.n)
    bcast(gam_b, gam_b[:1, :], D, bc_ps0)
    bcast(bet_b, bet_b[:1, :], D, bc_ps0)
    pfree("bc_ps0")

    hnf = alloc("hnf", [128, DC * S])
    xt = alloc("xt", [128, 2 * D])
    ht = alloc("ht", [128, D])
    zt = alloc("zt", [128, D])
    ptr = palloc("ptr", [128, 1024])
    ve.wait_ge(sA.h, sA.n)
    evac_tok = [0, 0]
    tile_done = [0, 0]
    for i in range(KT):
        b = i % 2
        xb = xt[:, b * D:b * D + D]
        sy.wait_ge(sT.h, tile_done[b])
        ld = sX[b].inc(sy.dma_start(out=xb, in_=x_hbm[i * 128:(i + 1) * 128, :]), 16)
        ve.wait_ge(sX[b].h, ld)
        ln_stats(xb, D)
        ln_rstd()
        ac.wait_ge(sS.h, sS.n)
        ln_apply(ht[:], xb)
        if i < QT:
            sy.wait_ge(sA.h, sA.n)
            sS.inc(sy.dma_start(out=h_sp[i * 128:(i + 1) * 128, :], in_=ht[:]), 16)
        ve.wait_ge(sA.h, sA.n)
        ln_stats(ht[:], D)
        ln_rstd()
        ln_apply(zt[:], ht[:])
        ve.wait_ge(sA.h, sA.n)
        ve.tensor_tensor(zt[:], zt[:], gam_b[:], op=ALU.mult)
        ve.drain()
        ve.tensor_tensor(zt[:], zt[:], bet_b[:], op=ALU.add)
        sV.inc(ve.drain())
        te.wait_ge(sV.h, sV.n)
        for dc in range(DC):
            pb = ptr[:, (dc % 2) * 512:(dc % 2) * 512 + 128]
            te.wait_ge(sA.h, evac_tok[dc % 2])
            sT.inc(te.transpose(pb, zt[:, dc * 128:(dc + 1) * 128], eye32[:]))
            ac.wait_ge(sT.h, sT.n)
            dst = hnf[:, dc * S + i * 128: dc * S + i * 128 + 128]
            evac_tok[dc % 2] = sA.inc(ac.copy(dst, pb))
            gp.wait_ge(sA.h, sA.n)
            sKV.inc(gp.dma_start(out=hn_sp[i][:, dc * 128:(dc + 1) * 128],
                                 in_=dst), 16)
        tile_done[b] = sT.n
        ve.wait_ge(sT.h, sT.n)
        ac.wait_ge(sT.h, sT.n)
    free("zt", "ht", "xt")
    pfree("ptr")
    barrier()

    wb = alloc("wb", [128, 2 * DC * 128])
    ksb = alloc("ksb", [128, S])
    ksq = alloc("ksq", [128, SLAB])
    nsb = alloc("nsb", [1, S], BF16)
    kp = palloc("kp", [128, S])
    npp = palloc("npsum", [1, 2 * SLAB])

    def proj(base_col, ntok, store, pump_hook=lambda: None):
        NG = ntok // SLAB
        wtok = [0, 0]
        copy_tok = [0]
        for o in range(DC):
            ob = base_col // 128 + o
            bb = o % 2
            wbo = wb[:, bb * DC * 128:(bb + 1) * DC * 128]
            sy.wait_ge(sT.h, wtok[bb])
            ld = sW[bb].inc(sy.dma_start(
                out=wbo, in_=qkvc_hbm[:, ob * DC * 128:(ob + 1) * DC * 128]), 16)
            te.wait_ge(sW[bb].h, ld)
            te.wait_ge(sV.h, copy_tok[0])
            for dc in range(DC):
                wbb = wbo[:, dc * 128:(dc + 1) * 128]
                for g in range(NG):
                    hsrc = hnf[:, dc * S + g * SLAB: dc * S + (g + 1) * SLAB]
                    sT.inc(te.matmul(kp[:, g * SLAB:(g + 1) * SLAB], wbb, hsrc,
                                     start=(dc == 0), stop=(dc == DC - 1)))
            wtok[bb] = sT.n
            ve.wait_ge(sT.h, sT.n)
            ve.wait_ge(sS.h, sS.n)
            ve.wait_ge(sKV.h, sKV.n)
            ve.tensor_copy(ksb[:, :ntok], kp[:, :ntok])
            copy_tok[0] = sV.inc(ve.drain())
            sy.wait_ge(sV.h, sV.n)
            gp.wait_ge(sV.h, sV.n)
            store(o)
            if o == 0:
                ve.memset(nsb[:1, :ntok], 0.0)
                sV.inc(ve.drain())
            for g in range(NG):
                gb_ = g % 2
                ve.wait_ge(sT.h, sT.n)
                ve.tensor_tensor(ksq[:], ksb[:, g * SLAB:(g + 1) * SLAB],
                                 ksb[:, g * SLAB:(g + 1) * SLAB], op=ALU.mult)
                sV.inc(ve.drain())
                te.wait_ge(sV.h, sV.n)
                nmm = sT.inc(te.matmul(npp[:1, gb_ * SLAB:(gb_ + 1) * SLAB],
                                       ones[:], ksq[:], start=True, stop=True))
                ve.wait_ge(sT.h, nmm)
                ve.tensor_tensor(nsb[:1, g * SLAB:(g + 1) * SLAB],
                                 nsb[:1, g * SLAB:(g + 1) * SLAB],
                                 npp[:1, gb_ * SLAB:(gb_ + 1) * SLAB], op=ALU.add)
                sV.inc(ve.drain())
            pump_hook()

    def store_k(o):
        sKV.inc(gp.dma_start(out=k_sp[:, :, o * 128:(o + 1) * 128],
                             in_=ksb[:, :S]), 16)

    proj(D, S, store_k, lambda: wpump.pump(25))
    ve.wait_ge(sT.h, sT.n)
    ve.tensor_reduce(maxkn[:], nsb[:1, :S], axis=AX.X, op=ALU.max)
    ve.drain()
    ve.tensor_scalar(maxkn[:], maxkn[:], 1e-30, None, op0=ALU.max)
    sV.inc(ve.drain())

    def store_q(o):
        sS.inc(sy.dma_start(out=q_sp[o], in_=ksb[:, :SO]), 16)

    proj(0, SO, store_q, lambda: wpump.pump(25))
    ve.wait_ge(sT.h, sT.n)
    ve.tensor_copy(qn_row[:], nsb[:1, :SO])
    ve.drain()
    ve.tensor_scalar(qn_row[:], qn_row[:], 1e-30, None, op0=ALU.max)
    sV.inc(ve.drain())
    ac.wait_ge(sV.h, sV.n)
    sA.inc(ac.activation(qn_row[:], qn_row[:], ACTF.Sqrt))
    sA.inc(ac.activation(maxkn[:], maxkn[:], ACTF.Sqrt))
    ve.wait_ge(sA.h, sA.n)
    ve.tensor_scalar(negMr[:], qn_row[:], maxkn[:1, :1], -1.0, op0=ALU.mult, op1=ALU.mult)
    sV.inc(ve.drain())
    pfree("npsum", "kp")
    free("nsb")
    bc_ps1 = palloc("bc_ps1", [128, 512])
    te.wait_ge(sV.h, sV.n)
    bcast(negM, negMr, SO, bc_ps1)
    pfree("bc_ps1")
    free("ksq", "ksb", "wb")
    free("hnf", "bet_b", "gam_b")
    free("negMr")
    barrier(full=False)

    wv_sb = alloc("wv_sb", [128, DC * D])
    hbuf = alloc("hbuf", [128, 2 * DC * 128])
    vsb = alloc("vsb", [128, D])
    vp = palloc("vp", [128, 1024])
    for dc in range(DC):
        sL.inc(sy.dma_start(out=wv_sb[:, dc * D:(dc + 1) * D],
                            in_=qkv_hbm[dc * 128:(dc + 1) * 128, 2 * D:3 * D]), 16)
    te.wait_ge(sL.h, sL.n)
    huse = [0, 0]
    vevac = [0, 0]
    for m in range(KT):
        b = m % 2
        hb = hbuf[:, b * DC * 128:(b + 1) * DC * 128]
        sy.wait_ge(sT.h, huse[b])
        sH[b].inc(sy.dma_start(out=hb, in_=hn_sp[m]), 16)
        te.wait_ge(sH[b].h, sH[b].n)
        if m > 0:
            te.wait_ge(sKV.h, sKV.n)
        for ds in range(DSL):
            pbk = vp[:, (ds % 2) * 512:(ds % 2) * 512 + SLAB]
            te.wait_ge(sV.h, vevac[ds % 2])
            for dc in range(DC):
                sT.inc(te.matmul(pbk, hb[:, dc * 128:(dc + 1) * 128],
                                 wv_sb[:, dc * D + ds * SLAB: dc * D + (ds + 1) * SLAB],
                                 start=(dc == 0), stop=(dc == DC - 1)))
            ve.wait_ge(sT.h, sT.n)
            ve.tensor_copy(vsb[:, ds * SLAB:(ds + 1) * SLAB], pbk)
            vevac[ds % 2] = sV.inc(ve.drain())
        huse[b] = sT.n
        gp.wait_ge(sV.h, sV.n)
        sKV.inc(gp.dma_start(out=v_sp[:, :, m * 128:(m + 1) * 128], in_=vsb[:]), 16)
        wpump.pump(24)
    free("vsb", "hbuf", "wv_sb")
    pfree("vp")
    barrier(full=False)

    ofm = alloc("ofm", [128, DC * SO])
    _qa_list = [SO - 128 * (m % QT) for m in range(KT)]
    ptp = alloc("ptp", [128, int(sum(_qa_list))])
    qres = alloc("qres", [128, DC * SO])
    kbuf = alloc("kbuf", [128, 2 * DC * 128])
    qa_list = _qa_list
    PTW = int(sum(qa_list))
    poff = np.cumsum([0] + qa_list)[:-1].tolist()
    sp_ps = palloc("sp_ps", [128, 2 * SO])
    l_ps = palloc("l_ps", [128, QT])
    for o in range(DC):
        sL.inc(sy.dma_start(out=qres[:, o * SO:(o + 1) * SO], in_=q_sp[o]), 16)
    te.wait_ge(sL.h, sL.n)
    kuse = [0, 0]
    pset_tok = [0, 0]
    ldk_tok = [0, 0]
    first_l = True
    ldk_tok[0] = sX[0].inc(sy.dma_start(out=kbuf[:, :DC * 128], in_=k_sp[:, 0, :]), 16)
    for m in range(KT):
        b = m % 2
        kb = kbuf[:, b * DC * 128:(b + 1) * DC * 128]
        if m + 1 < KT:
            b2 = (m + 1) % 2
            sy.wait_ge(sT.h, kuse[b2])
            ldk_tok[b2] = sX[b2].inc(sy.dma_start(
                out=kbuf[:, b2 * DC * 128:(b2 + 1) * DC * 128],
                in_=k_sp[:, m + 1, :]), 16)
        qoff = 128 * (m % QT)
        qa = SO - qoff
        te.wait_ge(sX[b].h, ldk_tok[b])
        te.wait_ge(sV.h, pset_tok[b])
        base = b * SO
        nseg = (qa + SLAB - 1) // SLAB
        for seg in range(nseg):
            lo = seg * SLAB
            hi = min(qa, lo + SLAB)
            for dc in range(DC):
                sT.inc(te.matmul(
                    sp_ps[:, base + lo:base + hi],
                    kb[:, dc * 128:(dc + 1) * 128],
                    qres[:, dc * SO + qoff + lo: dc * SO + qoff + hi],
                    start=(dc == 0), stop=(dc == DC - 1)))
        kuse[b] = sT.n
        wpump.pump(5)
        pm = ptp[:, poff[m]:poff[m] + qa]
        ve.wait_ge(sT.h, sT.n)
        ve.tensor_tensor(pm, sp_ps[:, base:base + qa], negM[:, qoff:SO], op=ALU.add)
        sV.inc(ve.drain())
        pset_tok[b] = sV.n
        mk = mask_own if m < QT else mask_peer
        ve.tensor_tensor(ptp[:, poff[m]:poff[m] + 128],
                         ptp[:, poff[m]:poff[m] + 128], mk[:], op=ALU.add)
        sV.inc(ve.drain())
        ac.wait_ge(sV.h, sV.n)
        sA.inc(ac.activation(pm, pm, ACTF.Exp, scale=c.scale))
        te.wait_ge(sA.h, sA.n)
        for jj in range(qa // 128):
            col = qoff // 128 + jj
            sT.inc(te.matmul(l_ps[:, col:col + 1],
                             ptp[:, poff[m] + jj * 128: poff[m] + (jj + 1) * 128],
                             ones[:], start=first_l, stop=(m == KT - 1),
                             skip_group_check=True))
            first_l = False
        wpump.pump(5)
    ve.wait_ge(sT.h, sT.n)
    ve.reciprocal(recip_l[:], l_ps[:])
    sV.inc(ve.drain())
    if c.debug:
        sy.wait_ge(sV.h, sV.n)
        sy.wait_ge(sA.h, sA.n)
        sS.inc(sy.dma_start(out=dbg_ptp[:], in_=ptp[:]), 16)
        sS.inc(sy.dma_start(out=dbg_l[:], in_=recip_l[:]), 16)
        sS.inc(sy.dma_start(out=dbg_negM[:], in_=negM[:]), 16)
    free("kbuf", "qres")
    pfree("l_ps", "sp_ps")
    barrier(full=False)

    vbuf = alloc("vbuf", [128, 2 * KT * 128])
    o_ps = palloc("o_ps", [128, 2 * SO])
    vuse = [0, 0]
    oset_tok = [0, 0]
    for cc in range(DC):
        b = cc % 2
        vb = vbuf[:, b * KT * 128:(b + 1) * KT * 128]
        sy.wait_ge(sT.h, vuse[b])
        ld = sU[b].inc(sy.dma_start(out=vb, in_=v_sp[:, cc, :]), 16)
        te.wait_ge(sU[b].h, ld)
        te.wait_ge(sV.h, oset_tok[b])
        base = b * SO
        for m in range(KT):
            qoff = 128 * (m % QT)
            for seg in range(OSL):
                lo = max(seg * SLAB, qoff)
                hi = (seg + 1) * SLAB
                if lo >= hi:
                    continue
                sT.inc(te.matmul(
                    o_ps[:, base + lo:base + hi],
                    vb[:, m * 128:(m + 1) * 128],
                    ptp[:, poff[m] + lo - qoff: poff[m] + hi - qoff],
                    start=(m == 0), stop=(m == KT - 1), skip_group_check=True))
        vuse[b] = sT.n
        ve.wait_ge(sT.h, sT.n)
        ve.tensor_copy(ofm[:, cc * SO:(cc + 1) * SO], o_ps[:, base:base + SO])
        oset_tok[b] = sV.inc(ve.drain())
        wpump.pump(20)
    if c.debug:
        sy.wait_ge(sA.h, sA.n)
        sS.inc(sy.dma_start(out=dbg_ofm[:], in_=ofm[:]), 16)
    free("vbuf")
    free("ptp")
    pfree("o_ps")
    barrier(full=False)

    ao = alloc("ao", [128, QT * D])
    wos = alloc("wos", [128, DC * SLAB])
    ao_ps = palloc("ao_ps", [128, 1024])
    aevac = [0, 0]
    for ds in range(DSL):
        te.wait_ge(sT.h, sT.n)
        sy.wait_ge(sT.h, sT.n)
        sWS.inc(sy.dma_start(
            out=wos[:], in_=wob_hbm[:, ds * DC * SLAB:(ds + 1) * DC * SLAB]), 16)
        te.wait_ge(sWS.h, sWS.n)
        for qc in range(QT):
            bk = qc % 2
            pbk = ao_ps[:, bk * 512:bk * 512 + SLAB]
            te.wait_ge(sV.h, aevac[bk])
            for dc in range(DC):
                sT.inc(te.matmul(pbk, ofm[:, dc * SO + qc * 128: dc * SO + (qc + 1) * 128],
                                 wos[:, dc * SLAB:(dc + 1) * SLAB],
                                 start=(dc == 0), stop=(dc == DC - 1)))
            ve.wait_ge(sT.h, sT.n)
            ve.tensor_scalar(ao[:, qc * D + ds * SLAB: qc * D + (ds + 1) * SLAB],
                             pbk, recip_l[:, qc:qc + 1], None, op0=ALU.mult)
            aevac[bk] = sV.inc(ve.drain())
    if c.debug:
        sy.wait_ge(sV.h, sV.n)
        sS.inc(sy.dma_start(out=dbg_ao[:], in_=ao[:]), 16)
        barrier()
    free("wos")
    pfree("ao_ps")
    hbt = alloc("hbt", [128, D])
    xbt = alloc("xbt", [128, D])
    z2t = alloc("z2t", [128, D])
    for qc in range(QT):
        sy.wait_ge(sV.h, sV.n)
        sL.inc(sy.dma_start(out=hbt[:], in_=h_sp[qc * 128:(qc + 1) * 128, :]), 16)
        ld = sL.inc(sy.dma_start(out=xbt[:], in_=x_hbm[qc * 128:(qc + 1) * 128, :]), 16)
        at = ao[:, qc * D:(qc + 1) * D]
        ve.wait_ge(sL.h, ld)
        ve.tensor_tensor(at, at, hbt[:], op=ALU.add)
        ve.drain()
        ln_stats(at, D)
        ln_rstd()
        ac.wait_ge(sS.h, sS.n)
        ln_apply(z2t[:], at)
        ve.wait_ge(sA.h, sA.n)
        ve.tensor_tensor(z2t[:], z2t[:], xbt[:], op=ALU.add)
        sV.inc(ve.drain())
        sy.wait_ge(sV.h, sV.n)
        sS.inc(sy.dma_start(out=x1_sp[qc * 128:(qc + 1) * 128, :], in_=z2t[:]), 16)
    wpump.drain()
    free("z2t", "xbt", "hbt", "ao", "ofm")
    free("wib", "wmb", "wpb", "wqb")
    barrier()

    free("negM", "maxkn", "qn_row")
    NSLOT = 3
    NH4 = c.NH4
    xiT = alloc("xiT", [128, DC * SO], BF16)
    ft = alloc("ft", [128, D])
    rs1 = alloc("rs1", [128, 1024])
    xit = alloc("xit", [128, D], BF16)
    wgsl = alloc("wgsl", [128, 2 * DC * SLAB], FP8)
    wvsl = alloc("wvsl", [128, 2 * DC * SLAB], FP8)
    gbb = alloc("gbb", [128, SLAB])
    vbb = alloc("vbb", [128, SLAB])
    gsl = alloc("gsl", [128, 2 * SLAB])
    vsl = alloc("vsl", [128, SLAB])
    sgn = alloc("sgn", [128, 4 * c.W6])
    ubuf = alloc("ubuf", [128, 4 * c.W6])
    scr16 = alloc("scr16", [128, 2 * c.W6], BF16)
    cm1275 = alloc("cm1275", [128, 1])
    c1285 = alloc("c1285", [128, 1])
    dsc = alloc("dsc", [128, 128])
    gvb = alloc("gvb", [128, NSLOT * H])
    pt16 = palloc("pt16", [128, 2048], BF16)
    gps2 = palloc("gps2", [128, 1024])
    vps2 = palloc("vps2", [128, 1024])
    ptq = palloc("ptq", [128, 1024])
    ve.memset(cm1275[:], -127.5)
    ve.memset(c1285[:], 128.5)
    ve.memset(ggvcol[:], 0.0)
    sV.inc(ve.drain())
    ac.wait_ge(sV.h, sV.n)

    slot_free = [None] * NSLOT
    wu = [0, 0]
    bias_rd = [0]
    silu_tok = [0, 0]
    pfree_tok = [0, 0]
    dsc_war = [0]
    evt6 = [0, 0]
    sgnwar = [0, 0, 0, 0]
    uwar = [0, 0, 0, 0]
    scr_tok = [0, 0]
    evt = [0, 0]
    xa_tok = [0] * QT
    ft_free = [0]

    def p6aA(qc):
        sy.wait_ge(sV.h, ft_free[0])
        ld = sX[0].inc(sy.dma_start(out=ft[:], in_=x1_sp[qc * 128:(qc + 1) * 128, :]), 16)
        ve.wait_ge(sX[0].h, ld)
        ln_stats(ft[:], D)
        ln_rstd()
        ln_apply(ft[:], ft[:])
        ve.wait_ge(sA.h, sA.n)
        ve.tensor_reduce(gxcol[:, qc:qc + 1], ft[:], axis=AX.X, op=ALU.max,
                         apply_absolute_value=True)
        ve.drain()
        ve.tensor_scalar(gxcol[:, qc:qc + 1], gxcol[:, qc:qc + 1], 1e-5, None, op0=ALU.max)
        ve.drain()
        ve.reciprocal(tt1[:], gxcol[:, qc:qc + 1])
        ve.drain()
        ve.tensor_scalar(sc_x[:, qc:qc + 1], tt1[:], 127.0, None, op0=ALU.mult)
        ve.drain()
        ve.tensor_scalar(tt0[:], gxcol[:, qc:qc + 1], 0.5 / 127.0, None, op0=ALU.mult)
        ve.drain()
        ve.tensor_tensor(scg_t[:, qc:qc + 1], tt0[:], gamw[:, 0:1], op=ALU.mult)
        ve.tensor_tensor(scv_t[:, qc:qc + 1], tt0[:], gamw[:, 1:2], op=ALU.mult)
        sV.inc(ve.drain())

    def p6aB(qc):
        HD = min(512, D // 2)
        wtok = sV.inc(ve.tensor_scalar(ft[:], ft[:], sc_x[:, qc:qc + 1], 128.5,
                                       op0=ALU.mult, op1=ALU.add))
        fin_tok = wtok
        for ps_ in range(D // (2 * HD)):
            ac.wait_ge(sV.h, fin_tok)
            ac.drain()
            up = {}
            for ci in range(2):
                sl_ = slice(ps_ * 2 * HD + ci * HD, ps_ * 2 * HD + (ci + 1) * HD)
                sA.inc(ac.activation(rs1[:, ci * HD:(ci + 1) * HD], ft[:, sl_],
                                     ACTF.Identity, bias=cm1275[:]))
                up[ci] = 0
            ac.drain()
            for j in range(8):
                kpow = -float(1 << (6 - j)) if j <= 6 else -0.5
                for ci in range(2):
                    sl_ = slice(ps_ * 2 * HD + ci * HD, ps_ * 2 * HD + (ci + 1) * HD)
                    uch = rs1[:, ci * HD:(ci + 1) * HD]
                    sch = xit[:, sl_]
                    if up[ci]:
                        ac.wait_ge(sV.h, up[ci])
                    sA.inc(ac.activation(sch, uch, ACTF.Sign, bias=chalf[:, 0:1]))
                    ve.wait_ge(sA.h, sA.n)
                    up[ci] = sV.inc(ve.scalar_tensor_tensor(
                        uch, sch, kpow, uch, op0=ALU.mult, op1=ALU.add))
            for ci in range(2):
                sl_ = slice(ps_ * 2 * HD + ci * HD, ps_ * 2 * HD + (ci + 1) * HD)
                ve.wait_ge(sV.h, up[ci])
                ve.scalar_tensor_tensor(xit[:, sl_], rs1[:, ci * HD:(ci + 1) * HD],
                                        -1.0, ft[:, sl_], op0=ALU.mult, op1=ALU.add)
                ve.drain()
                ve.tensor_scalar(xit[:, sl_], xit[:, sl_], -128.0, None, op0=ALU.add)
                fin_tok = sV.inc(ve.drain())
        ft_free[0] = fin_tok
        te.wait_ge(sV.h, ft_free[0])
        for dc in range(DC):
            pb = pt16[:, (dc % 2) * 1024:(dc % 2) * 1024 + 128]
            te.wait_ge(sA.h, evt[dc % 2])
            sT.inc(te.transpose(pb, xit[:, dc * 128:(dc + 1) * 128], eye16[:]))
            ac.wait_ge(sT.h, sT.n)
            evt[dc % 2] = sA.inc(ac.copy(
                xiT[:, dc * SO + qc * 128: dc * SO + (qc + 1) * 128], pb))
        xa_tok[qc] = sA.n

    def produce_gen(qg):
        for s in range(HSL):
            b = s % 2
            wgb = wgsl[:, b * DC * SLAB:(b + 1) * DC * SLAB]
            wvb2 = wvsl[:, b * DC * SLAB:(b + 1) * DC * SLAB]
            if s == 0:
                te.wait_ge(sA.h, max(xa_tok[qg * c.QG], xa_tok[qg * c.QG + 1]))
            sy.wait_ge(sT.h, wu[b])
            sW[b].inc(sy.dma_start(out=wgb[:], in_=wgi_sp[s]), 16)
            sW[b].inc(sy.dma_start(out=wvb2[:], in_=wvi_sp[s]), 16)
            ldw = sW[b].n
            sy.wait_ge(sV.h, bias_rd[0])
            sH[0].inc(sy.dma_start(out=gbb[:],
                                   in_=gbbt_hbm[:, s * SLAB:(s + 1) * SLAB]), 16)
            ld2 = sH[0].inc(sy.dma_start(out=vbb[:],
                                         in_=vbbt_hbm[:, s * SLAB:(s + 1) * SLAB]), 16)
            te.wait_ge(sW[b].h, ldw)
            for j in range(c.QG):
                qc = qg * c.QG + j
                slot = qc % NSLOT
                gvs = gvb[:, slot * H + s * SLAB: slot * H + (s + 1) * SLAB]
                gps_j = gps2[:, j * SLAB:(j + 1) * SLAB]
                vps_j = vps2[:, j * SLAB:(j + 1) * SLAB]
                gsl_j = gsl[:, j * SLAB:(j + 1) * SLAB]
                te.wait_ge(sV.h, pfree_tok[j])
                for dc in range(DC):
                    sT.inc(te.matmul(gps_j,
                                     xiT[:, dc * SO + qc * 128: dc * SO + (qc + 1) * 128],
                                     wgb[:, dc * SLAB:(dc + 1) * SLAB],
                                     start=(dc == 0), stop=(dc == DC - 1)))
                tok_g = sT.n
                for dc in range(DC):
                    sT.inc(te.matmul(vps_j,
                                     xiT[:, dc * SO + qc * 128: dc * SO + (qc + 1) * 128],
                                     wvb2[:, dc * SLAB:(dc + 1) * SLAB],
                                     start=(dc == 0), stop=(dc == DC - 1)))
                tok_v = sT.n
                ve.wait_ge(sT.h, tok_g)
                ve.wait_ge(sH[0].h, ld2)
                ve.wait_ge(sA.h, silu_tok[j])
                ve.scalar_tensor_tensor(gsl_j, gps_j, scg_t[:, qc:qc + 1],
                                        gbb[:], op0=ALU.mult, op1=ALU.add)
                sV.inc(ve.drain())
                ac.wait_ge(sV.h, sV.n)
                silu_tok[j] = sA.inc(ac.activation(gsl_j, gsl_j, ACTF.Silu))
                ve.wait_ge(sT.h, tok_v)
                ve.scalar_tensor_tensor(vsl[:], vps_j, scv_t[:, qc:qc + 1],
                                        vbb[:], op0=ALU.mult, op1=ALU.add)
                pfree_tok[j] = sV.inc(ve.drain())
                ve.wait_ge(sA.h, silu_tok[j])
                if slot_free[slot] is not None:
                    si, stok = slot_free[slot][s]
                    ve.wait_ge(sP6[si].h, stok)
                ve.tensor_tensor(gvs, gsl_j, vsl[:], op=ALU.mult)
                ve.drain()
                ve.tensor_reduce(tt0[:], gvs, axis=AX.X, op=ALU.max,
                                 apply_absolute_value=True)
                ve.drain()
                ve.tensor_tensor(ggvcol[:, qc:qc + 1], ggvcol[:, qc:qc + 1], tt0[:],
                                 op=ALU.max)
                sV.inc(ve.drain())
            bias_rd[0] = sV.n
            wu[b] = sT.n
            yield

    def qA(qc):
        slot = qc % NSLOT
        base = slot * H
        ve.tensor_scalar(ggvcol[:, qc:qc + 1], ggvcol[:, qc:qc + 1], 1e-5, None,
                         op0=ALU.max)
        ve.drain()
        ve.reciprocal(tt1[:], ggvcol[:, qc:qc + 1])
        ve.drain()
        ve.tensor_scalar(sc_gv[:, qc:qc + 1], tt1[:], 127.0, None, op0=ALU.mult)
        ve.drain()
        ve.tensor_scalar(tt0[:], ggvcol[:, qc:qc + 1], 0.5 / 127.0, None, op0=ALU.mult)
        ve.drain()
        ve.tensor_tensor(sc3[:, qc:qc + 1], tt0[:], gamw[:, 2:3], op=ALU.mult)
        ve.drain()
        ve.wait_ge(sT.h, dsc_war[0])
        ve.tensor_scalar(dsc[:], eye32[:], sc_gv[:, qc:qc + 1], None, op0=ALU.mult)
        dsc_tok = sV.inc(ve.drain())
        te.wait_ge(sV.h, dsc_tok)
        for ch in range(HC):
            p = ch % 2
            pb = ptq[:, p * 512:p * 512 + 128]
            blk = gvb[:, base + ch * 128: base + (ch + 1) * 128]
            te.wait_ge(sA.h, evt6[p])
            sT.inc(te.matmul(pb, blk, dsc[:], start=True, stop=True))
            ac.wait_ge(sT.h, sT.n)
            evt6[p] = sA.inc(ac.activation(blk, pb, ACTF.Identity, bias=c1285[:]))
        dsc_war[0] = sT.n
        slot_free[slot] = {}

    def pair_gen(qc):
        slot = qc % NSLOT
        base = slot * H
        W6 = c.W6
        for k0 in range(0, c.NHW, 4):
            pair = (k0, k0 + 1, k0 + 2, k0 + 3)
            upd = {}
            ac.drain()
            for ki in pair:
                i = ki % 4
                wch = gvb[:, base + ki * W6: base + (ki + 1) * W6]
                uch = ubuf[:, i * W6:(i + 1) * W6]
                if uwar[i]:
                    ac.wait_ge(sV.h, uwar[i])
                sA.inc(ac.activation(uch, wch, ACTF.Identity, bias=cm1275[:]))
                upd[ki] = sgnwar[i]
            ac.drain()
            for j in range(8):
                kpow = -float(1 << (6 - j)) if j <= 6 else -0.5
                for ki in pair:
                    i = ki % 4
                    uch = ubuf[:, i * W6:(i + 1) * W6]
                    sch = sgn[:, i * W6:(i + 1) * W6]
                    if upd[ki]:
                        ac.wait_ge(sV.h, upd[ki])
                    sA.inc(ac.activation(sch, uch, ACTF.Sign, bias=chalf[:, 0:1]))
                    ve.wait_ge(sA.h, sA.n)
                    upd[ki] = sV.inc(ve.scalar_tensor_tensor(
                        uch, sch, kpow, uch, op0=ALU.mult, op1=ALU.add))
            for ki in pair:
                i = ki % 4
                p2 = ki % 2
                sgnwar[i] = upd[ki]
                wch = gvb[:, base + ki * W6: base + (ki + 1) * W6]
                ve.wait_ge(sV.h, upd[ki])
                uwar[i] = sV.inc(ve.scalar_tensor_tensor(
                    wch, ubuf[:, i * W6:(i + 1) * W6], -1.0, wch,
                    op0=ALU.mult, op1=ALU.add))
                ac.wait_ge(sV.h, uwar[i])
                if scr_tok[p2]:
                    ac.wait_ge(sP6[p2].h, scr_tok[p2])
                ct = sA.inc(ac.copy(scr16[:, p2 * W6:(p2 + 1) * W6], wch))
                gp.wait_ge(sA.h, ct)
                scr_tok[p2] = sP6[p2].inc(gp.dma_start(
                    out=x2i_sp[qc][:, ki * W6:(ki + 1) * W6],
                    in_=scr16[:, p2 * W6:(p2 + 1) * W6]), 16)
                slot_free[slot][ki] = (p2, scr_tok[p2])
            yield

    assert c.QG == 2 and HSL % 4 == 0 and c.NHW == HSL
    p6a_pos = [3 * HSL // 16, 5 * HSL // 16, 11 * HSL // 16, 13 * HSL // 16]
    p6aA(0)
    p6aB(0)
    p6aA(1)
    p6aB(1)
    for qg in range(c.NQG):
        pairs = []
        if qg > 0:
            for q in (qg * 2 - 2, qg * 2 - 1):
                qA(q)
                pairs.append(pair_gen(q))
        nxt = [qg * 2 + 2, qg * 2 + 3]
        pg = produce_gen(qg)
        for s in range(HSL):
            if pairs:
                if s % 4 == 0:
                    next(pairs[0], None)
                elif s % 4 == 2:
                    next(pairs[1], None)
            if s == p6a_pos[0] and nxt[0] < QT:
                p6aA(nxt[0])
            if s == p6a_pos[1] and nxt[0] < QT:
                p6aB(nxt[0])
            if s == p6a_pos[2] and nxt[1] < QT:
                p6aA(nxt[1])
            if s == p6a_pos[3] and nxt[1] < QT:
                p6aB(nxt[1])
            next(pg)
        for g_ in pairs:
            for _ in g_:
                pass
    for q in (QT - 2, QT - 1):
        qA(q)
    for _ in zip(pair_gen(QT - 2), pair_gen(QT - 1)):
        pass
    free("gvb", "dsc", "c1285", "cm1275", "scr16", "ubuf", "sgn", "vsl",
         "gsl", "vbb", "gbb", "wvsl", "wgsl",
         "xit", "rs1", "ft", "xiT")
    pfree("ptq", "vps2", "gps2", "pt16")
    barrier()

    wo3 = alloc("wo3", [128, HC * D], FP8)
    xb2 = alloc("xb2", [128, 2 * HC * 128], BF16)
    o3sb = alloc("o3sb", [128, D])
    x1b = alloc("x1b", [128, D])
    z4t = alloc("z4t", [128, D])
    obb = alloc("obb", [128, D])
    csb = alloc("csb", [128, D])
    sL.inc(sy.dma_start(out=obb[:], in_=obbt_hbm[:]), 16)
    NW7 = max(1, HC // 16)
    for k7 in range(NW7):
        w7 = HC * D // NW7
        sL.inc(sy.dma_start(out=wo3[:, k7 * w7:(k7 + 1) * w7],
                            in_=woi_sp[:, k7 * w7:(k7 + 1) * w7]), 16)
    te.wait_ge(sL.h, sL.n)
    cs_ps = palloc("cs_ps", [1, D])
    for hc in range(HC):
        for dsl_ in range(DSL):
            sT.inc(te.matmul(cs_ps[:1, dsl_ * SLAB:(dsl_ + 1) * SLAB], ones16[:],
                             wo3[:, hc * D + dsl_ * SLAB: hc * D + (dsl_ + 1) * SLAB],
                             start=(hc == 0), stop=(hc == HC - 1)))
    ve.wait_ge(sT.h, sT.n)
    ve.tensor_scalar(csb[:1, :], cs_ps[:1, :], 128.0, None, op0=ALU.mult)
    sV.inc(ve.drain())
    pfree("cs_ps")
    bc_ps7 = palloc("bc_ps7", [128, 512])
    te.wait_ge(sV.h, sV.n)
    bcast(csb, csb[:1, :], D, bc_ps7)
    pfree("bc_ps7")
    o3ps = palloc("o3ps", [128, 2048])
    te.wait_ge(sA.h, sA.n)
    xu = [0, 0]
    ev7 = [0, 0]
    ld7 = [0, 0]
    ld7[0] = sX[0].inc(sy.dma_start(out=xb2[:, :HC * 128], in_=x2i_sp[0]), 16)
    for qc in range(QT):
        b7 = qc % 2
        xbb = xb2[:, b7 * HC * 128:(b7 + 1) * HC * 128]
        sy.wait_ge(sV.h, sV.n)
        lx7 = sK.inc(sy.dma_start(out=x1b[:], in_=x1_sp[qc * 128:(qc + 1) * 128, :]), 16)
        if qc + 1 < QT:
            b2 = (qc + 1) % 2
            sy.wait_ge(sT.h, xu[b2])
            ld7[b2] = sX[b2].inc(sy.dma_start(
                out=xb2[:, b2 * HC * 128:(b2 + 1) * HC * 128], in_=x2i_sp[qc + 1]), 16)
        te.wait_ge(sX[b7].h, ld7[b7])
        for dg in range((DSL + 1) // 2):
            dss = range(2 * dg, min(2 * dg + 2, DSL))
            te.wait_ge(sV.h, ev7[dg % 2])
            for hc in range(HC):
                for ds in dss:
                    sT.inc(te.matmul(o3ps[:, ds * SLAB:(ds + 1) * SLAB],
                                     xbb[:, hc * 128:(hc + 1) * 128],
                                     wo3[:, hc * D + ds * SLAB: hc * D + (ds + 1) * SLAB],
                                     start=(hc == 0), stop=(hc == HC - 1)))
            xu[b7] = sT.n
            ve.wait_ge(sT.h, sT.n)
            ve.wait_ge(sA.h, sA.n)
            for ds in dss:
                ve.tensor_tensor(o3sb[:, ds * SLAB:(ds + 1) * SLAB],
                                 o3ps[:, ds * SLAB:(ds + 1) * SLAB],
                                 csb[:, ds * SLAB:(ds + 1) * SLAB], op=ALU.subtract)
                ve.drain()
                ve.scalar_tensor_tensor(o3sb[:, ds * SLAB:(ds + 1) * SLAB],
                                        o3sb[:, ds * SLAB:(ds + 1) * SLAB],
                                        sc3[:, qc:qc + 1], obb[:, ds * SLAB:(ds + 1) * SLAB],
                                        op0=ALU.mult, op1=ALU.add)
                ve.drain()
            ev7[dg % 2] = sV.inc(ve.drain())
        ln_stats(o3sb[:], D)
        ln_rstd()
        ac.wait_ge(sO.h, sO.n)
        ln_apply(z4t[:], o3sb[:])
        ve.wait_ge(sA.h, sA.n)
        ve.wait_ge(sK.h, lx7)
        ve.tensor_tensor(z4t[:], z4t[:], x1b[:], op=ALU.add)
        sV.inc(ve.drain())
        sy.wait_ge(sV.h, sV.n)
        sO.inc(sy.dma_start(out=out_hbm[qc * 128:(qc + 1) * 128, :], in_=z4t[:]), 16)
    sy.wait_ge(sO.h, sO.n)
    barrier()

    for cm in reversed(list(live.values())):
        cm.__exit__(None, None, None)
    for cm in reversed(list(psum_live.values())):
        cm.__exit__(None, None, None)
    for cm in reversed(sem_cms):
        cm.__exit__(None, None, None)
    return nc


def _bf16():
    import ml_dtypes
    return ml_dtypes.bfloat16


def reshape_blocked(w, out_blk=128):
    Dd, N = w.shape
    dcn = Dd // 128
    nb = N // out_blk
    return np.ascontiguousarray(
        w.reshape(dcn, 128, nb, out_blk).transpose(1, 2, 0, 3).reshape(128, -1))


def make_core_inputs(cfg, x_b, parity, shared):
    own = np.ascontiguousarray(x_b[parity::2])
    peer = np.ascontiguousarray(x_b[1 - parity::2])
    xr = np.concatenate([own, peer], 0)
    ii = np.arange(128)[:, None]
    jj = np.arange(128)[None, :]
    mask_own = np.where(ii <= jj, 0.0, NEG_MASK).astype(np.float32)
    if parity == 0:
        mask_peer = np.where(ii < jj, 0.0, NEG_MASK).astype(np.float32)
    else:
        mask_peer = np.where(ii <= jj, 0.0, NEG_MASK).astype(np.float32)
    d = {"x": xr, "mask_own": mask_own, "mask_peer": mask_peer}
    d.update(shared)
    return d


def kernel(L0_attn_x, L0_attn_qkv, L0_attn_o_proj, L0_attn_gamma, L0_attn_beta,
           L0_ffn_gate_w, L0_ffn_gate_b, L0_ffn_val_w, L0_ffn_val_b,
           L0_ffn_out_w, L0_ffn_out_b):
    cfg = Cfg(D=2048, S=2048)
    x = np.asarray(L0_attn_x, dtype=np.float32)
    qkv_f = np.asarray(L0_attn_qkv, dtype=np.float32)
    wo_f = np.asarray(L0_attn_o_proj, dtype=np.float32)
    shared = {
        "qkv": np.ascontiguousarray(qkv_f),
        "qkvc": reshape_blocked(qkv_f, 128),
        "wob": reshape_blocked(wo_f, 512),
        "gamma": np.asarray(L0_attn_gamma, dtype=np.float32),
        "beta": np.asarray(L0_attn_beta, dtype=np.float32),
        "gwT": np.ascontiguousarray(np.asarray(L0_ffn_gate_w, dtype=np.float32).T),
        "vwT": np.ascontiguousarray(np.asarray(L0_ffn_val_w, dtype=np.float32).T),
        "owT": np.ascontiguousarray(np.asarray(L0_ffn_out_w, dtype=np.float32).T),
        "gate_b": np.asarray(L0_ffn_gate_b, dtype=np.float32),
        "gbbt": np.ascontiguousarray(np.broadcast_to(
            np.asarray(L0_ffn_gate_b, dtype=np.float32), (128, len(L0_ffn_gate_b)))),
        "vbbt": np.ascontiguousarray(np.broadcast_to(
            np.asarray(L0_ffn_val_b, dtype=np.float32), (128, len(L0_ffn_val_b)))),
        "obbt": np.ascontiguousarray(np.broadcast_to(
            np.asarray(L0_ffn_out_b, dtype=np.float32), (128, len(L0_ffn_out_b)))),
        "val_b": np.asarray(L0_ffn_val_b, dtype=np.float32),
        "out_b": np.asarray(L0_ffn_out_b, dtype=np.float32),
        "eye32": np.eye(128, dtype=np.float32),
        "eye16": np.eye(128, dtype=np.float32).astype(_bf16()),
    }
    nc = build_program(cfg)
    in_maps = []
    for core in range(8):
        b, p = divmod(core, 2)
        in_maps.append(make_core_inputs(cfg, x[b], p, shared))
    res = run_bass_kernel_spmd(nc, in_maps, core_ids=list(range(8)))
    out = np.empty((4, cfg.S, cfg.D), dtype=np.float32)
    for core in range(8):
        b, p = divmod(core, 2)
        out[b, p::2] = res.results[core]["out"]
    return out


# version 25
# speedup vs baseline: 1.0282x; 1.0282x over previous
import math
import numpy as np

import concourse.bass as bass
import concourse.mybir as mybir
from concourse.bass_utils import run_bass_kernel_spmd

F32 = mybir.dt.float32
BF16 = mybir.dt.bfloat16
FP8 = mybir.dt.float8e4
AX = mybir.AxisListType
ALU = mybir.AluOpType
ACTF = mybir.ActivationFunctionType

NEG_MASK = -1.0e33
EPS = 1.0e-5


class Cfg:
    def __init__(self, D=2048, S=2048, H=None):
        self.D = D
        self.S = S
        self.H = H if H is not None else 4 * D
        self.SO = S // 2
        self.DC = D // 128
        self.QT = self.SO // 128
        self.KT = S // 128
        self.HC = self.H // 128
        self.SLAB = min(512, self.H, D, self.SO)
        self.DSL = D // self.SLAB
        self.HSL = self.H // self.SLAB
        self.SSL = S // self.SLAB
        self.OSL = self.SO // self.SLAB
        self.QG = min(2, self.QT)
        self.NQG = self.QT // self.QG
        self.NH4 = self.H // 512
        self.W6 = 512
        self.NHW = self.H // self.W6
        self.scale = 1.0 / math.sqrt(float(D))
        self.debug = False
        assert self.SLAB == 512, "psum accumulation slabs must be bank-aligned"
        assert D % 128 == 0 and self.SO % 128 == 0 and self.H % self.SLAB == 0
        assert D % self.SLAB == 0 and self.SO % self.SLAB == 0
        assert self.NH4 % 2 == 0 and self.QT % self.QG == 0
        assert self.NHW % 4 == 0 and self.W6 % 512 == 0


class Sem:
    def __init__(self, h):
        self.h = h
        self.n = 0

    def inc(self, instr, k=1):
        instr.then_inc(self.h, k)
        self.n += k
        return self.n


def build_program(c: Cfg):
    nc = bass.Bass("TRN2", target_bir_lowering=False)
    D, S, H, SO = c.D, c.S, c.H, c.SO
    DC, QT, KT, HC = c.DC, c.QT, c.KT, c.HC
    SLAB, DSL, HSL, OSL = c.SLAB, c.DSL, c.HSL, c.OSL

    din = lambda n, sh, dt=F32: nc.dram_tensor(n, sh, dt, kind="ExternalInput").ap()
    x_hbm = din("x", [S, D])
    qkv_hbm = din("qkv", [D, 3 * D])
    qkvc_hbm = din("qkvc", [128, (3 * D // 128) * DC * 128])
    wob_hbm = din("wob", [128, DSL * DC * SLAB])
    gamma_hbm = din("gamma", [D])
    beta_hbm = din("beta", [D])
    gwT_hbm = din("gwT", [D, H])
    vwT_hbm = din("vwT", [D, H])
    owT_hbm = din("owT", [H, D])
    gb_hbm = din("gate_b", [H])
    vb_hbm = din("val_b", [H])
    ob_hbm = din("out_b", [D])
    gbbt_hbm = din("gbbt", [128, H])
    vbbt_hbm = din("vbbt", [128, H])
    obbt_hbm = din("obbt", [128, D])
    mko_hbm = din("mask_own", [128, 128])
    mkp_hbm = din("mask_peer", [128, 128])
    eye32_hbm = din("eye32", [128, 128])
    eye16_hbm = din("eye16", [128, 128], BF16)
    out_hbm = nc.dram_tensor("out", [SO, D], F32, kind="ExternalOutput").ap()

    dint = lambda n, sh, dt=F32: nc.dram_tensor(n, sh, dt, kind="Internal").ap()
    hn_sp = dint("hn_sp", [KT, 128, DC * 128])
    k_sp = dint("k_sp", [128, KT, DC * 128])
    v_sp = dint("v_sp", [128, DC, KT * 128])
    q_sp = dint("q_sp", [DC, 128, SO])
    h_sp = dint("h_sp", [SO, D])
    x1_sp = dint("x1_sp", [SO, D])
    wgi_sp = dint("wgi_sp", [HSL, 128, DC * SLAB], FP8)
    wvi_sp = dint("wvi_sp", [HSL, 128, DC * SLAB], FP8)
    woi_sp = dint("woi_sp", [128, HC * D], FP8)
    x2i_sp = dint("x2i_sp", [QT, 128, H], BF16)
    if c.debug:
        qa_l = [SO - 128 * (m % QT) for m in range(KT)]
        dbg_ptp = dint("dbg_ptp", [128, int(sum(qa_l))])
        dbg_l = dint("dbg_l", [128, QT])
        dbg_negM = dint("dbg_negM", [128, SO])
        dbg_ofm = dint("dbg_ofm", [128, DC * SO])
        dbg_ao = dint("dbg_ao", [128, QT * D])

    te, ve, ac, gp, sy = nc.tensor, nc.vector, nc.scalar, nc.gpsimd, nc.sync

    sem_cms = []

    def mksem(name):
        cm = nc.semaphore(name)
        h = cm.__enter__()
        sem_cms.append(cm)
        return Sem(h)

    sL = mksem("sL")
    sX = [mksem("sX0"), mksem("sX1")]
    sW = [mksem("sW0"), mksem("sW1")]
    sH = [mksem("sH0"), mksem("sH1")]
    sU = [mksem("sU0"), mksem("sU1")]
    sK = mksem("sK")
    sWS = mksem("sWS")
    sQW = [mksem("sQW0"), mksem("sQW1"), mksem("sQW2"), mksem("sQW3")]
    sA5 = mksem("sA5")
    sV5 = mksem("sV5")
    sS = mksem("sS")
    sP = mksem("sP")
    sP6 = [mksem("sP6a"), mksem("sP6b")]
    sKV = mksem("sKV")
    sT = mksem("sT")
    sV = mksem("sV")
    sA = mksem("sA")
    sG = mksem("sG")
    sO = mksem("sO")
    sB = mksem("sB")

    def barrier(full=True):
        drains = [sS, sL, sO, sK, sWS, sKV, sX[0], sX[1], sW[0], sW[1],
                  sH[0], sH[1], sU[0], sU[1]]
        if full:
            drains += [sP] + sP6 + sQW
        for s_ in drains:
            sy.wait_ge(s_.h, s_.n)
        gp.wait_ge(sG.h, sG.n)
        engs = [te, ve, ac, gp, sy]
        for e in engs:
            sB.inc(e.drain() if e is not sy else e.nop())
        for e in engs:
            e.wait_ge(sB.h, sB.n)

    live = {}

    def alloc(name, sh, dt=F32):
        cm = nc.sbuf_tensor(name, sh, dt)
        h = cm.__enter__()
        live[name] = cm
        return h

    def alloc_r(name, sh, dt=F32):
        cm = nc.sbuf_tensor(name, sh, dt, side="right")
        h = cm.__enter__()
        live[name] = cm
        return h

    def free(*names):
        for n in names:
            live.pop(n).__exit__(None, None, None)

    psum_live = {}

    def palloc(name, sh, dt=F32):
        cm = nc.psum_tensor(name, sh, dt)
        h = cm.__enter__()
        psum_live[name] = cm
        return h

    def pfree(*names):
        for n in names:
            psum_live.pop(n).__exit__(None, None, None)

    ones = alloc("ones", [128, 1])
    ones16 = alloc("ones16", [128, 1], BF16)
    ones_row = alloc("ones_row", [1, 128])
    c127 = alloc("c127", [128, 1])
    eye32 = alloc("eye32s", [128, 128])
    eye16 = alloc("eye16s", [128, 128], BF16)
    mask_own = alloc("mask_own_s", [128, 128])
    mask_peer = alloc("mask_peer_s", [128, 128])
    nstat = 6 * ((D + 511) // 512)
    stat = alloc("stat", [128, nstat])
    agg = alloc("agg", [128, 2])
    vep = alloc("vep", [128, 1])
    rr0 = alloc("rr0", [128, 1])
    tt0 = alloc("tt0", [128, 1])
    tt1 = alloc("tt1", [128, 1])
    recip_l = alloc("recip_l", [128, QT])
    gxcol = alloc("gxcol", [128, QT])
    ggvcol = alloc("ggvcol", [128, QT])
    sc_x = alloc("sc_x", [128, QT])
    sc_gv = alloc("sc_gv", [128, QT])
    scg_t = alloc("scg_t", [128, QT])
    scv_t = alloc("scv_t", [128, QT])
    sc3 = alloc("sc3", [128, QT])
    gwpart = alloc("gwpart", [128, 1])
    chalf = alloc("chalf", [128, 2])
    gwsum = alloc("gwsum", [128, 3])
    gwtot_sb = alloc("gwtot_sb", [1, 3])
    gwrec_sb = alloc("gwrec_sb", [1, 3])
    rgam = alloc("rgam", [128, 3])
    gamw = alloc("gamw", [128, 3])
    qn_row = alloc("qn_row", [1, SO])
    maxkn = alloc("maxkn", [1, 1])
    negM = alloc("negM", [128, SO])
    negMr = alloc("negMr", [1, SO])

    def ln_stats(x_ap, nfree):
        nchunks = (nfree + 511) // 512
        for i in range(nchunks):
            lo = i * 512
            hi = min(nfree, lo + 512)
            ve.bn_stats(stat[:, 6 * i:6 * (i + 1)], x_ap[:, lo:hi])
        ve.drain()
        ve.bn_aggr(agg[:], stat[:, :6 * nchunks])
        ve.drain()

    def ln_rstd():
        ve.tensor_scalar(vep[:], agg[:, 1:2], EPS, None, op0=ALU.add)
        sV.inc(ve.drain())
        ac.wait_ge(sV.h, sV.n)
        sA.inc(ac.activation(tt0[:], vep[:], ACTF.Sqrt))
        ve.wait_ge(sA.h, sA.n)
        ve.reciprocal(rr0[:], tt0[:])
        for _ in range(2):
            ve.drain()
            ve.tensor_tensor(tt0[:], rr0[:], rr0[:], op=ALU.mult)
            ve.drain()
            ve.tensor_tensor(tt0[:], tt0[:], vep[:], op=ALU.mult)
            ve.drain()
            ve.tensor_scalar(tt0[:], tt0[:], -0.5, 1.5, op0=ALU.mult, op1=ALU.add)
            ve.drain()
            ve.tensor_tensor(rr0[:], rr0[:], tt0[:], op=ALU.mult)
        ve.drain()
        ve.tensor_scalar(tt1[:], agg[:, 0:1], -1.0, rr0[:], op0=ALU.mult, op1=ALU.mult)
        sV.inc(ve.drain())

    def ln_apply(dst, src):
        ac.wait_ge(sV.h, sV.n)
        ac.drain()
        sA.inc(ac.activation(dst, src, ACTF.Identity, bias=tt1[:], scale=rr0[:]))

    def bcast(dst, src_row, n, ps, war_cell=None):
        for lo in range(0, n, 512):
            w = min(512, n - lo)
            te.wait_ge(sA.h, sA.n if war_cell is None else war_cell[0])
            sT.inc(te.matmul(ps[:, :w], ones_row[:1, :], src_row[:1, lo:lo + w],
                             start=True, stop=True))
            ac.wait_ge(sT.h, sT.n)
            sA.inc(ac.copy(dst[:, lo:lo + w], ps[:, :w]))
            if war_cell is not None:
                war_cell[0] = sA.n
        return sT.n

    def round_ladder(f_ap, sc_ap, out_ap, w_scr, u_scr, p_scr):
        ve.tensor_scalar(w_scr, f_ap, sc_ap, 128.5, op0=ALU.mult, op1=ALU.add)
        ve.drain()
        ve.tensor_copy(u_scr, w_scr)
        ve.drain()
        for j in range(7, -1, -1):
            k = float(1 << j)
            ve.tensor_scalar(p_scr, u_scr, k, None, op0=ALU.is_ge)
            ve.drain()
            ve.scalar_tensor_tensor(u_scr, p_scr, -k, u_scr, op0=ALU.mult, op1=ALU.add)
            ve.drain()
        ve.scalar_tensor_tensor(p_scr, u_scr, -1.0, w_scr, op0=ALU.mult, op1=ALU.add)
        ve.drain()
        ve.tensor_scalar(out_ap, p_scr, -128.0, None, op0=ALU.add)
        ve.drain()

    WCW = 512
    wqb = alloc_r("wqb", [128, 4 * WCW])
    wpb = alloc_r("wpb", [128, 2 * WCW])
    wmb = alloc_r("wmb", [128, 2 * WCW])
    wib = alloc_r("wib", [128, WCW], FP8)
    NCG = H // WCW

    def wchunks(src_hbm, mat_sp):
        out = []
        for dc in range(DC):
            for cg in range(NCG):
                def store(wib_, dc=dc, cg=cg):
                    s_lo = (cg * WCW) // SLAB
                    s_hi = ((cg + 1) * WCW) // SLAB
                    for s in range(s_lo, s_hi):
                        off = s * SLAB - cg * WCW
                        sP.inc(gp.dma_start(
                            out=mat_sp[s][:, dc * SLAB:(dc + 1) * SLAB],
                            in_=wib_[:, off:off + SLAB]), 16)
                out.append((src_hbm[dc * 128:(dc + 1) * 128, cg * WCW:(cg + 1) * WCW],
                            WCW, store))
        return out

    def owt_chunks():
        out = []
        for ch in range(HC):
            for hf in range(D // WCW):
                def store(wib_, ch=ch, hf=hf):
                    sP.inc(gp.dma_start(
                        out=woi_sp[:, ch * D + hf * WCW: ch * D + (hf + 1) * WCW],
                        in_=wib_[:, :WCW]), 16)
                out.append((owT_hbm[ch * 128:(ch + 1) * 128,
                                    hf * WCW:(hf + 1) * WCW], WCW, store))
        return out

    wq_us = [0, 0, 0, 0]

    class Pump:
        DEPTH = 4

        def __init__(self, stages):
            self.items = []
            self.nslot = 0
            for compute, idx, chunks in stages:
                if callable(chunks):
                    self.items.append((compute, idx, False, None, None))
                    continue
                for ci, ck in enumerate(chunks):
                    self.items.append((compute, idx, ci == 0, ck, self.nslot))
                    self.nslot += 1
            self.n = len(self.items)
            self.cursor = 0
            self.loaded = 0
            self.lds = {}

        def _load(self):
            i = self.loaded
            item = self.items[i]
            if item[3] is not None:
                compute, idx, first, (src_, ncols, store_fn), sl = item
                b = sl % 4
                gp.wait_ge(sA5.h, wq_us[b])
                self.lds[i] = sQW[b].inc(
                    gp.dma_start(out=wqb[:, b * WCW:b * WCW + ncols], in_=src_), 16)
            self.loaded += 1

        def pump(self, k):
            while k > 0 and self.cursor < self.n:
                while self.loaded < min(self.cursor + self.DEPTH, self.n):
                    self._load()
                compute, idx, first, ck, sl = self.items[self.cursor]
                if ck is None:
                    compute(idx)
                else:
                    self.compute_chunk(compute, self.cursor, idx, first, ck, sl)
                self.cursor += 1
                k -= 1

        def compute_chunk(self, compute, i, idx, first, ck, sl):
            src_, ncols, store_fn = ck
            compute(self, i, idx, first, ncols, store_fn, sl)

        def drain(self):
            self.pump(self.n)

    def gamma_compute(pp, i, idx, first, ncols, store_fn, sl):
        b = sl % 4
        wb_ = wqb[:, b * WCW:b * WCW + ncols]
        if first:
            ve.wait_ge(sA5.h, sA5.n)
            ve.memset(gwsum[:, idx:idx + 1], 0.0)
            sV5.inc(ve.drain())
            ac.wait_ge(sV5.h, sV5.n)
        ac.wait_ge(sQW[b].h, pp.lds[i])
        ac.wait_ge(sV5.h, sV5.n)
        ac.drain()
        sA5.inc(ac.activation(wpb[:, (sl % 2) * WCW:(sl % 2) * WCW + ncols], wb_,
                              ACTF.Abs, accum_out=gwpart[:]))
        wq_us[b] = sA5.n
        ve.wait_ge(sA5.h, sA5.n)
        ve.tensor_tensor(gwsum[:, idx:idx + 1], gwsum[:, idx:idx + 1],
                         gwpart[:], op=ALU.add)
        sV5.inc(ve.drain())

    def quant_compute(pp, i, idx, first, ncols, store_fn, sl):
        b = sl % 4
        wb_ = wqb[:, b * WCW:b * WCW + ncols]
        wpb_ = wpb[:, (sl % 2) * WCW:(sl % 2) * WCW + ncols]
        wmb_ = wmb[:, (sl % 2) * WCW:(sl % 2) * WCW + ncols]
        if not hasattr(pp, "sv2_tok"):
            pp.sv2_tok = [0, 0]
        ac.wait_ge(sQW[b].h, pp.lds[i])
        ac.wait_ge(sV5.h, pp.sv2_tok[sl % 2])
        sA5.inc(ac.activation(wpb_, wb_, ACTF.Sign,
                              bias=chalf[:, 0:1], scale=rgam[:, idx:idx + 1]))
        sA5.inc(ac.activation(wmb_, wb_, ACTF.Sign,
                              bias=chalf[:, 1:2], scale=rgam[:, idx:idx + 1]))
        wq_us[b] = sA5.n
        ve.wait_ge(sA5.h, sA5.n)
        ve.wait_ge(sP.h, sP.n)
        ve.tensor_tensor(wib[:, :ncols], wpb_, wmb_, op=ALU.add)
        pp.sv2_tok[sl % 2] = sV5.inc(ve.drain())
        gp.wait_ge(sV5.h, sV5.n)
        store_fn(wib)

    def wquant_bcast(idx):
        n_elems = D * H if idx < 2 else H * D
        te.wait_ge(sV5.h, sV5.n)
        gps = palloc("gwps" + str(idx), [1, 1])
        mm_tok = sT.inc(te.matmul(gps[:1, :1], gwsum[:, idx:idx + 1], ones[:],
                                  start=True, stop=True))
        ve.wait_ge(sT.h, mm_tok)
        ve.tensor_scalar(gwtot_sb[:1, idx:idx + 1], gps[:1, :1], 1.0 / float(n_elems),
                         1e-5, op0=ALU.mult, op1=ALU.max)
        ve.drain()
        ve.reciprocal(gwrec_sb[:1, idx:idx + 1], gwtot_sb[:1, idx:idx + 1])
        sV5.inc(ve.drain())
        pfree("gwps" + str(idx))
        bc_psw = palloc("bc_psw" + str(idx), [128, 512])
        te.wait_ge(sV5.h, sV5.n)
        cell = [0]
        bcast(rgam[:, idx:idx + 1], gwrec_sb[:1, idx:idx + 1], 1, bc_psw,
              war_cell=cell)
        bcast(gamw[:, idx:idx + 1], gwtot_sb[:1, idx:idx + 1], 1, bc_psw,
              war_cell=cell)
        ac.drain()
        pfree("bc_psw" + str(idx))

    wpump = Pump([
        (gamma_compute, 0, wchunks(gwT_hbm, wgi_sp)),
        (wquant_bcast, 0, wquant_bcast),
        (gamma_compute, 1, wchunks(vwT_hbm, wvi_sp)),
        (wquant_bcast, 1, wquant_bcast),
        (quant_compute, 0, wchunks(gwT_hbm, wgi_sp)),
        (gamma_compute, 2, owt_chunks()),
        (wquant_bcast, 2, wquant_bcast),
        (quant_compute, 1, wchunks(vwT_hbm, wvi_sp)),
        (quant_compute, 2, owt_chunks()),
    ])

    gam_b = alloc("gam_b", [128, D])
    bet_b = alloc("bet_b", [128, D])
    sL.inc(sy.dma_start(out=eye32[:], in_=eye32_hbm), 16)
    sL.inc(sy.dma_start(out=eye16[:], in_=eye16_hbm), 16)
    sL.inc(sy.dma_start(out=mask_own[:], in_=mko_hbm), 16)
    sL.inc(sy.dma_start(out=mask_peer[:], in_=mkp_hbm), 16)
    sL.inc(sy.dma_start(out=gam_b[:1, :], in_=gamma_hbm[None, :]), 16)
    sL.inc(sy.dma_start(out=bet_b[:1, :], in_=beta_hbm[None, :]), 16)
    ve.memset(ones[:], 1.0)
    ve.memset(ones16[:], 1.0)
    ve.memset(ones_row[:], 1.0)
    ve.memset(c127[:], 127.0)
    ve.memset(chalf[:, 0:1], -0.5)
    ve.memset(chalf[:, 1:2], 0.5)
    sV.inc(ve.drain())
    bc_ps0 = palloc("bc_ps0", [128, 512])
    te.wait_ge(sL.h, sL.n)
    te.wait_ge(sV.h, sV.n)
    bcast(gam_b, gam_b[:1, :], D, bc_ps0)
    bcast(bet_b, bet_b[:1, :], D, bc_ps0)
    pfree("bc_ps0")

    hnf = alloc("hnf", [128, DC * S])
    xt = alloc("xt", [128, 2 * D])
    ht = alloc("ht", [128, D])
    zt = alloc("zt", [128, D])
    ptr = palloc("ptr", [128, 1024])
    ve.wait_ge(sA.h, sA.n)
    evac_tok = [0, 0]
    tile_done = [0, 0]
    for i in range(KT):
        b = i % 2
        xb = xt[:, b * D:b * D + D]
        sy.wait_ge(sT.h, tile_done[b])
        ld = sX[b].inc(sy.dma_start(out=xb, in_=x_hbm[i * 128:(i + 1) * 128, :]), 16)
        ve.wait_ge(sX[b].h, ld)
        ln_stats(xb, D)
        ln_rstd()
        ac.wait_ge(sS.h, sS.n)
        ln_apply(ht[:], xb)
        if i < QT:
            sy.wait_ge(sA.h, sA.n)
            sS.inc(sy.dma_start(out=h_sp[i * 128:(i + 1) * 128, :], in_=ht[:]), 16)
        ve.wait_ge(sA.h, sA.n)
        ln_stats(ht[:], D)
        ln_rstd()
        ln_apply(zt[:], ht[:])
        ve.wait_ge(sA.h, sA.n)
        ve.tensor_tensor(zt[:], zt[:], gam_b[:], op=ALU.mult)
        ve.drain()
        ve.tensor_tensor(zt[:], zt[:], bet_b[:], op=ALU.add)
        sV.inc(ve.drain())
        te.wait_ge(sV.h, sV.n)
        for dc in range(DC):
            pb = ptr[:, (dc % 2) * 512:(dc % 2) * 512 + 128]
            te.wait_ge(sA.h, evac_tok[dc % 2])
            sT.inc(te.transpose(pb, zt[:, dc * 128:(dc + 1) * 128], eye32[:]))
            ac.wait_ge(sT.h, sT.n)
            dst = hnf[:, dc * S + i * 128: dc * S + i * 128 + 128]
            evac_tok[dc % 2] = sA.inc(ac.copy(dst, pb))
            gp.wait_ge(sA.h, sA.n)
            sKV.inc(gp.dma_start(out=hn_sp[i][:, dc * 128:(dc + 1) * 128],
                                 in_=dst), 16)
        tile_done[b] = sT.n
        ve.wait_ge(sT.h, sT.n)
        ac.wait_ge(sT.h, sT.n)
    free("zt", "ht", "xt")
    pfree("ptr")
    barrier()

    wb = alloc("wb", [128, 2 * DC * 128])
    ksb = alloc("ksb", [128, S])
    ksq = alloc("ksq", [128, SLAB])
    nsb = alloc("nsb", [1, S], BF16)
    kp = palloc("kp", [128, S])
    npp = palloc("npsum", [1, 2 * SLAB])

    def proj(base_col, ntok, store, pump_hook=lambda: None):
        NG = ntok // SLAB
        wtok = [0, 0]
        copy_tok = [0]
        for o in range(DC):
            ob = base_col // 128 + o
            bb = o % 2
            wbo = wb[:, bb * DC * 128:(bb + 1) * DC * 128]
            sy.wait_ge(sT.h, wtok[bb])
            ld = sW[bb].inc(sy.dma_start(
                out=wbo, in_=qkvc_hbm[:, ob * DC * 128:(ob + 1) * DC * 128]), 16)
            te.wait_ge(sW[bb].h, ld)
            te.wait_ge(sV.h, copy_tok[0])
            for dc in range(DC):
                wbb = wbo[:, dc * 128:(dc + 1) * 128]
                for g in range(NG):
                    hsrc = hnf[:, dc * S + g * SLAB: dc * S + (g + 1) * SLAB]
                    sT.inc(te.matmul(kp[:, g * SLAB:(g + 1) * SLAB], wbb, hsrc,
                                     start=(dc == 0), stop=(dc == DC - 1)))
            wtok[bb] = sT.n
            ve.wait_ge(sT.h, sT.n)
            ve.wait_ge(sS.h, sS.n)
            ve.wait_ge(sKV.h, sKV.n)
            ve.tensor_copy(ksb[:, :ntok], kp[:, :ntok])
            copy_tok[0] = sV.inc(ve.drain())
            sy.wait_ge(sV.h, sV.n)
            gp.wait_ge(sV.h, sV.n)
            store(o)
            if o == 0:
                ve.memset(nsb[:1, :ntok], 0.0)
                sV.inc(ve.drain())
            for g in range(NG):
                gb_ = g % 2
                ve.wait_ge(sT.h, sT.n)
                ve.tensor_tensor(ksq[:], ksb[:, g * SLAB:(g + 1) * SLAB],
                                 ksb[:, g * SLAB:(g + 1) * SLAB], op=ALU.mult)
                sV.inc(ve.drain())
                te.wait_ge(sV.h, sV.n)
                nmm = sT.inc(te.matmul(npp[:1, gb_ * SLAB:(gb_ + 1) * SLAB],
                                       ones[:], ksq[:], start=True, stop=True))
                ve.wait_ge(sT.h, nmm)
                ve.tensor_tensor(nsb[:1, g * SLAB:(g + 1) * SLAB],
                                 nsb[:1, g * SLAB:(g + 1) * SLAB],
                                 npp[:1, gb_ * SLAB:(gb_ + 1) * SLAB], op=ALU.add)
                sV.inc(ve.drain())
            pump_hook()

    def store_k(o):
        sKV.inc(gp.dma_start(out=k_sp[:, :, o * 128:(o + 1) * 128],
                             in_=ksb[:, :S]), 16)

    proj(D, S, store_k, lambda: wpump.pump(25))
    ve.wait_ge(sT.h, sT.n)
    ve.tensor_reduce(maxkn[:], nsb[:1, :S], axis=AX.X, op=ALU.max)
    ve.drain()
    ve.tensor_scalar(maxkn[:], maxkn[:], 1e-30, None, op0=ALU.max)
    sV.inc(ve.drain())

    def store_q(o):
        sS.inc(sy.dma_start(out=q_sp[o], in_=ksb[:, :SO]), 16)

    proj(0, SO, store_q, lambda: wpump.pump(25))
    ve.wait_ge(sT.h, sT.n)
    ve.tensor_copy(qn_row[:], nsb[:1, :SO])
    ve.drain()
    ve.tensor_scalar(qn_row[:], qn_row[:], 1e-30, None, op0=ALU.max)
    sV.inc(ve.drain())
    ac.wait_ge(sV.h, sV.n)
    sA.inc(ac.activation(qn_row[:], qn_row[:], ACTF.Sqrt))
    sA.inc(ac.activation(maxkn[:], maxkn[:], ACTF.Sqrt))
    ve.wait_ge(sA.h, sA.n)
    ve.tensor_scalar(negMr[:], qn_row[:], maxkn[:1, :1], -1.0, op0=ALU.mult, op1=ALU.mult)
    sV.inc(ve.drain())
    pfree("npsum", "kp")
    free("nsb")
    bc_ps1 = palloc("bc_ps1", [128, 512])
    te.wait_ge(sV.h, sV.n)
    bcast(negM, negMr, SO, bc_ps1)
    pfree("bc_ps1")
    free("ksq", "ksb", "wb")
    free("hnf", "bet_b", "gam_b")
    free("negMr")
    b1_t, b1_v, b1_a = sT.n, sV.n, sA.n

    wv_sb = alloc("wv_sb", [128, DC * D])
    hbuf = alloc("hbuf", [128, 2 * DC * 128])
    vsb = alloc("vsb", [128, D])
    vp = palloc("vp", [128, 1024])
    sy.wait_ge(sT.h, b1_t)
    sy.wait_ge(sV.h, b1_v)
    for dc in range(DC):
        sL.inc(sy.dma_start(out=wv_sb[:, dc * D:(dc + 1) * D],
                            in_=qkv_hbm[dc * 128:(dc + 1) * 128, 2 * D:3 * D]), 16)
    te.wait_ge(sL.h, sL.n)
    te.wait_ge(sA.h, b1_a)
    te.wait_ge(sV.h, b1_v)
    huse = [0, 0]
    vevac = [0, 0]
    for m in range(KT):
        b = m % 2
        hb = hbuf[:, b * DC * 128:(b + 1) * DC * 128]
        sy.wait_ge(sT.h, huse[b])
        sH[b].inc(sy.dma_start(out=hb, in_=hn_sp[m]), 16)
        te.wait_ge(sH[b].h, sH[b].n)
        if m > 0:
            te.wait_ge(sKV.h, sKV.n)
        for ds in range(DSL):
            pbk = vp[:, (ds % 2) * 512:(ds % 2) * 512 + SLAB]
            te.wait_ge(sV.h, vevac[ds % 2])
            for dc in range(DC):
                sT.inc(te.matmul(pbk, hb[:, dc * 128:(dc + 1) * 128],
                                 wv_sb[:, dc * D + ds * SLAB: dc * D + (ds + 1) * SLAB],
                                 start=(dc == 0), stop=(dc == DC - 1)))
            ve.wait_ge(sT.h, sT.n)
            ve.tensor_copy(vsb[:, ds * SLAB:(ds + 1) * SLAB], pbk)
            vevac[ds % 2] = sV.inc(ve.drain())
        huse[b] = sT.n
        gp.wait_ge(sV.h, sV.n)
        sKV.inc(gp.dma_start(out=v_sp[:, :, m * 128:(m + 1) * 128], in_=vsb[:]), 16)
        wpump.pump(24)
    free("vsb", "hbuf", "wv_sb")
    pfree("vp")
    b2_t, b2_v = sT.n, sV.n

    ofm = alloc("ofm", [128, DC * SO])
    _qa_list = [SO - 128 * (m % QT) for m in range(KT)]
    ptp = alloc("ptp", [128, int(sum(_qa_list))])
    qres = alloc("qres", [128, DC * SO])
    kbuf = alloc("kbuf", [128, 2 * DC * 128])
    qa_list = _qa_list
    PTW = int(sum(qa_list))
    poff = np.cumsum([0] + qa_list)[:-1].tolist()
    sp_ps = palloc("sp_ps", [128, 2 * SO])
    l_ps = palloc("l_ps", [128, QT])
    sy.wait_ge(sT.h, b2_t)
    sy.wait_ge(sV.h, b2_v)
    sy.wait_ge(sS.h, sS.n)
    sy.wait_ge(sKV.h, sKV.n)
    for o in range(DC):
        sL.inc(sy.dma_start(out=qres[:, o * SO:(o + 1) * SO], in_=q_sp[o]), 16)
    te.wait_ge(sL.h, sL.n)
    te.wait_ge(sV.h, b2_v)
    kuse = [0, 0]
    pset_tok = [0, 0]
    ldk_tok = [0, 0]
    exp_tok = [0]
    first_l = True
    ldk_tok[0] = sX[0].inc(sy.dma_start(out=kbuf[:, :DC * 128], in_=k_sp[:, 0, :]), 16)
    for m in range(KT):
        b = m % 2
        kb = kbuf[:, b * DC * 128:(b + 1) * DC * 128]
        if m + 1 < KT:
            b2 = (m + 1) % 2
            sy.wait_ge(sT.h, kuse[b2])
            ldk_tok[b2] = sX[b2].inc(sy.dma_start(
                out=kbuf[:, b2 * DC * 128:(b2 + 1) * DC * 128],
                in_=k_sp[:, m + 1, :]), 16)
        qoff = 128 * (m % QT)
        qa = SO - qoff
        te.wait_ge(sX[b].h, ldk_tok[b])
        te.wait_ge(sV.h, pset_tok[b])
        base = b * SO
        nseg = (qa + SLAB - 1) // SLAB
        for seg in range(nseg):
            lo = seg * SLAB
            hi = min(qa, lo + SLAB)
            for dc in range(DC):
                sT.inc(te.matmul(
                    sp_ps[:, base + lo:base + hi],
                    kb[:, dc * 128:(dc + 1) * 128],
                    qres[:, dc * SO + qoff + lo: dc * SO + qoff + hi],
                    start=(dc == 0), stop=(dc == DC - 1)))
        kuse[b] = sT.n
        wpump.pump(5)
        pm = ptp[:, poff[m]:poff[m] + qa]
        ve.wait_ge(sT.h, sT.n)
        ve.tensor_tensor(pm, sp_ps[:, base:base + qa], negM[:, qoff:SO], op=ALU.add)
        sV.inc(ve.drain())
        pset_tok[b] = sV.n
        mk = mask_own if m < QT else mask_peer
        ve.tensor_tensor(ptp[:, poff[m]:poff[m] + 128],
                         ptp[:, poff[m]:poff[m] + 128], mk[:], op=ALU.add)
        sV.inc(ve.drain())
        ac.wait_ge(sV.h, sV.n)
        exp_tok[0] = sA.inc(ac.activation(pm, pm, ACTF.Exp, scale=c.scale))
        te.wait_ge(sA.h, sA.n)
        for jj in range(qa // 128):
            col = qoff // 128 + jj
            sT.inc(te.matmul(l_ps[:, col:col + 1],
                             ptp[:, poff[m] + jj * 128: poff[m] + (jj + 1) * 128],
                             ones[:], start=first_l, stop=(m == KT - 1),
                             skip_group_check=True))
            first_l = False
        wpump.pump(5)
    ve.wait_ge(sT.h, sT.n)
    ve.reciprocal(recip_l[:], l_ps[:])
    sV.inc(ve.drain())
    if c.debug:
        sy.wait_ge(sV.h, sV.n)
        sy.wait_ge(sA.h, sA.n)
        sS.inc(sy.dma_start(out=dbg_ptp[:], in_=ptp[:]), 16)
        sS.inc(sy.dma_start(out=dbg_l[:], in_=recip_l[:]), 16)
        sS.inc(sy.dma_start(out=dbg_negM[:], in_=negM[:]), 16)
    free("kbuf", "qres")
    pfree("l_ps", "sp_ps")
    b3_t, b3_v, b3_a = sT.n, sV.n, exp_tok[0]

    vbuf = alloc("vbuf", [128, 2 * KT * 128])
    o_ps = palloc("o_ps", [128, 2 * SO])
    vuse = [0, 0]
    oset_tok = [0, 0]
    sy.wait_ge(sT.h, b3_t)
    te.wait_ge(sV.h, b3_v)
    te.wait_ge(sA.h, b3_a)
    for cc in range(DC):
        b = cc % 2
        vb = vbuf[:, b * KT * 128:(b + 1) * KT * 128]
        sy.wait_ge(sT.h, vuse[b])
        ld = sU[b].inc(sy.dma_start(out=vb, in_=v_sp[:, cc, :]), 16)
        te.wait_ge(sU[b].h, ld)
        te.wait_ge(sV.h, oset_tok[b])
        base = b * SO
        for m in range(KT):
            qoff = 128 * (m % QT)
            for seg in range(OSL):
                lo = max(seg * SLAB, qoff)
                hi = (seg + 1) * SLAB
                if lo >= hi:
                    continue
                sT.inc(te.matmul(
                    o_ps[:, base + lo:base + hi],
                    vb[:, m * 128:(m + 1) * 128],
                    ptp[:, poff[m] + lo - qoff: poff[m] + hi - qoff],
                    start=(m == 0), stop=(m == KT - 1), skip_group_check=True))
        vuse[b] = sT.n
        ve.wait_ge(sT.h, sT.n)
        ve.tensor_copy(ofm[:, cc * SO:(cc + 1) * SO], o_ps[:, base:base + SO])
        oset_tok[b] = sV.inc(ve.drain())
        wpump.pump(20)
    if c.debug:
        sy.wait_ge(sA.h, sA.n)
        sS.inc(sy.dma_start(out=dbg_ofm[:], in_=ofm[:]), 16)
    free("vbuf")
    free("ptp")
    pfree("o_ps")
    b4_t, b4_v = sT.n, sV.n

    ao = alloc("ao", [128, QT * D])
    wos = alloc("wos", [128, DC * SLAB])
    ao_ps = palloc("ao_ps", [128, 1024])
    aevac = [0, 0]
    sy.wait_ge(sT.h, b4_t)
    te.wait_ge(sV.h, b4_v)
    for ds in range(DSL):
        te.wait_ge(sT.h, sT.n)
        sy.wait_ge(sT.h, sT.n)
        sWS.inc(sy.dma_start(
            out=wos[:], in_=wob_hbm[:, ds * DC * SLAB:(ds + 1) * DC * SLAB]), 16)
        te.wait_ge(sWS.h, sWS.n)
        for qc in range(QT):
            bk = qc % 2
            pbk = ao_ps[:, bk * 512:bk * 512 + SLAB]
            te.wait_ge(sV.h, aevac[bk])
            for dc in range(DC):
                sT.inc(te.matmul(pbk, ofm[:, dc * SO + qc * 128: dc * SO + (qc + 1) * 128],
                                 wos[:, dc * SLAB:(dc + 1) * SLAB],
                                 start=(dc == 0), stop=(dc == DC - 1)))
            ve.wait_ge(sT.h, sT.n)
            ve.tensor_scalar(ao[:, qc * D + ds * SLAB: qc * D + (ds + 1) * SLAB],
                             pbk, recip_l[:, qc:qc + 1], None, op0=ALU.mult)
            aevac[bk] = sV.inc(ve.drain())
    if c.debug:
        sy.wait_ge(sV.h, sV.n)
        sS.inc(sy.dma_start(out=dbg_ao[:], in_=ao[:]), 16)
        barrier()
    free("wos")
    pfree("ao_ps")
    hbt = alloc("hbt", [128, D])
    xbt = alloc("xbt", [128, D])
    z2t = alloc("z2t", [128, D])
    for qc in range(QT):
        sy.wait_ge(sV.h, sV.n)
        sL.inc(sy.dma_start(out=hbt[:], in_=h_sp[qc * 128:(qc + 1) * 128, :]), 16)
        ld = sL.inc(sy.dma_start(out=xbt[:], in_=x_hbm[qc * 128:(qc + 1) * 128, :]), 16)
        at = ao[:, qc * D:(qc + 1) * D]
        ve.wait_ge(sL.h, ld)
        ve.tensor_tensor(at, at, hbt[:], op=ALU.add)
        ve.drain()
        ln_stats(at, D)
        ln_rstd()
        ac.wait_ge(sS.h, sS.n)
        ln_apply(z2t[:], at)
        ve.wait_ge(sA.h, sA.n)
        ve.tensor_tensor(z2t[:], z2t[:], xbt[:], op=ALU.add)
        sV.inc(ve.drain())
        sy.wait_ge(sV.h, sV.n)
        sS.inc(sy.dma_start(out=x1_sp[qc * 128:(qc + 1) * 128, :], in_=z2t[:]), 16)
    wpump.drain()
    free("z2t", "xbt", "hbt", "ao", "ofm")
    free("wib", "wmb", "wpb", "wqb")
    barrier()

    free("negM", "maxkn", "qn_row")
    NSLOT = 3
    NH4 = c.NH4
    xiT = alloc("xiT", [128, DC * SO], BF16)
    ft = alloc("ft", [128, D])
    rs1 = alloc("rs1", [128, 1024])
    xit = alloc("xit", [128, D], BF16)
    wgsl = alloc("wgsl", [128, 2 * DC * SLAB], FP8)
    wvsl = alloc("wvsl", [128, 2 * DC * SLAB], FP8)
    gbb = alloc("gbb", [128, SLAB])
    vbb = alloc("vbb", [128, SLAB])
    gsl = alloc("gsl", [128, 2 * SLAB])
    vsl = alloc("vsl", [128, SLAB])
    sgn = alloc("sgn", [128, 4 * c.W6])
    ubuf = alloc("ubuf", [128, 4 * c.W6])
    scr16 = alloc("scr16", [128, 2 * c.W6], BF16)
    cm1275 = alloc("cm1275", [128, 1])
    c1285 = alloc("c1285", [128, 1])
    dsc = alloc("dsc", [128, 128])
    gvb = alloc("gvb", [128, NSLOT * H])
    pt16 = palloc("pt16", [128, 2048], BF16)
    gps2 = palloc("gps2", [128, 1024])
    vps2 = palloc("vps2", [128, 1024])
    ptq = palloc("ptq", [128, 1024])
    ve.memset(cm1275[:], -127.5)
    ve.memset(c1285[:], 128.5)
    ve.memset(ggvcol[:], 0.0)
    sV.inc(ve.drain())
    ac.wait_ge(sV.h, sV.n)

    slot_free = [None] * NSLOT
    wu = [0, 0]
    bias_rd = [0]
    silu_tok = [0, 0]
    pfree_tok = [0, 0]
    dsc_war = [0]
    evt6 = [0, 0]
    sgnwar = [0, 0, 0, 0]
    uwar = [0, 0, 0, 0]
    scr_tok = [0, 0]
    evt = [0, 0]
    xa_tok = [0] * QT
    ft_free = [0]

    def p6aA(qc):
        sy.wait_ge(sV.h, ft_free[0])
        ld = sX[0].inc(sy.dma_start(out=ft[:], in_=x1_sp[qc * 128:(qc + 1) * 128, :]), 16)
        ve.wait_ge(sX[0].h, ld)
        ln_stats(ft[:], D)
        ln_rstd()
        ln_apply(ft[:], ft[:])
        ve.wait_ge(sA.h, sA.n)
        ve.tensor_reduce(gxcol[:, qc:qc + 1], ft[:], axis=AX.X, op=ALU.max,
                         apply_absolute_value=True)
        ve.drain()
        ve.tensor_scalar(gxcol[:, qc:qc + 1], gxcol[:, qc:qc + 1], 1e-5, None, op0=ALU.max)
        ve.drain()
        ve.reciprocal(tt1[:], gxcol[:, qc:qc + 1])
        ve.drain()
        ve.tensor_scalar(sc_x[:, qc:qc + 1], tt1[:], 127.0, None, op0=ALU.mult)
        ve.drain()
        ve.tensor_scalar(tt0[:], gxcol[:, qc:qc + 1], 0.5 / 127.0, None, op0=ALU.mult)
        ve.drain()
        ve.tensor_tensor(scg_t[:, qc:qc + 1], tt0[:], gamw[:, 0:1], op=ALU.mult)
        ve.tensor_tensor(scv_t[:, qc:qc + 1], tt0[:], gamw[:, 1:2], op=ALU.mult)
        sV.inc(ve.drain())

    def p6aB(qc):
        HD = min(512, D // 2)
        wtok = sV.inc(ve.tensor_scalar(ft[:], ft[:], sc_x[:, qc:qc + 1], 128.5,
                                       op0=ALU.mult, op1=ALU.add))
        fin_tok = wtok
        for ps_ in range(D // (2 * HD)):
            ac.wait_ge(sV.h, fin_tok)
            ac.drain()
            up = {}
            for ci in range(2):
                sl_ = slice(ps_ * 2 * HD + ci * HD, ps_ * 2 * HD + (ci + 1) * HD)
                sA.inc(ac.activation(rs1[:, ci * HD:(ci + 1) * HD], ft[:, sl_],
                                     ACTF.Identity, bias=cm1275[:]))
                up[ci] = 0
            ac.drain()
            for j in range(8):
                kpow = -float(1 << (6 - j)) if j <= 6 else -0.5
                for ci in range(2):
                    sl_ = slice(ps_ * 2 * HD + ci * HD, ps_ * 2 * HD + (ci + 1) * HD)
                    uch = rs1[:, ci * HD:(ci + 1) * HD]
                    sch = xit[:, sl_]
                    if up[ci]:
                        ac.wait_ge(sV.h, up[ci])
                    sA.inc(ac.activation(sch, uch, ACTF.Sign, bias=chalf[:, 0:1]))
                    ve.wait_ge(sA.h, sA.n)
                    up[ci] = sV.inc(ve.scalar_tensor_tensor(
                        uch, sch, kpow, uch, op0=ALU.mult, op1=ALU.add))
            for ci in range(2):
                sl_ = slice(ps_ * 2 * HD + ci * HD, ps_ * 2 * HD + (ci + 1) * HD)
                ve.wait_ge(sV.h, up[ci])
                ve.scalar_tensor_tensor(xit[:, sl_], rs1[:, ci * HD:(ci + 1) * HD],
                                        -1.0, ft[:, sl_], op0=ALU.mult, op1=ALU.add)
                ve.drain()
                ve.tensor_scalar(xit[:, sl_], xit[:, sl_], -128.0, None, op0=ALU.add)
                fin_tok = sV.inc(ve.drain())
        ft_free[0] = fin_tok
        te.wait_ge(sV.h, ft_free[0])
        for dc in range(DC):
            pb = pt16[:, (dc % 2) * 1024:(dc % 2) * 1024 + 128]
            te.wait_ge(sA.h, evt[dc % 2])
            sT.inc(te.transpose(pb, xit[:, dc * 128:(dc + 1) * 128], eye16[:]))
            ac.wait_ge(sT.h, sT.n)
            evt[dc % 2] = sA.inc(ac.copy(
                xiT[:, dc * SO + qc * 128: dc * SO + (qc + 1) * 128], pb))
        xa_tok[qc] = sA.n

    def produce_gen(qg):
        for s in range(HSL):
            b = s % 2
            wgb = wgsl[:, b * DC * SLAB:(b + 1) * DC * SLAB]
            wvb2 = wvsl[:, b * DC * SLAB:(b + 1) * DC * SLAB]
            if s == 0:
                te.wait_ge(sA.h, max(xa_tok[qg * c.QG], xa_tok[qg * c.QG + 1]))
            sy.wait_ge(sT.h, wu[b])
            sW[b].inc(sy.dma_start(out=wgb[:], in_=wgi_sp[s]), 16)
            sW[b].inc(sy.dma_start(out=wvb2[:], in_=wvi_sp[s]), 16)
            ldw = sW[b].n
            sy.wait_ge(sV.h, bias_rd[0])
            sH[0].inc(sy.dma_start(out=gbb[:],
                                   in_=gbbt_hbm[:, s * SLAB:(s + 1) * SLAB]), 16)
            ld2 = sH[0].inc(sy.dma_start(out=vbb[:],
                                         in_=vbbt_hbm[:, s * SLAB:(s + 1) * SLAB]), 16)
            te.wait_ge(sW[b].h, ldw)
            for j in range(c.QG):
                qc = qg * c.QG + j
                slot = qc % NSLOT
                gvs = gvb[:, slot * H + s * SLAB: slot * H + (s + 1) * SLAB]
                gps_j = gps2[:, j * SLAB:(j + 1) * SLAB]
                vps_j = vps2[:, j * SLAB:(j + 1) * SLAB]
                gsl_j = gsl[:, j * SLAB:(j + 1) * SLAB]
                te.wait_ge(sV.h, pfree_tok[j])
                for dc in range(DC):
                    sT.inc(te.matmul(gps_j,
                                     xiT[:, dc * SO + qc * 128: dc * SO + (qc + 1) * 128],
                                     wgb[:, dc * SLAB:(dc + 1) * SLAB],
                                     start=(dc == 0), stop=(dc == DC - 1)))
                tok_g = sT.n
                for dc in range(DC):
                    sT.inc(te.matmul(vps_j,
                                     xiT[:, dc * SO + qc * 128: dc * SO + (qc + 1) * 128],
                                     wvb2[:, dc * SLAB:(dc + 1) * SLAB],
                                     start=(dc == 0), stop=(dc == DC - 1)))
                tok_v = sT.n
                ve.wait_ge(sT.h, tok_g)
                ve.wait_ge(sH[0].h, ld2)
                ve.wait_ge(sA.h, silu_tok[j])
                ve.scalar_tensor_tensor(gsl_j, gps_j, scg_t[:, qc:qc + 1],
                                        gbb[:], op0=ALU.mult, op1=ALU.add)
                sV.inc(ve.drain())
                ac.wait_ge(sV.h, sV.n)
                silu_tok[j] = sA.inc(ac.activation(gsl_j, gsl_j, ACTF.Silu))
                ve.wait_ge(sT.h, tok_v)
                ve.scalar_tensor_tensor(vsl[:], vps_j, scv_t[:, qc:qc + 1],
                                        vbb[:], op0=ALU.mult, op1=ALU.add)
                pfree_tok[j] = sV.inc(ve.drain())
                ve.wait_ge(sA.h, silu_tok[j])
                if slot_free[slot] is not None:
                    si, stok = slot_free[slot][s]
                    ve.wait_ge(sP6[si].h, stok)
                ve.tensor_tensor(gvs, gsl_j, vsl[:], op=ALU.mult)
                ve.drain()
                ve.tensor_reduce(tt0[:], gvs, axis=AX.X, op=ALU.max,
                                 apply_absolute_value=True)
                ve.drain()
                ve.tensor_tensor(ggvcol[:, qc:qc + 1], ggvcol[:, qc:qc + 1], tt0[:],
                                 op=ALU.max)
                sV.inc(ve.drain())
            bias_rd[0] = sV.n
            wu[b] = sT.n
            yield

    def qA(qc):
        slot = qc % NSLOT
        base = slot * H
        ve.tensor_scalar(ggvcol[:, qc:qc + 1], ggvcol[:, qc:qc + 1], 1e-5, None,
                         op0=ALU.max)
        ve.drain()
        ve.reciprocal(tt1[:], ggvcol[:, qc:qc + 1])
        ve.drain()
        ve.tensor_scalar(sc_gv[:, qc:qc + 1], tt1[:], 127.0, None, op0=ALU.mult)
        ve.drain()
        ve.tensor_scalar(tt0[:], ggvcol[:, qc:qc + 1], 0.5 / 127.0, None, op0=ALU.mult)
        ve.drain()
        ve.tensor_tensor(sc3[:, qc:qc + 1], tt0[:], gamw[:, 2:3], op=ALU.mult)
        ve.drain()
        ve.wait_ge(sT.h, dsc_war[0])
        ve.tensor_scalar(dsc[:], eye32[:], sc_gv[:, qc:qc + 1], None, op0=ALU.mult)
        dsc_tok = sV.inc(ve.drain())
        te.wait_ge(sV.h, dsc_tok)
        for ch in range(HC):
            p = ch % 2
            pb = ptq[:, p * 512:p * 512 + 128]
            blk = gvb[:, base + ch * 128: base + (ch + 1) * 128]
            te.wait_ge(sA.h, evt6[p])
            sT.inc(te.matmul(pb, blk, dsc[:], start=True, stop=True))
            ac.wait_ge(sT.h, sT.n)
            evt6[p] = sA.inc(ac.activation(blk, pb, ACTF.Identity, bias=c1285[:]))
        dsc_war[0] = sT.n
        slot_free[slot] = {}

    def pair_gen(qc):
        slot = qc % NSLOT
        base = slot * H
        W6 = c.W6
        for k0 in range(0, c.NHW, 4):
            pair = (k0, k0 + 1, k0 + 2, k0 + 3)
            upd = {}
            ac.drain()
            for ki in pair:
                i = ki % 4
                wch = gvb[:, base + ki * W6: base + (ki + 1) * W6]
                uch = ubuf[:, i * W6:(i + 1) * W6]
                if uwar[i]:
                    ac.wait_ge(sV.h, uwar[i])
                sA.inc(ac.activation(uch, wch, ACTF.Identity, bias=cm1275[:]))
                upd[ki] = sgnwar[i]
            ac.drain()
            for j in range(8):
                kpow = -float(1 << (6 - j)) if j <= 6 else -0.5
                for ki in pair:
                    i = ki % 4
                    uch = ubuf[:, i * W6:(i + 1) * W6]
                    sch = sgn[:, i * W6:(i + 1) * W6]
                    if upd[ki]:
                        ac.wait_ge(sV.h, upd[ki])
                    sA.inc(ac.activation(sch, uch, ACTF.Sign, bias=chalf[:, 0:1]))
                    ve.wait_ge(sA.h, sA.n)
                    upd[ki] = sV.inc(ve.scalar_tensor_tensor(
                        uch, sch, kpow, uch, op0=ALU.mult, op1=ALU.add))
            for ki in pair:
                i = ki % 4
                p2 = ki % 2
                sgnwar[i] = upd[ki]
                wch = gvb[:, base + ki * W6: base + (ki + 1) * W6]
                ve.wait_ge(sV.h, upd[ki])
                uwar[i] = sV.inc(ve.scalar_tensor_tensor(
                    wch, ubuf[:, i * W6:(i + 1) * W6], -1.0, wch,
                    op0=ALU.mult, op1=ALU.add))
                ac.wait_ge(sV.h, uwar[i])
                if scr_tok[p2]:
                    ac.wait_ge(sP6[p2].h, scr_tok[p2])
                ct = sA.inc(ac.copy(scr16[:, p2 * W6:(p2 + 1) * W6], wch))
                gp.wait_ge(sA.h, ct)
                scr_tok[p2] = sP6[p2].inc(gp.dma_start(
                    out=x2i_sp[qc][:, ki * W6:(ki + 1) * W6],
                    in_=scr16[:, p2 * W6:(p2 + 1) * W6]), 16)
                slot_free[slot][ki] = (p2, scr_tok[p2])
            yield

    assert c.QG == 2 and HSL % 4 == 0 and c.NHW == HSL
    p6a_pos = [3 * HSL // 16, 5 * HSL // 16, 11 * HSL // 16, 13 * HSL // 16]
    p6aA(0)
    p6aB(0)
    p6aA(1)
    p6aB(1)
    for qg in range(c.NQG):
        pairs = []
        if qg > 0:
            for q in (qg * 2 - 2, qg * 2 - 1):
                qA(q)
                pairs.append(pair_gen(q))
        nxt = [qg * 2 + 2, qg * 2 + 3]
        pg = produce_gen(qg)
        for s in range(HSL):
            if pairs:
                if s % 4 == 0:
                    next(pairs[0], None)
                elif s % 4 == 2:
                    next(pairs[1], None)
            if s == p6a_pos[0] and nxt[0] < QT:
                p6aA(nxt[0])
            if s == p6a_pos[1] and nxt[0] < QT:
                p6aB(nxt[0])
            if s == p6a_pos[2] and nxt[1] < QT:
                p6aA(nxt[1])
            if s == p6a_pos[3] and nxt[1] < QT:
                p6aB(nxt[1])
            next(pg)
        for g_ in pairs:
            for _ in g_:
                pass
    for q in (QT - 2, QT - 1):
        qA(q)
    for _ in zip(pair_gen(QT - 2), pair_gen(QT - 1)):
        pass
    free("gvb", "dsc", "c1285", "cm1275", "scr16", "ubuf", "sgn", "vsl",
         "gsl", "vbb", "gbb", "wvsl", "wgsl",
         "xit", "rs1", "ft", "xiT")
    pfree("ptq", "vps2", "gps2", "pt16")
    barrier()

    wo3 = alloc("wo3", [128, HC * D], FP8)
    xb2 = alloc("xb2", [128, 2 * HC * 128], BF16)
    o3sb = alloc("o3sb", [128, D])
    x1b = alloc("x1b", [128, D])
    z4t = alloc("z4t", [128, D])
    obb = alloc("obb", [128, D])
    csb = alloc("csb", [128, D])
    sL.inc(sy.dma_start(out=obb[:], in_=obbt_hbm[:]), 16)
    NW7 = max(1, HC // 16)
    for k7 in range(NW7):
        w7 = HC * D // NW7
        sL.inc(sy.dma_start(out=wo3[:, k7 * w7:(k7 + 1) * w7],
                            in_=woi_sp[:, k7 * w7:(k7 + 1) * w7]), 16)
    te.wait_ge(sL.h, sL.n)
    cs_ps = palloc("cs_ps", [1, D])
    for hc in range(HC):
        for dsl_ in range(DSL):
            sT.inc(te.matmul(cs_ps[:1, dsl_ * SLAB:(dsl_ + 1) * SLAB], ones16[:],
                             wo3[:, hc * D + dsl_ * SLAB: hc * D + (dsl_ + 1) * SLAB],
                             start=(hc == 0), stop=(hc == HC - 1)))
    ve.wait_ge(sT.h, sT.n)
    ve.tensor_scalar(csb[:1, :], cs_ps[:1, :], 128.0, None, op0=ALU.mult)
    sV.inc(ve.drain())
    pfree("cs_ps")
    bc_ps7 = palloc("bc_ps7", [128, 512])
    te.wait_ge(sV.h, sV.n)
    bcast(csb, csb[:1, :], D, bc_ps7)
    pfree("bc_ps7")
    o3ps = palloc("o3ps", [128, 2048])
    te.wait_ge(sA.h, sA.n)
    xu = [0, 0]
    ev7 = [0, 0]
    ld7 = [0, 0]
    ld7[0] = sX[0].inc(sy.dma_start(out=xb2[:, :HC * 128], in_=x2i_sp[0]), 16)
    for qc in range(QT):
        b7 = qc % 2
        xbb = xb2[:, b7 * HC * 128:(b7 + 1) * HC * 128]
        sy.wait_ge(sV.h, sV.n)
        lx7 = sK.inc(sy.dma_start(out=x1b[:], in_=x1_sp[qc * 128:(qc + 1) * 128, :]), 16)
        if qc + 1 < QT:
            b2 = (qc + 1) % 2
            sy.wait_ge(sT.h, xu[b2])
            ld7[b2] = sX[b2].inc(sy.dma_start(
                out=xb2[:, b2 * HC * 128:(b2 + 1) * HC * 128], in_=x2i_sp[qc + 1]), 16)
        te.wait_ge(sX[b7].h, ld7[b7])
        for dg in range((DSL + 1) // 2):
            dss = range(2 * dg, min(2 * dg + 2, DSL))
            te.wait_ge(sV.h, ev7[dg % 2])
            for hc in range(HC):
                for ds in dss:
                    sT.inc(te.matmul(o3ps[:, ds * SLAB:(ds + 1) * SLAB],
                                     xbb[:, hc * 128:(hc + 1) * 128],
                                     wo3[:, hc * D + ds * SLAB: hc * D + (ds + 1) * SLAB],
                                     start=(hc == 0), stop=(hc == HC - 1)))
            xu[b7] = sT.n
            ve.wait_ge(sT.h, sT.n)
            ve.wait_ge(sA.h, sA.n)
            for ds in dss:
                ve.tensor_tensor(o3sb[:, ds * SLAB:(ds + 1) * SLAB],
                                 o3ps[:, ds * SLAB:(ds + 1) * SLAB],
                                 csb[:, ds * SLAB:(ds + 1) * SLAB], op=ALU.subtract)
                ve.drain()
                ve.scalar_tensor_tensor(o3sb[:, ds * SLAB:(ds + 1) * SLAB],
                                        o3sb[:, ds * SLAB:(ds + 1) * SLAB],
                                        sc3[:, qc:qc + 1], obb[:, ds * SLAB:(ds + 1) * SLAB],
                                        op0=ALU.mult, op1=ALU.add)
                ve.drain()
            ev7[dg % 2] = sV.inc(ve.drain())
        ln_stats(o3sb[:], D)
        ln_rstd()
        ac.wait_ge(sO.h, sO.n)
        ln_apply(z4t[:], o3sb[:])
        ve.wait_ge(sA.h, sA.n)
        ve.wait_ge(sK.h, lx7)
        ve.tensor_tensor(z4t[:], z4t[:], x1b[:], op=ALU.add)
        sV.inc(ve.drain())
        sy.wait_ge(sV.h, sV.n)
        sO.inc(sy.dma_start(out=out_hbm[qc * 128:(qc + 1) * 128, :], in_=z4t[:]), 16)
    sy.wait_ge(sO.h, sO.n)
    barrier()

    for cm in reversed(list(live.values())):
        cm.__exit__(None, None, None)
    for cm in reversed(list(psum_live.values())):
        cm.__exit__(None, None, None)
    for cm in reversed(sem_cms):
        cm.__exit__(None, None, None)
    return nc


def _bf16():
    import ml_dtypes
    return ml_dtypes.bfloat16


def reshape_blocked(w, out_blk=128):
    Dd, N = w.shape
    dcn = Dd // 128
    nb = N // out_blk
    return np.ascontiguousarray(
        w.reshape(dcn, 128, nb, out_blk).transpose(1, 2, 0, 3).reshape(128, -1))


def make_core_inputs(cfg, x_b, parity, shared):
    own = np.ascontiguousarray(x_b[parity::2])
    peer = np.ascontiguousarray(x_b[1 - parity::2])
    xr = np.concatenate([own, peer], 0)
    ii = np.arange(128)[:, None]
    jj = np.arange(128)[None, :]
    mask_own = np.where(ii <= jj, 0.0, NEG_MASK).astype(np.float32)
    if parity == 0:
        mask_peer = np.where(ii < jj, 0.0, NEG_MASK).astype(np.float32)
    else:
        mask_peer = np.where(ii <= jj, 0.0, NEG_MASK).astype(np.float32)
    d = {"x": xr, "mask_own": mask_own, "mask_peer": mask_peer}
    d.update(shared)
    return d


def kernel(L0_attn_x, L0_attn_qkv, L0_attn_o_proj, L0_attn_gamma, L0_attn_beta,
           L0_ffn_gate_w, L0_ffn_gate_b, L0_ffn_val_w, L0_ffn_val_b,
           L0_ffn_out_w, L0_ffn_out_b):
    cfg = Cfg(D=2048, S=2048)
    x = np.asarray(L0_attn_x, dtype=np.float32)
    qkv_f = np.asarray(L0_attn_qkv, dtype=np.float32)
    wo_f = np.asarray(L0_attn_o_proj, dtype=np.float32)
    shared = {
        "qkv": np.ascontiguousarray(qkv_f),
        "qkvc": reshape_blocked(qkv_f, 128),
        "wob": reshape_blocked(wo_f, 512),
        "gamma": np.asarray(L0_attn_gamma, dtype=np.float32),
        "beta": np.asarray(L0_attn_beta, dtype=np.float32),
        "gwT": np.ascontiguousarray(np.asarray(L0_ffn_gate_w, dtype=np.float32).T),
        "vwT": np.ascontiguousarray(np.asarray(L0_ffn_val_w, dtype=np.float32).T),
        "owT": np.ascontiguousarray(np.asarray(L0_ffn_out_w, dtype=np.float32).T),
        "gate_b": np.asarray(L0_ffn_gate_b, dtype=np.float32),
        "gbbt": np.ascontiguousarray(np.broadcast_to(
            np.asarray(L0_ffn_gate_b, dtype=np.float32), (128, len(L0_ffn_gate_b)))),
        "vbbt": np.ascontiguousarray(np.broadcast_to(
            np.asarray(L0_ffn_val_b, dtype=np.float32), (128, len(L0_ffn_val_b)))),
        "obbt": np.ascontiguousarray(np.broadcast_to(
            np.asarray(L0_ffn_out_b, dtype=np.float32), (128, len(L0_ffn_out_b)))),
        "val_b": np.asarray(L0_ffn_val_b, dtype=np.float32),
        "out_b": np.asarray(L0_ffn_out_b, dtype=np.float32),
        "eye32": np.eye(128, dtype=np.float32),
        "eye16": np.eye(128, dtype=np.float32).astype(_bf16()),
    }
    nc = build_program(cfg)
    in_maps = []
    for core in range(8):
        b, p = divmod(core, 2)
        in_maps.append(make_core_inputs(cfg, x[b], p, shared))
    res = run_bass_kernel_spmd(nc, in_maps, core_ids=list(range(8)))
    out = np.empty((4, cfg.S, cfg.D), dtype=np.float32)
    for core in range(8):
        b, p = divmod(core, 2)
        out[b, p::2] = res.results[core]["out"]
    return out


# version 29
# speedup vs baseline: 1.1347x; 1.1036x over previous
import math
import numpy as np

import concourse.bass as bass
import concourse.mybir as mybir
from concourse.bass_utils import run_bass_kernel_spmd

F32 = mybir.dt.float32
BF16 = mybir.dt.bfloat16
FP8 = mybir.dt.float8e4
AX = mybir.AxisListType
ALU = mybir.AluOpType
ACTF = mybir.ActivationFunctionType

NEG_MASK = -1.0e33
EPS = 1.0e-5


class Cfg:
    def __init__(self, D=2048, S=2048, H=None):
        self.D = D
        self.S = S
        self.H = H if H is not None else 4 * D
        self.SO = S // 2
        self.DC = D // 128
        self.QT = self.SO // 128
        self.KT = S // 128
        self.HC = self.H // 128
        self.SLAB = min(512, self.H, D, self.SO)
        self.DSL = D // self.SLAB
        self.HSL = self.H // self.SLAB
        self.SSL = S // self.SLAB
        self.OSL = self.SO // self.SLAB
        self.QG = min(2, self.QT)
        self.NQG = self.QT // self.QG
        self.NH4 = self.H // 512
        self.W6 = 512
        self.NHW = self.H // self.W6
        self.scale = 1.0 / math.sqrt(float(D))
        self.debug = False
        assert self.SLAB == 512, "psum accumulation slabs must be bank-aligned"
        assert D % 128 == 0 and self.SO % 128 == 0 and self.H % self.SLAB == 0
        assert D % self.SLAB == 0 and self.SO % self.SLAB == 0
        assert self.NH4 % 2 == 0 and self.QT % self.QG == 0
        assert self.NHW % 4 == 0 and self.W6 % 512 == 0


class Sem:
    def __init__(self, h):
        self.h = h
        self.n = 0

    def inc(self, instr, k=1):
        instr.then_inc(self.h, k)
        self.n += k
        return self.n


def build_program(c: Cfg):
    nc = bass.Bass("TRN2", target_bir_lowering=False)
    D, S, H, SO = c.D, c.S, c.H, c.SO
    DC, QT, KT, HC = c.DC, c.QT, c.KT, c.HC
    SLAB, DSL, HSL, OSL = c.SLAB, c.DSL, c.HSL, c.OSL

    din = lambda n, sh, dt=F32: nc.dram_tensor(n, sh, dt, kind="ExternalInput").ap()
    x_hbm = din("x", [S, D])
    qkv_hbm = din("qkv", [D, 3 * D])
    qkvc_hbm = din("qkvc", [128, (3 * D // 128) * DC * 128])
    wob_hbm = din("wob", [128, DSL * DC * SLAB])
    gamma_hbm = din("gamma", [D])
    beta_hbm = din("beta", [D])
    gwT_hbm = din("gwT", [D, H])
    vwT_hbm = din("vwT", [D, H])
    owT_hbm = din("owT", [H, D])
    gb_hbm = din("gate_b", [H])
    vb_hbm = din("val_b", [H])
    ob_hbm = din("out_b", [D])
    gbbt_hbm = din("gbbt", [128, H])
    vbbt_hbm = din("vbbt", [128, H])
    obbt_hbm = din("obbt", [128, D])
    mko_hbm = din("mask_own", [128, 128])
    mkp_hbm = din("mask_peer", [128, 128])
    eye32_hbm = din("eye32", [128, 128])
    eye16_hbm = din("eye16", [128, 128], BF16)
    out_hbm = nc.dram_tensor("out", [SO, D], F32, kind="ExternalOutput").ap()

    dint = lambda n, sh, dt=F32: nc.dram_tensor(n, sh, dt, kind="Internal").ap()
    hn_sp = dint("hn_sp", [KT, 128, DC * 128])
    k_sp = dint("k_sp", [128, KT, DC * 128])
    v_sp = dint("v_sp", [128, DC, KT * 128])
    q_sp = dint("q_sp", [DC, 128, SO])
    h_sp = dint("h_sp", [SO, D])
    x1_sp = dint("x1_sp", [SO, D])
    wgi_sp = dint("wgi_sp", [HSL, 128, DC * SLAB], FP8)
    wvi_sp = dint("wvi_sp", [HSL, 128, DC * SLAB], FP8)
    woi_sp = dint("woi_sp", [128, HC * D], FP8)
    x2i_sp = dint("x2i_sp", [QT, 128, H], BF16)
    if c.debug:
        qa_l = [SO - 128 * (m % QT) for m in range(KT)]
        dbg_ptp = dint("dbg_ptp", [128, int(sum(qa_l))])
        dbg_l = dint("dbg_l", [128, QT])
        dbg_negM = dint("dbg_negM", [128, SO])
        dbg_ofm = dint("dbg_ofm", [128, DC * SO])
        dbg_ao = dint("dbg_ao", [128, QT * D])

    te, ve, ac, gp, sy = nc.tensor, nc.vector, nc.scalar, nc.gpsimd, nc.sync

    sem_cms = []

    def mksem(name):
        cm = nc.semaphore(name)
        h = cm.__enter__()
        sem_cms.append(cm)
        return Sem(h)

    sL = mksem("sL")
    sX = [mksem("sX0"), mksem("sX1")]
    sW = [mksem("sW0"), mksem("sW1")]
    sH = [mksem("sH0"), mksem("sH1")]
    sU = [mksem("sU0"), mksem("sU1")]
    sK = mksem("sK")
    sWS = mksem("sWS")
    sQW = [mksem("sQW0"), mksem("sQW1"), mksem("sQW2"), mksem("sQW3")]
    sA5 = mksem("sA5")
    sV5 = mksem("sV5")
    sS = mksem("sS")
    sP = mksem("sP")
    sP6 = [mksem("sP6a"), mksem("sP6b")]
    sKV = mksem("sKV")
    sT = mksem("sT")
    sV = mksem("sV")
    sA = mksem("sA")
    sG = mksem("sG")
    sO = mksem("sO")
    sB = mksem("sB")

    def barrier(full=True):
        drains = [sS, sL, sO, sK, sWS, sKV, sX[0], sX[1], sW[0], sW[1],
                  sH[0], sH[1], sU[0], sU[1]]
        if full:
            drains += [sP] + sP6 + sQW
        for s_ in drains:
            sy.wait_ge(s_.h, s_.n)
        gp.wait_ge(sG.h, sG.n)
        engs = [te, ve, ac, gp, sy]
        for e in engs:
            sB.inc(e.drain() if e is not sy else e.nop())
        for e in engs:
            e.wait_ge(sB.h, sB.n)

    live = {}

    def alloc(name, sh, dt=F32):
        cm = nc.sbuf_tensor(name, sh, dt)
        h = cm.__enter__()
        live[name] = cm
        return h

    def alloc_r(name, sh, dt=F32):
        cm = nc.sbuf_tensor(name, sh, dt, side="right")
        h = cm.__enter__()
        live[name] = cm
        return h

    def free(*names):
        for n in names:
            live.pop(n).__exit__(None, None, None)

    psum_live = {}

    def palloc(name, sh, dt=F32):
        cm = nc.psum_tensor(name, sh, dt)
        h = cm.__enter__()
        psum_live[name] = cm
        return h

    def pfree(*names):
        for n in names:
            psum_live.pop(n).__exit__(None, None, None)

    ones = alloc("ones", [128, 1])
    ones16 = alloc("ones16", [128, 1], BF16)
    ones_row = alloc("ones_row", [1, 128])
    eye32 = alloc("eye32s", [128, 128])
    eye16 = alloc("eye16s", [128, 128], BF16)
    mask_own = alloc("mask_own_s", [128, 128])
    mask_peer = alloc("mask_peer_s", [128, 128])
    nstat = 6 * ((D + 511) // 512)
    stat = alloc("stat", [128, nstat])
    agg = alloc("agg", [128, 2])
    vep = alloc("vep", [128, 1])
    rr0 = alloc("rr0", [128, 1])
    tt0 = alloc("tt0", [128, 1])
    tt1 = alloc("tt1", [128, 1])
    recip_l = alloc("recip_l", [128, QT])
    gxcol = alloc("gxcol", [128, QT])
    ggvcol = alloc("ggvcol", [128, QT])
    sc_x = alloc("sc_x", [128, QT])
    sc_gv = alloc("sc_gv", [128, QT])
    scg_t = alloc("scg_t", [128, QT])
    scv_t = alloc("scv_t", [128, QT])
    sc3 = alloc("sc3", [128, QT])
    gwpart = alloc("gwpart", [128, 2])
    chalf = alloc("chalf", [128, 2])
    gwsum = alloc("gwsum", [128, 3])
    gwtot_sb = alloc("gwtot_sb", [1, 3])
    gwrec_sb = alloc("gwrec_sb", [1, 3])
    rgam = alloc("rgam", [128, 3])
    gamw = alloc("gamw", [128, 3])
    negM = alloc("negM", [128, SO])
    qn_row = alloc("qn_row", [1, SO])
    maxkn = alloc("maxkn", [1, 1])
    negMr = alloc("negMr", [1, SO])

    def ln_stats(x_ap, nfree):
        nchunks = (nfree + 511) // 512
        for i in range(nchunks):
            lo = i * 512
            hi = min(nfree, lo + 512)
            ve.bn_stats(stat[:, 6 * i:6 * (i + 1)], x_ap[:, lo:hi])
        ve.drain()
        ve.bn_aggr(agg[:], stat[:, :6 * nchunks])
        ve.drain()

    def ln_rstd():
        ve.tensor_scalar(vep[:], agg[:, 1:2], EPS, None, op0=ALU.add)
        sV.inc(ve.drain())
        ac.wait_ge(sV.h, sV.n)
        sA.inc(ac.activation(tt0[:], vep[:], ACTF.Sqrt))
        ve.wait_ge(sA.h, sA.n)
        ve.reciprocal(rr0[:], tt0[:])
        for _ in range(2):
            ve.drain()
            ve.tensor_tensor(tt0[:], rr0[:], rr0[:], op=ALU.mult)
            ve.drain()
            ve.tensor_tensor(tt0[:], tt0[:], vep[:], op=ALU.mult)
            ve.drain()
            ve.tensor_scalar(tt0[:], tt0[:], -0.5, 1.5, op0=ALU.mult, op1=ALU.add)
            ve.drain()
            ve.tensor_tensor(rr0[:], rr0[:], tt0[:], op=ALU.mult)
        ve.drain()
        ve.tensor_scalar(tt1[:], agg[:, 0:1], -1.0, rr0[:], op0=ALU.mult, op1=ALU.mult)
        sV.inc(ve.drain())

    def ln_apply(dst, src):
        ac.wait_ge(sV.h, sV.n)
        ac.drain()
        sA.inc(ac.activation(dst, src, ACTF.Identity, bias=tt1[:], scale=rr0[:]))

    def bcast(dst, src_row, n, ps, war_cell=None):
        for lo in range(0, n, 512):
            w = min(512, n - lo)
            te.wait_ge(sA.h, sA.n if war_cell is None else war_cell[0])
            sT.inc(te.matmul(ps[:, :w], ones_row[:1, :], src_row[:1, lo:lo + w],
                             start=True, stop=True))
            ac.wait_ge(sT.h, sT.n)
            sA.inc(ac.copy(dst[:, lo:lo + w], ps[:, :w]))
            if war_cell is not None:
                war_cell[0] = sA.n
        return sT.n

    def round_ladder(f_ap, sc_ap, out_ap, w_scr, u_scr, p_scr):
        ve.tensor_scalar(w_scr, f_ap, sc_ap, 128.5, op0=ALU.mult, op1=ALU.add)
        ve.drain()
        ve.tensor_copy(u_scr, w_scr)
        ve.drain()
        for j in range(7, -1, -1):
            k = float(1 << j)
            ve.tensor_scalar(p_scr, u_scr, k, None, op0=ALU.is_ge)
            ve.drain()
            ve.scalar_tensor_tensor(u_scr, p_scr, -k, u_scr, op0=ALU.mult, op1=ALU.add)
            ve.drain()
        ve.scalar_tensor_tensor(p_scr, u_scr, -1.0, w_scr, op0=ALU.mult, op1=ALU.add)
        ve.drain()
        ve.tensor_scalar(out_ap, p_scr, -128.0, None, op0=ALU.add)
        ve.drain()

    WCW = 512
    wqb = alloc_r("wqb", [128, 4 * WCW])
    wpb = alloc_r("wpb", [128, 2 * WCW])
    wmb = alloc_r("wmb", [128, 2 * WCW])
    wib = alloc_r("wib", [128, 2 * WCW], FP8)
    NCG = H // WCW

    def wchunks(src_hbm, mat_sp):
        out = []
        for dc in range(DC):
            for cg in range(NCG):
                def store(wib_, sem, dc=dc, cg=cg):
                    s_lo = (cg * WCW) // SLAB
                    s_hi = ((cg + 1) * WCW) // SLAB
                    for s in range(s_lo, s_hi):
                        off = s * SLAB - cg * WCW
                        sem.inc(gp.dma_start(
                            out=mat_sp[s][:, dc * SLAB:(dc + 1) * SLAB],
                            in_=wib_[:, off:off + SLAB]), 16)
                out.append((src_hbm[dc * 128:(dc + 1) * 128, cg * WCW:(cg + 1) * WCW],
                            WCW, store))
        return out

    def owt_chunks():
        out = []
        for ch in range(HC):
            for hf in range(D // WCW):
                def store(wib_, sem, ch=ch, hf=hf):
                    sem.inc(gp.dma_start(
                        out=woi_sp[:, ch * D + hf * WCW: ch * D + (hf + 1) * WCW],
                        in_=wib_[:, :WCW]), 16)
                out.append((owT_hbm[ch * 128:(ch + 1) * 128,
                                    hf * WCW:(hf + 1) * WCW], WCW, store))
        return out

    wq_us = [0, 0, 0, 0]

    class Pump:
        DEPTH = 4

        def __init__(self, stages):
            self.items = []
            self.nslot = 0
            for compute, idx, chunks in stages:
                if callable(chunks):
                    self.items.append((compute, idx, False, None, None))
                    continue
                for ci, ck in enumerate(chunks):
                    self.items.append((compute, idx, ci == 0, ck, self.nslot))
                    self.nslot += 1
            self.n = len(self.items)
            self.cursor = 0
            self.loaded = 0
            self.lds = {}

        def _load(self):
            i = self.loaded
            item = self.items[i]
            if item[3] is not None:
                compute, idx, first, (src_, ncols, store_fn), sl = item
                b = sl % 4
                gp.wait_ge(sA5.h, wq_us[b])
                self.lds[i] = sQW[b].inc(
                    gp.dma_start(out=wqb[:, b * WCW:b * WCW + ncols], in_=src_), 16)
            self.loaded += 1

        def pump(self, k):
            while k > 0 and self.cursor < self.n:
                while self.loaded < min(self.cursor + self.DEPTH, self.n):
                    self._load()
                compute, idx, first, ck, sl = self.items[self.cursor]
                if ck is None:
                    compute(idx)
                else:
                    self.compute_chunk(compute, self.cursor, idx, first, ck, sl)
                self.cursor += 1
                k -= 1

        def compute_chunk(self, compute, i, idx, first, ck, sl):
            src_, ncols, store_fn = ck
            compute(self, i, idx, first, ncols, store_fn, sl)

        def drain(self):
            self.pump(self.n)

    def gamma_compute(pp, i, idx, first, ncols, store_fn, sl):
        b = sl % 4
        wb_ = wqb[:, b * WCW:b * WCW + ncols]
        if first:
            ve.wait_ge(sA5.h, sA5.n)
            ve.memset(gwsum[:, idx:idx + 1], 0.0)
            sV5.inc(ve.drain())
            ac.wait_ge(sV5.h, sV5.n)
        if not hasattr(pp, "gv_tok"):
            pp.gv_tok = [0, 0]
        h2 = sl % 2
        ac.wait_ge(sQW[b].h, pp.lds[i])
        if pp.gv_tok[h2]:
            ac.wait_ge(sV5.h, pp.gv_tok[h2])
        if getattr(pp, "sg_tok", None) and pp.sg_tok[h2]:
            ac.wait_ge(sG.h, pp.sg_tok[h2])
        ac.drain()
        sA5.inc(ac.activation(wpb[:, h2 * WCW:h2 * WCW + ncols], wb_,
                              ACTF.Abs, accum_out=gwpart[:, h2:h2 + 1]))
        wq_us[b] = sA5.n
        ve.wait_ge(sA5.h, sA5.n)
        ve.tensor_tensor(gwsum[:, idx:idx + 1], gwsum[:, idx:idx + 1],
                         gwpart[:, h2:h2 + 1], op=ALU.add)
        pp.gv_tok[h2] = sV5.inc(ve.drain())

    def quant_compute(pp, i, idx, first, ncols, store_fn, sl):
        b = sl % 4
        h2 = sl % 2
        wb_ = wqb[:, b * WCW:b * WCW + ncols]
        wpb_ = wpb[:, h2 * WCW:h2 * WCW + ncols]
        wmb_ = wmb[:, h2 * WCW:h2 * WCW + ncols]
        wib_ = wib[:, h2 * WCW:h2 * WCW + ncols]
        if not hasattr(pp, "sg_tok"):
            pp.sg_tok = [0, 0]
            pp.st_tok = [0, 0]
        ac.wait_ge(sQW[b].h, pp.lds[i])
        if pp.sg_tok[h2]:
            ac.wait_ge(sG.h, pp.sg_tok[h2])
        sA5.inc(ac.activation(wpb_, wb_, ACTF.Sign,
                              bias=chalf[:, 0:1], scale=rgam[:, idx:idx + 1]))
        sA5.inc(ac.activation(wmb_, wb_, ACTF.Sign,
                              bias=chalf[:, 1:2], scale=rgam[:, idx:idx + 1]))
        wq_us[b] = sA5.n
        gp.wait_ge(sA5.h, sA5.n)
        if pp.st_tok[h2]:
            gp.wait_ge(sP6[h2].h, pp.st_tok[h2])
        pp.sg_tok[h2] = sG.inc(gp.tensor_tensor(wib_, wpb_, wmb_, op=ALU.add))
        gp.wait_ge(sG.h, pp.sg_tok[h2])
        store_fn(wib[:, h2 * WCW:(h2 + 1) * WCW], sP6[h2])
        pp.st_tok[h2] = sP6[h2].n

    def wquant_bcast(idx):
        n_elems = D * H if idx < 2 else H * D
        te.wait_ge(sV5.h, sV5.n)
        gps = palloc("gwps" + str(idx), [1, 1])
        mm_tok = sT.inc(te.matmul(gps[:1, :1], gwsum[:, idx:idx + 1], ones[:],
                                  start=True, stop=True))
        ve.wait_ge(sT.h, mm_tok)
        ve.tensor_scalar(gwtot_sb[:1, idx:idx + 1], gps[:1, :1], 1.0 / float(n_elems),
                         1e-5, op0=ALU.mult, op1=ALU.max)
        ve.drain()
        ve.reciprocal(gwrec_sb[:1, idx:idx + 1], gwtot_sb[:1, idx:idx + 1])
        sV5.inc(ve.drain())
        pfree("gwps" + str(idx))
        bc_psw = palloc("bc_psw" + str(idx), [128, 512])
        te.wait_ge(sV5.h, sV5.n)
        cell = [0]
        bcast(rgam[:, idx:idx + 1], gwrec_sb[:1, idx:idx + 1], 1, bc_psw,
              war_cell=cell)
        bcast(gamw[:, idx:idx + 1], gwtot_sb[:1, idx:idx + 1], 1, bc_psw,
              war_cell=cell)
        ac.drain()
        pfree("bc_psw" + str(idx))

    wpump = Pump([
        (gamma_compute, 0, wchunks(gwT_hbm, wgi_sp)),
        (wquant_bcast, 0, wquant_bcast),
        (gamma_compute, 1, wchunks(vwT_hbm, wvi_sp)),
        (wquant_bcast, 1, wquant_bcast),
        (quant_compute, 0, wchunks(gwT_hbm, wgi_sp)),
        (gamma_compute, 2, owt_chunks()),
        (wquant_bcast, 2, wquant_bcast),
        (quant_compute, 1, wchunks(vwT_hbm, wvi_sp)),
        (quant_compute, 2, owt_chunks()),
    ])

    gam_b = alloc("gam_b", [128, D])
    bet_b = alloc("bet_b", [128, D])
    sL.inc(sy.dma_start(out=eye32[:], in_=eye32_hbm), 16)
    sL.inc(sy.dma_start(out=eye16[:], in_=eye16_hbm), 16)
    sL.inc(sy.dma_start(out=mask_own[:], in_=mko_hbm), 16)
    sL.inc(sy.dma_start(out=mask_peer[:], in_=mkp_hbm), 16)
    sL.inc(sy.dma_start(out=gam_b[:1, :], in_=gamma_hbm[None, :]), 16)
    sL.inc(sy.dma_start(out=bet_b[:1, :], in_=beta_hbm[None, :]), 16)
    ve.memset(ones[:], 1.0)
    ve.memset(ones16[:], 1.0)
    ve.memset(ones_row[:], 1.0)
    ve.memset(chalf[:, 0:1], -0.5)
    ve.memset(chalf[:, 1:2], 0.5)
    sV.inc(ve.drain())
    bc_ps0 = palloc("bc_ps0", [128, 512])
    te.wait_ge(sL.h, sL.n)
    te.wait_ge(sV.h, sV.n)
    bcast(gam_b, gam_b[:1, :], D, bc_ps0)
    bcast(bet_b, bet_b[:1, :], D, bc_ps0)
    pfree("bc_ps0")

    hnf = alloc("hnf", [128, DC * S])
    xt = alloc("xt", [128, 2 * D])
    ptr = palloc("ptr", [128, 1024])
    ve.wait_ge(sA.h, sA.n)
    evac_tok = [0, 0]
    tile_done = [0, 0]
    for i in range(KT):
        b = i % 2
        xb = xt[:, b * D:b * D + D]
        sy.wait_ge(sT.h, tile_done[b])
        ld = sX[b].inc(sy.dma_start(out=xb, in_=x_hbm[i * 128:(i + 1) * 128, :]), 16)
        ve.wait_ge(sX[b].h, ld)
        ln_stats(xb, D)
        ln_rstd()
        ln_apply(xb, xb)
        if i < QT:
            sy.wait_ge(sA.h, sA.n)
            sS.inc(sy.dma_start(out=h_sp[i * 128:(i + 1) * 128, :], in_=xb), 16)
        ve.wait_ge(sA.h, sA.n)
        ln_stats(xb, D)
        ln_rstd()
        ac.wait_ge(sS.h, sS.n)
        ln_apply(xb, xb)
        ve.wait_ge(sA.h, sA.n)
        ve.tensor_tensor(xb, xb, gam_b[:], op=ALU.mult)
        ve.drain()
        ve.tensor_tensor(xb, xb, bet_b[:], op=ALU.add)
        sV.inc(ve.drain())
        te.wait_ge(sV.h, sV.n)
        for dc in range(DC):
            pb = ptr[:, (dc % 2) * 512:(dc % 2) * 512 + 128]
            te.wait_ge(sA.h, evac_tok[dc % 2])
            sT.inc(te.transpose(pb, xb[:, dc * 128:(dc + 1) * 128], eye32[:]))
            ac.wait_ge(sT.h, sT.n)
            dst = hnf[:, dc * S + i * 128: dc * S + i * 128 + 128]
            evac_tok[dc % 2] = sA.inc(ac.copy(dst, pb))
            gp.wait_ge(sA.h, sA.n)
            sKV.inc(gp.dma_start(out=hn_sp[i][:, dc * 128:(dc + 1) * 128],
                                 in_=dst), 16)
        tile_done[b] = sT.n
    free("xt")
    pfree("ptr")
    barrier()

    wb = alloc("wb", [128, 2 * DC * 128])
    ksb = alloc("ksb", [128, S])
    ksq = alloc("ksq", [128, SLAB])
    nsb = alloc("nsb", [1, S], BF16)
    kp = palloc("kp", [128, S])
    npp = palloc("npsum", [1, 2 * SLAB])

    def proj(base_col, ntok, store, pump_hook=lambda: None):
        NG = ntok // SLAB
        wtok = [0, 0]
        copy_tok = [0]
        for o in range(DC):
            ob = base_col // 128 + o
            bb = o % 2
            wbo = wb[:, bb * DC * 128:(bb + 1) * DC * 128]
            sy.wait_ge(sT.h, wtok[bb])
            ld = sW[bb].inc(sy.dma_start(
                out=wbo, in_=qkvc_hbm[:, ob * DC * 128:(ob + 1) * DC * 128]), 16)
            te.wait_ge(sW[bb].h, ld)
            te.wait_ge(sV.h, copy_tok[0])
            for dc in range(DC):
                wbb = wbo[:, dc * 128:(dc + 1) * 128]
                for g in range(NG):
                    hsrc = hnf[:, dc * S + g * SLAB: dc * S + (g + 1) * SLAB]
                    sT.inc(te.matmul(kp[:, g * SLAB:(g + 1) * SLAB], wbb, hsrc,
                                     start=(dc == 0), stop=(dc == DC - 1)))
            wtok[bb] = sT.n
            ve.wait_ge(sT.h, sT.n)
            ve.wait_ge(sS.h, sS.n)
            ve.wait_ge(sKV.h, sKV.n)
            ve.tensor_copy(ksb[:, :ntok], kp[:, :ntok])
            copy_tok[0] = sV.inc(ve.drain())
            sy.wait_ge(sV.h, sV.n)
            gp.wait_ge(sV.h, sV.n)
            store(o)
            if o == 0:
                ve.memset(nsb[:1, :ntok], 0.0)
                sV.inc(ve.drain())
            for g in range(NG):
                gb_ = g % 2
                ve.wait_ge(sT.h, sT.n)
                ve.tensor_tensor(ksq[:], ksb[:, g * SLAB:(g + 1) * SLAB],
                                 ksb[:, g * SLAB:(g + 1) * SLAB], op=ALU.mult)
                sV.inc(ve.drain())
                te.wait_ge(sV.h, sV.n)
                nmm = sT.inc(te.matmul(npp[:1, gb_ * SLAB:(gb_ + 1) * SLAB],
                                       ones[:], ksq[:], start=True, stop=True))
                ve.wait_ge(sT.h, nmm)
                ve.tensor_tensor(nsb[:1, g * SLAB:(g + 1) * SLAB],
                                 nsb[:1, g * SLAB:(g + 1) * SLAB],
                                 npp[:1, gb_ * SLAB:(gb_ + 1) * SLAB], op=ALU.add)
                sV.inc(ve.drain())
            pump_hook()

    def store_k(o):
        sKV.inc(gp.dma_start(out=k_sp[:, :, o * 128:(o + 1) * 128],
                             in_=ksb[:, :S]), 16)

    proj(D, S, store_k, lambda: wpump.pump(25))
    ve.wait_ge(sT.h, sT.n)
    ve.tensor_reduce(maxkn[:], nsb[:1, :S], axis=AX.X, op=ALU.max)
    ve.drain()
    ve.tensor_scalar(maxkn[:], maxkn[:], 1e-30, None, op0=ALU.max)
    sV.inc(ve.drain())

    def store_q(o):
        sS.inc(sy.dma_start(out=q_sp[o], in_=ksb[:, :SO]), 16)

    proj(0, SO, store_q, lambda: wpump.pump(25))
    ve.wait_ge(sT.h, sT.n)
    ve.tensor_copy(qn_row[:], nsb[:1, :SO])
    ve.drain()
    ve.tensor_scalar(qn_row[:], qn_row[:], 1e-30, None, op0=ALU.max)
    sV.inc(ve.drain())
    ac.wait_ge(sV.h, sV.n)
    sA.inc(ac.activation(qn_row[:], qn_row[:], ACTF.Sqrt))
    sA.inc(ac.activation(maxkn[:], maxkn[:], ACTF.Sqrt))
    ve.wait_ge(sA.h, sA.n)
    ve.tensor_scalar(negMr[:], qn_row[:], maxkn[:1, :1], -1.0, op0=ALU.mult, op1=ALU.mult)
    sV.inc(ve.drain())
    pfree("npsum", "kp")
    free("nsb")
    bc_ps1 = palloc("bc_ps1", [128, 512])
    te.wait_ge(sV.h, sV.n)
    bcast(negM, negMr, SO, bc_ps1)
    pfree("bc_ps1")
    free("ksq", "ksb", "wb")
    free("hnf", "bet_b", "gam_b")
    free("negMr", "maxkn", "qn_row")
    b1_t, b1_v, b1_a = sT.n, sV.n, sA.n

    wv_sb = alloc("wv_sb", [128, DC * D])
    hbuf = alloc("hbuf", [128, 2 * DC * 128])
    vsb = alloc("vsb", [128, D])
    vp = palloc("vp", [128, 1024])
    sy.wait_ge(sT.h, b1_t)
    sy.wait_ge(sV.h, b1_v)
    for dc in range(DC):
        sL.inc(sy.dma_start(out=wv_sb[:, dc * D:(dc + 1) * D],
                            in_=qkv_hbm[dc * 128:(dc + 1) * 128, 2 * D:3 * D]), 16)
    te.wait_ge(sL.h, sL.n)
    te.wait_ge(sA.h, b1_a)
    te.wait_ge(sV.h, b1_v)
    huse = [0, 0]
    vevac = [0, 0]
    for m in range(KT):
        b = m % 2
        hb = hbuf[:, b * DC * 128:(b + 1) * DC * 128]
        sy.wait_ge(sT.h, huse[b])
        sH[b].inc(sy.dma_start(out=hb, in_=hn_sp[m]), 16)
        te.wait_ge(sH[b].h, sH[b].n)
        if m > 0:
            te.wait_ge(sKV.h, sKV.n)
        for ds in range(DSL):
            pbk = vp[:, (ds % 2) * 512:(ds % 2) * 512 + SLAB]
            te.wait_ge(sV.h, vevac[ds % 2])
            for dc in range(DC):
                sT.inc(te.matmul(pbk, hb[:, dc * 128:(dc + 1) * 128],
                                 wv_sb[:, dc * D + ds * SLAB: dc * D + (ds + 1) * SLAB],
                                 start=(dc == 0), stop=(dc == DC - 1)))
            ve.wait_ge(sT.h, sT.n)
            ve.tensor_copy(vsb[:, ds * SLAB:(ds + 1) * SLAB], pbk)
            vevac[ds % 2] = sV.inc(ve.drain())
        huse[b] = sT.n
        gp.wait_ge(sV.h, sV.n)
        sKV.inc(gp.dma_start(out=v_sp[:, :, m * 128:(m + 1) * 128], in_=vsb[:]), 16)
        wpump.pump(24)
    free("vsb", "hbuf", "wv_sb")
    pfree("vp")
    b2_t, b2_v = sT.n, sV.n

    ofm = alloc("ofm", [128, DC * SO])
    _qa_list = [SO - 128 * (m % QT) for m in range(KT)]
    ptp = alloc("ptp", [128, int(sum(_qa_list))])
    qres = alloc("qres", [128, DC * SO])
    kbuf = alloc("kbuf", [128, 2 * DC * 128])
    qa_list = _qa_list
    PTW = int(sum(qa_list))
    poff = np.cumsum([0] + qa_list)[:-1].tolist()
    sp_ps = palloc("sp_ps", [128, 2 * SO])
    l_ps = palloc("l_ps", [128, QT])
    sy.wait_ge(sT.h, b2_t)
    sy.wait_ge(sV.h, b2_v)
    sy.wait_ge(sS.h, sS.n)
    sy.wait_ge(sKV.h, sKV.n)
    for o in range(DC):
        sL.inc(sy.dma_start(out=qres[:, o * SO:(o + 1) * SO], in_=q_sp[o]), 16)
    te.wait_ge(sL.h, sL.n)
    te.wait_ge(sV.h, b2_v)
    kuse = [0, 0]
    pset_tok = [0, 0]
    ldk_tok = [0, 0]
    exp_tok = [0]
    first_l = True
    ldk_tok[0] = sX[0].inc(sy.dma_start(out=kbuf[:, :DC * 128], in_=k_sp[:, 0, :]), 16)
    for m in range(KT):
        b = m % 2
        kb = kbuf[:, b * DC * 128:(b + 1) * DC * 128]
        if m + 1 < KT:
            b2 = (m + 1) % 2
            sy.wait_ge(sT.h, kuse[b2])
            ldk_tok[b2] = sX[b2].inc(sy.dma_start(
                out=kbuf[:, b2 * DC * 128:(b2 + 1) * DC * 128],
                in_=k_sp[:, m + 1, :]), 16)
        qoff = 128 * (m % QT)
        qa = SO - qoff
        te.wait_ge(sX[b].h, ldk_tok[b])
        te.wait_ge(sV.h, pset_tok[b])
        base = b * SO
        nseg = (qa + SLAB - 1) // SLAB
        for seg in range(nseg):
            lo = seg * SLAB
            hi = min(qa, lo + SLAB)
            for dc in range(DC):
                sT.inc(te.matmul(
                    sp_ps[:, base + lo:base + hi],
                    kb[:, dc * 128:(dc + 1) * 128],
                    qres[:, dc * SO + qoff + lo: dc * SO + qoff + hi],
                    start=(dc == 0), stop=(dc == DC - 1)))
        kuse[b] = sT.n
        wpump.pump(5)
        pm = ptp[:, poff[m]:poff[m] + qa]
        ve.wait_ge(sT.h, sT.n)
        ve.tensor_tensor(pm, sp_ps[:, base:base + qa], negM[:, qoff:SO], op=ALU.add)
        sV.inc(ve.drain())
        pset_tok[b] = sV.n
        mk = mask_own if m < QT else mask_peer
        ve.tensor_tensor(ptp[:, poff[m]:poff[m] + 128],
                         ptp[:, poff[m]:poff[m] + 128], mk[:], op=ALU.add)
        sV.inc(ve.drain())
        ac.wait_ge(sV.h, sV.n)
        exp_tok[0] = sA.inc(ac.activation(pm, pm, ACTF.Exp, scale=c.scale))
        te.wait_ge(sA.h, sA.n)
        for jj in range(qa // 128):
            col = qoff // 128 + jj
            sT.inc(te.matmul(l_ps[:, col:col + 1],
                             ptp[:, poff[m] + jj * 128: poff[m] + (jj + 1) * 128],
                             ones[:], start=first_l, stop=(m == KT - 1),
                             skip_group_check=True))
            first_l = False
        wpump.pump(5)
    ve.wait_ge(sT.h, sT.n)
    ve.reciprocal(recip_l[:], l_ps[:])
    sV.inc(ve.drain())
    if c.debug:
        sy.wait_ge(sV.h, sV.n)
        sy.wait_ge(sA.h, sA.n)
        sS.inc(sy.dma_start(out=dbg_ptp[:], in_=ptp[:]), 16)
        sS.inc(sy.dma_start(out=dbg_l[:], in_=recip_l[:]), 16)
        sS.inc(sy.dma_start(out=dbg_negM[:], in_=negM[:]), 16)
    free("kbuf", "qres")
    pfree("l_ps", "sp_ps")
    b3_t, b3_v, b3_a = sT.n, sV.n, exp_tok[0]

    vbuf = alloc("vbuf", [128, 2 * KT * 128])
    o_ps = palloc("o_ps", [128, 2 * SO])
    vuse = [0, 0]
    oset_tok = [0, 0]
    sy.wait_ge(sT.h, b3_t)
    te.wait_ge(sV.h, b3_v)
    te.wait_ge(sA.h, b3_a)
    for cc in range(DC):
        b = cc % 2
        vb = vbuf[:, b * KT * 128:(b + 1) * KT * 128]
        sy.wait_ge(sT.h, vuse[b])
        ld = sU[b].inc(sy.dma_start(out=vb, in_=v_sp[:, cc, :]), 16)
        te.wait_ge(sU[b].h, ld)
        te.wait_ge(sV.h, oset_tok[b])
        base = b * SO
        for m in range(KT):
            qoff = 128 * (m % QT)
            for seg in range(OSL):
                lo = max(seg * SLAB, qoff)
                hi = (seg + 1) * SLAB
                if lo >= hi:
                    continue
                sT.inc(te.matmul(
                    o_ps[:, base + lo:base + hi],
                    vb[:, m * 128:(m + 1) * 128],
                    ptp[:, poff[m] + lo - qoff: poff[m] + hi - qoff],
                    start=(m == 0), stop=(m == KT - 1), skip_group_check=True))
        vuse[b] = sT.n
        ve.wait_ge(sT.h, sT.n)
        ve.tensor_copy(ofm[:, cc * SO:(cc + 1) * SO], o_ps[:, base:base + SO])
        oset_tok[b] = sV.inc(ve.drain())
        wpump.pump(20)
    if c.debug:
        sy.wait_ge(sA.h, sA.n)
        sS.inc(sy.dma_start(out=dbg_ofm[:], in_=ofm[:]), 16)
    free("vbuf")
    free("ptp")
    pfree("o_ps")
    b4_t, b4_v = sT.n, sV.n

    ao = alloc("ao", [128, QT * D])
    wos = alloc("wos", [128, DC * SLAB])
    ao_ps = palloc("ao_ps", [128, 1024])
    aevac = [0, 0]
    sy.wait_ge(sT.h, b4_t)
    te.wait_ge(sV.h, b4_v)
    for ds in range(DSL):
        te.wait_ge(sT.h, sT.n)
        sy.wait_ge(sT.h, sT.n)
        sWS.inc(sy.dma_start(
            out=wos[:], in_=wob_hbm[:, ds * DC * SLAB:(ds + 1) * DC * SLAB]), 16)
        te.wait_ge(sWS.h, sWS.n)
        for qc in range(QT):
            bk = qc % 2
            pbk = ao_ps[:, bk * 512:bk * 512 + SLAB]
            te.wait_ge(sV.h, aevac[bk])
            for dc in range(DC):
                sT.inc(te.matmul(pbk, ofm[:, dc * SO + qc * 128: dc * SO + (qc + 1) * 128],
                                 wos[:, dc * SLAB:(dc + 1) * SLAB],
                                 start=(dc == 0), stop=(dc == DC - 1)))
            ve.wait_ge(sT.h, sT.n)
            ve.tensor_scalar(ao[:, qc * D + ds * SLAB: qc * D + (ds + 1) * SLAB],
                             pbk, recip_l[:, qc:qc + 1], None, op0=ALU.mult)
            aevac[bk] = sV.inc(ve.drain())
    if c.debug:
        sy.wait_ge(sV.h, sV.n)
        sS.inc(sy.dma_start(out=dbg_ao[:], in_=ao[:]), 16)
        barrier()
    free("wos")
    pfree("ao_ps")
    hbt = alloc("hbt", [128, D])
    xbt = alloc("xbt", [128, D])
    z2t = alloc("z2t", [128, D])
    for qc in range(QT):
        sy.wait_ge(sV.h, sV.n)
        sL.inc(sy.dma_start(out=hbt[:], in_=h_sp[qc * 128:(qc + 1) * 128, :]), 16)
        ld = sL.inc(sy.dma_start(out=xbt[:], in_=x_hbm[qc * 128:(qc + 1) * 128, :]), 16)
        at = ao[:, qc * D:(qc + 1) * D]
        ve.wait_ge(sL.h, ld)
        ve.tensor_tensor(at, at, hbt[:], op=ALU.add)
        ve.drain()
        ln_stats(at, D)
        ln_rstd()
        ac.wait_ge(sS.h, sS.n)
        ln_apply(z2t[:], at)
        ve.wait_ge(sA.h, sA.n)
        ve.tensor_tensor(z2t[:], z2t[:], xbt[:], op=ALU.add)
        sV.inc(ve.drain())
        sy.wait_ge(sV.h, sV.n)
        sS.inc(sy.dma_start(out=x1_sp[qc * 128:(qc + 1) * 128, :], in_=z2t[:]), 16)
    wpump.drain()
    free("z2t", "xbt", "hbt", "ao", "ofm")
    free("wib", "wmb", "wpb", "wqb")
    barrier()

    free("negM")
    NSLOT = 3
    NH4 = c.NH4
    xiT = alloc("xiT", [128, DC * SO], BF16)
    ft = alloc("ft", [128, D])
    rs1 = alloc("rs1", [128, 1024])
    xit = alloc("xit", [128, D], BF16)
    wgsl = alloc("wgsl", [128, 2 * DC * SLAB], FP8)
    wvsl = alloc("wvsl", [128, 2 * DC * SLAB], FP8)
    gbb = alloc("gbb", [128, SLAB])
    vbb = alloc("vbb", [128, SLAB])
    gsl = alloc("gsl", [128, 2 * SLAB])
    vsl = alloc("vsl", [128, SLAB])
    sgn = alloc("sgn", [128, 4 * c.W6])
    ubuf = alloc("ubuf", [128, 4 * c.W6])
    scr16 = alloc("scr16", [128, 2 * c.W6], BF16)
    cm1275 = alloc("cm1275", [128, 1])
    c1285 = alloc("c1285", [128, 1])
    dsc = alloc("dsc", [128, 128])
    gvb = alloc("gvb", [128, NSLOT * H])
    pt16 = palloc("pt16", [128, 2048], BF16)
    gps2 = palloc("gps2", [128, 1024])
    vps2 = palloc("vps2", [128, 1024])
    ptq = palloc("ptq", [128, 1024])
    ve.memset(cm1275[:], -127.5)
    ve.memset(c1285[:], 128.5)
    ve.memset(ggvcol[:], 0.0)
    sV.inc(ve.drain())
    ac.wait_ge(sV.h, sV.n)

    slot_free = [None] * NSLOT
    wu = [0, 0]
    bias_rd = [0]
    silu_tok = [0, 0]
    pfree_tok = [0, 0]
    dsc_war = [0]
    evt6 = [0, 0]
    sgnwar = [0, 0, 0, 0]
    uwar = [0, 0, 0, 0]
    scr_tok = [0, 0]
    evt = [0, 0]
    xa_tok = [0] * QT
    ft_free = [0]

    def p6aA(qc):
        sy.wait_ge(sV.h, ft_free[0])
        ld = sX[0].inc(sy.dma_start(out=ft[:], in_=x1_sp[qc * 128:(qc + 1) * 128, :]), 16)
        ve.wait_ge(sX[0].h, ld)
        ln_stats(ft[:], D)
        ln_rstd()
        ln_apply(ft[:], ft[:])
        ve.wait_ge(sA.h, sA.n)
        ve.tensor_reduce(gxcol[:, qc:qc + 1], ft[:], axis=AX.X, op=ALU.max,
                         apply_absolute_value=True)
        ve.drain()
        ve.tensor_scalar(gxcol[:, qc:qc + 1], gxcol[:, qc:qc + 1], 1e-5, None, op0=ALU.max)
        ve.drain()
        ve.reciprocal(tt1[:], gxcol[:, qc:qc + 1])
        ve.drain()
        ve.tensor_scalar(sc_x[:, qc:qc + 1], tt1[:], 127.0, None, op0=ALU.mult)
        ve.drain()
        ve.tensor_scalar(tt0[:], gxcol[:, qc:qc + 1], 0.5 / 127.0, None, op0=ALU.mult)
        ve.drain()
        ve.tensor_tensor(scg_t[:, qc:qc + 1], tt0[:], gamw[:, 0:1], op=ALU.mult)
        ve.tensor_tensor(scv_t[:, qc:qc + 1], tt0[:], gamw[:, 1:2], op=ALU.mult)
        sV.inc(ve.drain())

    def p6aB(qc):
        HD = min(512, D // 2)
        wtok = sV.inc(ve.tensor_scalar(ft[:], ft[:], sc_x[:, qc:qc + 1], 128.5,
                                       op0=ALU.mult, op1=ALU.add))
        fin_tok = wtok
        for ps_ in range(D // (2 * HD)):
            ac.wait_ge(sV.h, fin_tok)
            ac.drain()
            up = {}
            for ci in range(2):
                sl_ = slice(ps_ * 2 * HD + ci * HD, ps_ * 2 * HD + (ci + 1) * HD)
                sA.inc(ac.activation(rs1[:, ci * HD:(ci + 1) * HD], ft[:, sl_],
                                     ACTF.Identity, bias=cm1275[:]))
                up[ci] = 0
            ac.drain()
            for j in range(8):
                kpow = -float(1 << (6 - j)) if j <= 6 else -0.5
                for ci in range(2):
                    sl_ = slice(ps_ * 2 * HD + ci * HD, ps_ * 2 * HD + (ci + 1) * HD)
                    uch = rs1[:, ci * HD:(ci + 1) * HD]
                    sch = xit[:, sl_]
                    if up[ci]:
                        ac.wait_ge(sV.h, up[ci])
                    sA.inc(ac.activation(sch, uch, ACTF.Sign, bias=chalf[:, 0:1]))
                    ve.wait_ge(sA.h, sA.n)
                    up[ci] = sV.inc(ve.scalar_tensor_tensor(
                        uch, sch, kpow, uch, op0=ALU.mult, op1=ALU.add))
            for ci in range(2):
                sl_ = slice(ps_ * 2 * HD + ci * HD, ps_ * 2 * HD + (ci + 1) * HD)
                ve.wait_ge(sV.h, up[ci])
                ve.scalar_tensor_tensor(xit[:, sl_], rs1[:, ci * HD:(ci + 1) * HD],
                                        -1.0, ft[:, sl_], op0=ALU.mult, op1=ALU.add)
                ve.drain()
                ve.tensor_scalar(xit[:, sl_], xit[:, sl_], -128.0, None, op0=ALU.add)
                fin_tok = sV.inc(ve.drain())
        ft_free[0] = fin_tok
        te.wait_ge(sV.h, ft_free[0])
        for dc in range(DC):
            pb = pt16[:, (dc % 2) * 1024:(dc % 2) * 1024 + 128]
            te.wait_ge(sA.h, evt[dc % 2])
            sT.inc(te.transpose(pb, xit[:, dc * 128:(dc + 1) * 128], eye16[:]))
            ac.wait_ge(sT.h, sT.n)
            evt[dc % 2] = sA.inc(ac.copy(
                xiT[:, dc * SO + qc * 128: dc * SO + (qc + 1) * 128], pb))
        xa_tok[qc] = sA.n

    def produce_gen(qg):
        for s in range(HSL):
            b = s % 2
            wgb = wgsl[:, b * DC * SLAB:(b + 1) * DC * SLAB]
            wvb2 = wvsl[:, b * DC * SLAB:(b + 1) * DC * SLAB]
            if s == 0:
                te.wait_ge(sA.h, max(xa_tok[qg * c.QG], xa_tok[qg * c.QG + 1]))
            sy.wait_ge(sT.h, wu[b])
            sW[b].inc(sy.dma_start(out=wgb[:], in_=wgi_sp[s]), 16)
            sW[b].inc(sy.dma_start(out=wvb2[:], in_=wvi_sp[s]), 16)
            ldw = sW[b].n
            sy.wait_ge(sV.h, bias_rd[0])
            sH[0].inc(sy.dma_start(out=gbb[:],
                                   in_=gbbt_hbm[:, s * SLAB:(s + 1) * SLAB]), 16)
            ld2 = sH[0].inc(sy.dma_start(out=vbb[:],
                                         in_=vbbt_hbm[:, s * SLAB:(s + 1) * SLAB]), 16)
            te.wait_ge(sW[b].h, ldw)
            for j in range(c.QG):
                qc = qg * c.QG + j
                slot = qc % NSLOT
                gvs = gvb[:, slot * H + s * SLAB: slot * H + (s + 1) * SLAB]
                gps_j = gps2[:, j * SLAB:(j + 1) * SLAB]
                vps_j = vps2[:, j * SLAB:(j + 1) * SLAB]
                gsl_j = gsl[:, j * SLAB:(j + 1) * SLAB]
                te.wait_ge(sV.h, pfree_tok[j])
                for dc in range(DC):
                    sT.inc(te.matmul(gps_j,
                                     xiT[:, dc * SO + qc * 128: dc * SO + (qc + 1) * 128],
                                     wgb[:, dc * SLAB:(dc + 1) * SLAB],
                                     start=(dc == 0), stop=(dc == DC - 1)))
                tok_g = sT.n
                for dc in range(DC):
                    sT.inc(te.matmul(vps_j,
                                     xiT[:, dc * SO + qc * 128: dc * SO + (qc + 1) * 128],
                                     wvb2[:, dc * SLAB:(dc + 1) * SLAB],
                                     start=(dc == 0), stop=(dc == DC - 1)))
                tok_v = sT.n
                ve.wait_ge(sT.h, tok_g)
                ve.wait_ge(sH[0].h, ld2)
                ve.wait_ge(sA.h, silu_tok[j])
                ve.scalar_tensor_tensor(gsl_j, gps_j, scg_t[:, qc:qc + 1],
                                        gbb[:], op0=ALU.mult, op1=ALU.add)
                sV.inc(ve.drain())
                ac.wait_ge(sV.h, sV.n)
                silu_tok[j] = sA.inc(ac.activation(gsl_j, gsl_j, ACTF.Silu))
                ve.wait_ge(sT.h, tok_v)
                ve.scalar_tensor_tensor(vsl[:], vps_j, scv_t[:, qc:qc + 1],
                                        vbb[:], op0=ALU.mult, op1=ALU.add)
                pfree_tok[j] = sV.inc(ve.drain())
                ve.wait_ge(sA.h, silu_tok[j])
                if slot_free[slot] is not None:
                    si, stok = slot_free[slot][s]
                    ve.wait_ge(sP6[si].h, stok)
                ve.tensor_tensor(gvs, gsl_j, vsl[:], op=ALU.mult)
                ve.drain()
                ve.tensor_reduce(tt0[:], gvs, axis=AX.X, op=ALU.max,
                                 apply_absolute_value=True)
                ve.drain()
                ve.tensor_tensor(ggvcol[:, qc:qc + 1], ggvcol[:, qc:qc + 1], tt0[:],
                                 op=ALU.max)
                sV.inc(ve.drain())
            bias_rd[0] = sV.n
            wu[b] = sT.n
            yield

    def qA(qc):
        slot = qc % NSLOT
        base = slot * H
        ve.tensor_scalar(ggvcol[:, qc:qc + 1], ggvcol[:, qc:qc + 1], 1e-5, None,
                         op0=ALU.max)
        ve.drain()
        ve.reciprocal(tt1[:], ggvcol[:, qc:qc + 1])
        ve.drain()
        ve.tensor_scalar(sc_gv[:, qc:qc + 1], tt1[:], 127.0, None, op0=ALU.mult)
        ve.drain()
        ve.tensor_scalar(tt0[:], ggvcol[:, qc:qc + 1], 0.5 / 127.0, None, op0=ALU.mult)
        ve.drain()
        ve.tensor_tensor(sc3[:, qc:qc + 1], tt0[:], gamw[:, 2:3], op=ALU.mult)
        ve.drain()
        ve.wait_ge(sT.h, dsc_war[0])
        ve.tensor_scalar(dsc[:], eye32[:], sc_gv[:, qc:qc + 1], None, op0=ALU.mult)
        dsc_tok = sV.inc(ve.drain())
        te.wait_ge(sV.h, dsc_tok)
        for ch in range(HC):
            p = ch % 2
            pb = ptq[:, p * 512:p * 512 + 128]
            blk = gvb[:, base + ch * 128: base + (ch + 1) * 128]
            te.wait_ge(sA.h, evt6[p])
            sT.inc(te.matmul(pb, blk, dsc[:], start=True, stop=True))
            ac.wait_ge(sT.h, sT.n)
            evt6[p] = sA.inc(ac.activation(blk, pb, ACTF.Identity, bias=c1285[:]))
        dsc_war[0] = sT.n
        slot_free[slot] = {}

    def pair_gen(qc):
        slot = qc % NSLOT
        base = slot * H
        W6 = c.W6
        for k0 in range(0, c.NHW, 4):
            pair = (k0, k0 + 1, k0 + 2, k0 + 3)
            upd = {}
            ac.drain()
            for ki in pair:
                i = ki % 4
                wch = gvb[:, base + ki * W6: base + (ki + 1) * W6]
                uch = ubuf[:, i * W6:(i + 1) * W6]
                if uwar[i]:
                    ac.wait_ge(sV.h, uwar[i])
                sA.inc(ac.activation(uch, wch, ACTF.Identity, bias=cm1275[:]))
                upd[ki] = sgnwar[i]
            ac.drain()
            for j in range(8):
                kpow = -float(1 << (6 - j)) if j <= 6 else -0.5
                for ki in pair:
                    i = ki % 4
                    uch = ubuf[:, i * W6:(i + 1) * W6]
                    sch = sgn[:, i * W6:(i + 1) * W6]
                    if upd[ki]:
                        ac.wait_ge(sV.h, upd[ki])
                    sA.inc(ac.activation(sch, uch, ACTF.Sign, bias=chalf[:, 0:1]))
                    ve.wait_ge(sA.h, sA.n)
                    upd[ki] = sV.inc(ve.scalar_tensor_tensor(
                        uch, sch, kpow, uch, op0=ALU.mult, op1=ALU.add))
            for ki in pair:
                i = ki % 4
                p2 = ki % 2
                sgnwar[i] = upd[ki]
                wch = gvb[:, base + ki * W6: base + (ki + 1) * W6]
                ve.wait_ge(sV.h, upd[ki])
                uwar[i] = sV.inc(ve.scalar_tensor_tensor(
                    wch, ubuf[:, i * W6:(i + 1) * W6], -1.0, wch,
                    op0=ALU.mult, op1=ALU.add))
                ac.wait_ge(sV.h, uwar[i])
                if scr_tok[p2]:
                    ac.wait_ge(sP6[p2].h, scr_tok[p2])
                ct = sA.inc(ac.copy(scr16[:, p2 * W6:(p2 + 1) * W6], wch))
                gp.wait_ge(sA.h, ct)
                scr_tok[p2] = sP6[p2].inc(gp.dma_start(
                    out=x2i_sp[qc][:, ki * W6:(ki + 1) * W6],
                    in_=scr16[:, p2 * W6:(p2 + 1) * W6]), 16)
                slot_free[slot][ki] = (p2, scr_tok[p2])
            yield

    assert c.QG == 2 and HSL % 4 == 0 and c.NHW == HSL
    p6a_pos = [3 * HSL // 16, 5 * HSL // 16, 11 * HSL // 16, 13 * HSL // 16]
    p6aA(0)
    p6aB(0)
    p6aA(1)
    p6aB(1)
    for qg in range(c.NQG):
        pairs = []
        if qg > 0:
            for q in (qg * 2 - 2, qg * 2 - 1):
                qA(q)
                pairs.append(pair_gen(q))
        nxt = [qg * 2 + 2, qg * 2 + 3]
        pg = produce_gen(qg)
        for s in range(HSL):
            if pairs:
                if s % 4 == 0:
                    next(pairs[0], None)
                elif s % 4 == 2:
                    next(pairs[1], None)
            if s == p6a_pos[0] and nxt[0] < QT:
                p6aA(nxt[0])
            if s == p6a_pos[1] and nxt[0] < QT:
                p6aB(nxt[0])
            if s == p6a_pos[2] and nxt[1] < QT:
                p6aA(nxt[1])
            if s == p6a_pos[3] and nxt[1] < QT:
                p6aB(nxt[1])
            next(pg)
        for g_ in pairs:
            for _ in g_:
                pass
    for q in (QT - 2, QT - 1):
        qA(q)
    for _ in zip(pair_gen(QT - 2), pair_gen(QT - 1)):
        pass
    free("gvb", "dsc", "c1285", "cm1275", "scr16", "ubuf", "sgn", "vsl",
         "gsl", "vbb", "gbb", "wvsl", "wgsl",
         "xit", "rs1", "ft", "xiT")
    pfree("ptq", "vps2", "gps2", "pt16")
    barrier()

    wo3 = alloc("wo3", [128, HC * D], FP8)
    xb2 = alloc("xb2", [128, 2 * HC * 128], BF16)
    o3sb = alloc("o3sb", [128, D])
    x1b = alloc("x1b", [128, D])
    z4t = alloc("z4t", [128, D])
    obb = alloc("obb", [128, D])
    csb = alloc("csb", [128, D])
    sL.inc(sy.dma_start(out=obb[:], in_=obbt_hbm[:]), 16)
    NW7 = max(1, HC // 16)
    for k7 in range(NW7):
        w7 = HC * D // NW7
        sL.inc(sy.dma_start(out=wo3[:, k7 * w7:(k7 + 1) * w7],
                            in_=woi_sp[:, k7 * w7:(k7 + 1) * w7]), 16)
    te.wait_ge(sL.h, sL.n)
    cs_ps = palloc("cs_ps", [1, D])
    for hc in range(HC):
        for dsl_ in range(DSL):
            sT.inc(te.matmul(cs_ps[:1, dsl_ * SLAB:(dsl_ + 1) * SLAB], ones16[:],
                             wo3[:, hc * D + dsl_ * SLAB: hc * D + (dsl_ + 1) * SLAB],
                             start=(hc == 0), stop=(hc == HC - 1)))
    ve.wait_ge(sT.h, sT.n)
    ve.tensor_scalar(csb[:1, :], cs_ps[:1, :], 128.0, None, op0=ALU.mult)
    sV.inc(ve.drain())
    pfree("cs_ps")
    bc_ps7 = palloc("bc_ps7", [128, 512])
    te.wait_ge(sV.h, sV.n)
    bcast(csb, csb[:1, :], D, bc_ps7)
    pfree("bc_ps7")
    o3ps = palloc("o3ps", [128, 2048])
    te.wait_ge(sA.h, sA.n)
    xu = [0, 0]
    ev7 = [0, 0]
    ld7 = [0, 0]
    ld7[0] = sX[0].inc(sy.dma_start(out=xb2[:, :HC * 128], in_=x2i_sp[0]), 16)
    for qc in range(QT):
        b7 = qc % 2
        xbb = xb2[:, b7 * HC * 128:(b7 + 1) * HC * 128]
        sy.wait_ge(sV.h, sV.n)
        lx7 = sK.inc(sy.dma_start(out=x1b[:], in_=x1_sp[qc * 128:(qc + 1) * 128, :]), 16)
        if qc + 1 < QT:
            b2 = (qc + 1) % 2
            sy.wait_ge(sT.h, xu[b2])
            ld7[b2] = sX[b2].inc(sy.dma_start(
                out=xb2[:, b2 * HC * 128:(b2 + 1) * HC * 128], in_=x2i_sp[qc + 1]), 16)
        te.wait_ge(sX[b7].h, ld7[b7])
        for dg in range((DSL + 1) // 2):
            dss = range(2 * dg, min(2 * dg + 2, DSL))
            te.wait_ge(sV.h, ev7[dg % 2])
            for hc in range(HC):
                for ds in dss:
                    sT.inc(te.matmul(o3ps[:, ds * SLAB:(ds + 1) * SLAB],
                                     xbb[:, hc * 128:(hc + 1) * 128],
                                     wo3[:, hc * D + ds * SLAB: hc * D + (ds + 1) * SLAB],
                                     start=(hc == 0), stop=(hc == HC - 1)))
            xu[b7] = sT.n
            ve.wait_ge(sT.h, sT.n)
            ve.wait_ge(sA.h, sA.n)
            for ds in dss:
                ve.tensor_tensor(o3sb[:, ds * SLAB:(ds + 1) * SLAB],
                                 o3ps[:, ds * SLAB:(ds + 1) * SLAB],
                                 csb[:, ds * SLAB:(ds + 1) * SLAB], op=ALU.subtract)
                ve.drain()
                ve.scalar_tensor_tensor(o3sb[:, ds * SLAB:(ds + 1) * SLAB],
                                        o3sb[:, ds * SLAB:(ds + 1) * SLAB],
                                        sc3[:, qc:qc + 1], obb[:, ds * SLAB:(ds + 1) * SLAB],
                                        op0=ALU.mult, op1=ALU.add)
                ve.drain()
            ev7[dg % 2] = sV.inc(ve.drain())
        ln_stats(o3sb[:], D)
        ln_rstd()
        ac.wait_ge(sO.h, sO.n)
        ln_apply(z4t[:], o3sb[:])
        ve.wait_ge(sA.h, sA.n)
        ve.wait_ge(sK.h, lx7)
        ve.tensor_tensor(z4t[:], z4t[:], x1b[:], op=ALU.add)
        sV.inc(ve.drain())
        sy.wait_ge(sV.h, sV.n)
        sO.inc(sy.dma_start(out=out_hbm[qc * 128:(qc + 1) * 128, :], in_=z4t[:]), 16)
    sy.wait_ge(sO.h, sO.n)
    barrier()

    for cm in reversed(list(live.values())):
        cm.__exit__(None, None, None)
    for cm in reversed(list(psum_live.values())):
        cm.__exit__(None, None, None)
    for cm in reversed(sem_cms):
        cm.__exit__(None, None, None)
    return nc


def _bf16():
    import ml_dtypes
    return ml_dtypes.bfloat16


def reshape_blocked(w, out_blk=128):
    Dd, N = w.shape
    dcn = Dd // 128
    nb = N // out_blk
    return np.ascontiguousarray(
        w.reshape(dcn, 128, nb, out_blk).transpose(1, 2, 0, 3).reshape(128, -1))


def make_core_inputs(cfg, x_b, parity, shared):
    own = np.ascontiguousarray(x_b[parity::2])
    peer = np.ascontiguousarray(x_b[1 - parity::2])
    xr = np.concatenate([own, peer], 0)
    ii = np.arange(128)[:, None]
    jj = np.arange(128)[None, :]
    mask_own = np.where(ii <= jj, 0.0, NEG_MASK).astype(np.float32)
    if parity == 0:
        mask_peer = np.where(ii < jj, 0.0, NEG_MASK).astype(np.float32)
    else:
        mask_peer = np.where(ii <= jj, 0.0, NEG_MASK).astype(np.float32)
    d = {"x": xr, "mask_own": mask_own, "mask_peer": mask_peer}
    d.update(shared)
    return d


def kernel(L0_attn_x, L0_attn_qkv, L0_attn_o_proj, L0_attn_gamma, L0_attn_beta,
           L0_ffn_gate_w, L0_ffn_gate_b, L0_ffn_val_w, L0_ffn_val_b,
           L0_ffn_out_w, L0_ffn_out_b):
    cfg = Cfg(D=2048, S=2048)
    x = np.asarray(L0_attn_x, dtype=np.float32)
    qkv_f = np.asarray(L0_attn_qkv, dtype=np.float32)
    wo_f = np.asarray(L0_attn_o_proj, dtype=np.float32)
    shared = {
        "qkv": np.ascontiguousarray(qkv_f),
        "qkvc": reshape_blocked(qkv_f, 128),
        "wob": reshape_blocked(wo_f, 512),
        "gamma": np.asarray(L0_attn_gamma, dtype=np.float32),
        "beta": np.asarray(L0_attn_beta, dtype=np.float32),
        "gwT": np.ascontiguousarray(np.asarray(L0_ffn_gate_w, dtype=np.float32).T),
        "vwT": np.ascontiguousarray(np.asarray(L0_ffn_val_w, dtype=np.float32).T),
        "owT": np.ascontiguousarray(np.asarray(L0_ffn_out_w, dtype=np.float32).T),
        "gate_b": np.asarray(L0_ffn_gate_b, dtype=np.float32),
        "gbbt": np.ascontiguousarray(np.broadcast_to(
            np.asarray(L0_ffn_gate_b, dtype=np.float32), (128, len(L0_ffn_gate_b)))),
        "vbbt": np.ascontiguousarray(np.broadcast_to(
            np.asarray(L0_ffn_val_b, dtype=np.float32), (128, len(L0_ffn_val_b)))),
        "obbt": np.ascontiguousarray(np.broadcast_to(
            np.asarray(L0_ffn_out_b, dtype=np.float32), (128, len(L0_ffn_out_b)))),
        "val_b": np.asarray(L0_ffn_val_b, dtype=np.float32),
        "out_b": np.asarray(L0_ffn_out_b, dtype=np.float32),
        "eye32": np.eye(128, dtype=np.float32),
        "eye16": np.eye(128, dtype=np.float32).astype(_bf16()),
    }
    nc = build_program(cfg)
    in_maps = []
    for core in range(8):
        b, p = divmod(core, 2)
        in_maps.append(make_core_inputs(cfg, x[b], p, shared))
    res = run_bass_kernel_spmd(nc, in_maps, core_ids=list(range(8)))
    out = np.empty((4, cfg.S, cfg.D), dtype=np.float32)
    for core in range(8):
        b, p = divmod(core, 2)
        out[b, p::2] = res.results[core]["out"]
    return out


# version 30
# speedup vs baseline: 1.1426x; 1.0070x over previous
import math
import numpy as np

import concourse.bass as bass
import concourse.mybir as mybir
from concourse.bass_utils import run_bass_kernel_spmd

F32 = mybir.dt.float32
BF16 = mybir.dt.bfloat16
FP8 = mybir.dt.float8e4
AX = mybir.AxisListType
ALU = mybir.AluOpType
ACTF = mybir.ActivationFunctionType

NEG_MASK = -1.0e33
EPS = 1.0e-5


class Cfg:
    def __init__(self, D=2048, S=2048, H=None):
        self.D = D
        self.S = S
        self.H = H if H is not None else 4 * D
        self.SO = S // 2
        self.DC = D // 128
        self.QT = self.SO // 128
        self.KT = S // 128
        self.HC = self.H // 128
        self.SLAB = min(512, self.H, D, self.SO)
        self.DSL = D // self.SLAB
        self.HSL = self.H // self.SLAB
        self.SSL = S // self.SLAB
        self.OSL = self.SO // self.SLAB
        self.QG = min(2, self.QT)
        self.NQG = self.QT // self.QG
        self.NH4 = self.H // 512
        self.W6 = 512
        self.NHW = self.H // self.W6
        self.scale = 1.0 / math.sqrt(float(D))
        self.debug = False
        assert self.SLAB == 512, "psum accumulation slabs must be bank-aligned"
        assert D % 128 == 0 and self.SO % 128 == 0 and self.H % self.SLAB == 0
        assert D % self.SLAB == 0 and self.SO % self.SLAB == 0
        assert self.NH4 % 2 == 0 and self.QT % self.QG == 0
        assert self.NHW % 4 == 0 and self.W6 % 512 == 0


class Sem:
    def __init__(self, h):
        self.h = h
        self.n = 0

    def inc(self, instr, k=1):
        instr.then_inc(self.h, k)
        self.n += k
        return self.n


def build_program(c: Cfg):
    nc = bass.Bass("TRN2", target_bir_lowering=False)
    D, S, H, SO = c.D, c.S, c.H, c.SO
    DC, QT, KT, HC = c.DC, c.QT, c.KT, c.HC
    SLAB, DSL, HSL, OSL = c.SLAB, c.DSL, c.HSL, c.OSL

    din = lambda n, sh, dt=F32: nc.dram_tensor(n, sh, dt, kind="ExternalInput").ap()
    x_hbm = din("x", [S, D])
    qkv_hbm = din("qkv", [D, 3 * D])
    qkvc_hbm = din("qkvc", [128, (3 * D // 128) * DC * 128])
    wob_hbm = din("wob", [128, DSL * DC * SLAB])
    gamma_hbm = din("gamma", [D])
    beta_hbm = din("beta", [D])
    gwT_hbm = din("gwT", [D, H])
    vwT_hbm = din("vwT", [D, H])
    owT_hbm = din("owT", [H, D])
    gb_hbm = din("gate_b", [H])
    vb_hbm = din("val_b", [H])
    ob_hbm = din("out_b", [D])
    gbbt_hbm = din("gbbt", [128, H])
    vbbt_hbm = din("vbbt", [128, H])
    obbt_hbm = din("obbt", [128, D])
    mko_hbm = din("mask_own", [128, 128])
    mkp_hbm = din("mask_peer", [128, 128])
    eye32_hbm = din("eye32", [128, 128])
    eye16_hbm = din("eye16", [128, 128], BF16)
    out_hbm = nc.dram_tensor("out", [SO, D], F32, kind="ExternalOutput").ap()

    dint = lambda n, sh, dt=F32: nc.dram_tensor(n, sh, dt, kind="Internal").ap()
    hn_sp = dint("hn_sp", [KT, 128, DC * 128])
    k_sp = dint("k_sp", [128, KT, DC * 128])
    v_sp = dint("v_sp", [128, DC, KT * 128])
    q_sp = dint("q_sp", [DC, 128, SO])
    h_sp = dint("h_sp", [SO, D])
    x1_sp = dint("x1_sp", [SO, D])
    wgi_sp = dint("wgi_sp", [HSL, 128, DC * SLAB], FP8)
    wvi_sp = dint("wvi_sp", [HSL, 128, DC * SLAB], FP8)
    woi_sp = dint("woi_sp", [128, HC * D], FP8)
    x2i_sp = dint("x2i_sp", [QT, 128, H], BF16)
    if c.debug:
        qa_l = [SO - 128 * (m % QT) for m in range(KT)]
        dbg_ptp = dint("dbg_ptp", [128, int(sum(qa_l))])
        dbg_l = dint("dbg_l", [128, QT])
        dbg_negM = dint("dbg_negM", [128, SO])
        dbg_ofm = dint("dbg_ofm", [128, DC * SO])
        dbg_ao = dint("dbg_ao", [128, QT * D])

    te, ve, ac, gp, sy = nc.tensor, nc.vector, nc.scalar, nc.gpsimd, nc.sync

    sem_cms = []

    def mksem(name):
        cm = nc.semaphore(name)
        h = cm.__enter__()
        sem_cms.append(cm)
        return Sem(h)

    sL = mksem("sL")
    sX = [mksem("sX0"), mksem("sX1")]
    sW = [mksem("sW0"), mksem("sW1")]
    sH = [mksem("sH0"), mksem("sH1")]
    sU = [mksem("sU0"), mksem("sU1")]
    sK = mksem("sK")
    sWS = mksem("sWS")
    sQW = [mksem("sQW0"), mksem("sQW1"), mksem("sQW2"), mksem("sQW3")]
    sQZ = [mksem("sQZ0"), mksem("sQZ1"), mksem("sQZ2"), mksem("sQZ3")]
    sA5 = mksem("sA5")
    sV5 = mksem("sV5")
    sS = mksem("sS")
    sP = mksem("sP")
    sP6 = [mksem("sP6a"), mksem("sP6b")]
    sKV = mksem("sKV")
    sT = mksem("sT")
    sV = mksem("sV")
    sA = mksem("sA")
    sG = mksem("sG")
    sO = mksem("sO")
    sB = mksem("sB")

    def barrier(full=True):
        drains = [sS, sL, sO, sK, sWS, sKV, sX[0], sX[1], sW[0], sW[1],
                  sH[0], sH[1], sU[0], sU[1]]
        if full:
            drains += [sP] + sP6 + sQW + sQZ
        for s_ in drains:
            sy.wait_ge(s_.h, s_.n)
        gp.wait_ge(sG.h, sG.n)
        engs = [te, ve, ac, gp, sy]
        for e in engs:
            sB.inc(e.drain() if e is not sy else e.nop())
        for e in engs:
            e.wait_ge(sB.h, sB.n)

    live = {}

    def alloc(name, sh, dt=F32):
        cm = nc.sbuf_tensor(name, sh, dt)
        h = cm.__enter__()
        live[name] = cm
        return h

    def alloc_r(name, sh, dt=F32):
        cm = nc.sbuf_tensor(name, sh, dt, side="right")
        h = cm.__enter__()
        live[name] = cm
        return h

    def free(*names):
        for n in names:
            live.pop(n).__exit__(None, None, None)

    psum_live = {}

    def palloc(name, sh, dt=F32):
        cm = nc.psum_tensor(name, sh, dt)
        h = cm.__enter__()
        psum_live[name] = cm
        return h

    def pfree(*names):
        for n in names:
            psum_live.pop(n).__exit__(None, None, None)

    ones = alloc("ones", [128, 1])
    ones16 = alloc("ones16", [128, 1], BF16)
    ones_row = alloc("ones_row", [1, 128])
    eye32 = alloc("eye32s", [128, 128])
    eye16 = alloc("eye16s", [128, 128], BF16)
    mask_own = alloc("mask_own_s", [128, 128])
    mask_peer = alloc("mask_peer_s", [128, 128])
    nstat = 6 * ((D + 511) // 512)
    stat = alloc("stat", [128, nstat])
    agg = alloc("agg", [128, 2])
    vep = alloc("vep", [128, 1])
    rr0 = alloc("rr0", [128, 1])
    tt0 = alloc("tt0", [128, 1])
    tt1 = alloc("tt1", [128, 1])
    recip_l = alloc("recip_l", [128, QT])
    gxcol = alloc("gxcol", [128, QT])
    ggvcol = alloc("ggvcol", [128, QT])
    sc_x = alloc("sc_x", [128, QT])
    sc_gv = alloc("sc_gv", [128, QT])
    scg_t = alloc("scg_t", [128, QT])
    scv_t = alloc("scv_t", [128, QT])
    sc3 = alloc("sc3", [128, QT])
    gwpart = alloc("gwpart", [128, 2])
    chalf = alloc("chalf", [128, 2])
    gwsum = alloc("gwsum", [128, 3])
    gwtot_sb = alloc("gwtot_sb", [1, 3])
    gwrec_sb = alloc("gwrec_sb", [1, 3])
    rgam = alloc("rgam", [128, 3])
    gamw = alloc("gamw", [128, 3])
    negM = alloc("negM", [128, SO])
    qn_row = alloc("qn_row", [1, SO])
    maxkn = alloc("maxkn", [1, 1])
    negMr = alloc("negMr", [1, SO])

    def ln_stats(x_ap, nfree):
        nchunks = (nfree + 511) // 512
        for i in range(nchunks):
            lo = i * 512
            hi = min(nfree, lo + 512)
            ve.bn_stats(stat[:, 6 * i:6 * (i + 1)], x_ap[:, lo:hi])
        ve.drain()
        ve.bn_aggr(agg[:], stat[:, :6 * nchunks])
        ve.drain()

    def ln_rstd():
        ve.tensor_scalar(vep[:], agg[:, 1:2], EPS, None, op0=ALU.add)
        sV.inc(ve.drain())
        ac.wait_ge(sV.h, sV.n)
        sA.inc(ac.activation(tt0[:], vep[:], ACTF.Sqrt))
        ve.wait_ge(sA.h, sA.n)
        ve.reciprocal(rr0[:], tt0[:])
        for _ in range(2):
            ve.drain()
            ve.tensor_tensor(tt0[:], rr0[:], rr0[:], op=ALU.mult)
            ve.drain()
            ve.tensor_tensor(tt0[:], tt0[:], vep[:], op=ALU.mult)
            ve.drain()
            ve.tensor_scalar(tt0[:], tt0[:], -0.5, 1.5, op0=ALU.mult, op1=ALU.add)
            ve.drain()
            ve.tensor_tensor(rr0[:], rr0[:], tt0[:], op=ALU.mult)
        ve.drain()
        ve.tensor_scalar(tt1[:], agg[:, 0:1], -1.0, rr0[:], op0=ALU.mult, op1=ALU.mult)
        sV.inc(ve.drain())

    def ln_apply(dst, src):
        ac.wait_ge(sV.h, sV.n)
        ac.drain()
        sA.inc(ac.activation(dst, src, ACTF.Identity, bias=tt1[:], scale=rr0[:]))

    def bcast(dst, src_row, n, ps, war_cell=None):
        for lo in range(0, n, 512):
            w = min(512, n - lo)
            te.wait_ge(sA.h, sA.n if war_cell is None else war_cell[0])
            sT.inc(te.matmul(ps[:, :w], ones_row[:1, :], src_row[:1, lo:lo + w],
                             start=True, stop=True))
            ac.wait_ge(sT.h, sT.n)
            sA.inc(ac.copy(dst[:, lo:lo + w], ps[:, :w]))
            if war_cell is not None:
                war_cell[0] = sA.n
        return sT.n

    def round_ladder(f_ap, sc_ap, out_ap, w_scr, u_scr, p_scr):
        ve.tensor_scalar(w_scr, f_ap, sc_ap, 128.5, op0=ALU.mult, op1=ALU.add)
        ve.drain()
        ve.tensor_copy(u_scr, w_scr)
        ve.drain()
        for j in range(7, -1, -1):
            k = float(1 << j)
            ve.tensor_scalar(p_scr, u_scr, k, None, op0=ALU.is_ge)
            ve.drain()
            ve.scalar_tensor_tensor(u_scr, p_scr, -k, u_scr, op0=ALU.mult, op1=ALU.add)
            ve.drain()
        ve.scalar_tensor_tensor(p_scr, u_scr, -1.0, w_scr, op0=ALU.mult, op1=ALU.add)
        ve.drain()
        ve.tensor_scalar(out_ap, p_scr, -128.0, None, op0=ALU.add)
        ve.drain()

    WCW = 512
    wqb = alloc_r("wqb", [128, 4 * WCW])
    wpb = alloc_r("wpb", [128, 2 * WCW])
    wmb = alloc_r("wmb", [128, 2 * WCW])
    wib = alloc_r("wib", [128, 2 * WCW], FP8)
    NCG = H // WCW

    def wchunks(src_hbm, mat_sp):
        out = []
        for dc in range(DC):
            for cg in range(NCG):
                def store(wib_, sem, dc=dc, cg=cg):
                    s_lo = (cg * WCW) // SLAB
                    s_hi = ((cg + 1) * WCW) // SLAB
                    for s in range(s_lo, s_hi):
                        off = s * SLAB - cg * WCW
                        sem.inc(gp.dma_start(
                            out=mat_sp[s][:, dc * SLAB:(dc + 1) * SLAB],
                            in_=wib_[:, off:off + SLAB]), 16)
                out.append((src_hbm[dc * 128:(dc + 1) * 128, cg * WCW:(cg + 1) * WCW],
                            WCW, store))
        return out

    def owt_chunks():
        out = []
        for ch in range(HC):
            for hf in range(D // WCW):
                def store(wib_, sem, ch=ch, hf=hf):
                    sem.inc(gp.dma_start(
                        out=woi_sp[:, ch * D + hf * WCW: ch * D + (hf + 1) * WCW],
                        in_=wib_[:, :WCW]), 16)
                out.append((owT_hbm[ch * 128:(ch + 1) * 128,
                                    hf * WCW:(hf + 1) * WCW], WCW, store))
        return out

    wq_us = [0, 0, 0, 0]

    class Pump:
        DEPTH = 4

        def __init__(self, stages):
            self.items = []
            self.nslot = 0
            for compute, idx, chunks in stages:
                if callable(chunks):
                    self.items.append((compute, idx, False, None, None))
                    continue
                for ci, ck in enumerate(chunks):
                    self.items.append((compute, idx, ci == 0, ck, self.nslot))
                    self.nslot += 1
            self.n = len(self.items)
            self.cursor = 0
            self.loaded = 0
            self.lds = {}

        def _load(self):
            i = self.loaded
            item = self.items[i]
            if item[3] is not None:
                compute, idx, first, (src_, ncols, store_fn), sl = item
                b = sl % 4
                if compute is gamma_compute:
                    qeng, sem = ac, sQW[b]
                else:
                    qeng, sem = gp, sQZ[b]
                qeng.wait_ge(sA5.h, wq_us[b])
                tok = sem.inc(
                    qeng.dma_start(out=wqb[:, b * WCW:b * WCW + ncols], in_=src_), 16)
                self.lds[i] = (sem, tok)
            self.loaded += 1

        def pump(self, k):
            while k > 0 and self.cursor < self.n:
                while self.loaded < min(self.cursor + self.DEPTH, self.n):
                    self._load()
                compute, idx, first, ck, sl = self.items[self.cursor]
                if ck is None:
                    compute(idx)
                else:
                    self.compute_chunk(compute, self.cursor, idx, first, ck, sl)
                self.cursor += 1
                k -= 1

        def compute_chunk(self, compute, i, idx, first, ck, sl):
            src_, ncols, store_fn = ck
            compute(self, i, idx, first, ncols, store_fn, sl)

        def drain(self):
            self.pump(self.n)

    def gamma_compute(pp, i, idx, first, ncols, store_fn, sl):
        b = sl % 4
        wb_ = wqb[:, b * WCW:b * WCW + ncols]
        if first:
            ve.wait_ge(sA5.h, sA5.n)
            ve.memset(gwsum[:, idx:idx + 1], 0.0)
            sV5.inc(ve.drain())
            ac.wait_ge(sV5.h, sV5.n)
        if not hasattr(pp, "gv_tok"):
            pp.gv_tok = [0, 0]
        h2 = sl % 2
        ldsem, ldtok = pp.lds[i]
        ac.wait_ge(ldsem.h, ldtok)
        if pp.gv_tok[h2]:
            ac.wait_ge(sV5.h, pp.gv_tok[h2])
        if getattr(pp, "sg_tok", None) and pp.sg_tok[h2]:
            ac.wait_ge(sG.h, pp.sg_tok[h2])
        ac.drain()
        sA5.inc(ac.activation(wpb[:, h2 * WCW:h2 * WCW + ncols], wb_,
                              ACTF.Abs, accum_out=gwpart[:, h2:h2 + 1]))
        wq_us[b] = sA5.n
        ve.wait_ge(sA5.h, sA5.n)
        ve.tensor_tensor(gwsum[:, idx:idx + 1], gwsum[:, idx:idx + 1],
                         gwpart[:, h2:h2 + 1], op=ALU.add)
        pp.gv_tok[h2] = sV5.inc(ve.drain())

    def quant_compute(pp, i, idx, first, ncols, store_fn, sl):
        b = sl % 4
        h2 = sl % 2
        wb_ = wqb[:, b * WCW:b * WCW + ncols]
        wpb_ = wpb[:, h2 * WCW:h2 * WCW + ncols]
        wmb_ = wmb[:, h2 * WCW:h2 * WCW + ncols]
        wib_ = wib[:, h2 * WCW:h2 * WCW + ncols]
        if not hasattr(pp, "sg_tok"):
            pp.sg_tok = [0, 0]
            pp.st_tok = [0, 0]
        ldsem, ldtok = pp.lds[i]
        ac.wait_ge(ldsem.h, ldtok)
        if pp.sg_tok[h2]:
            ac.wait_ge(sG.h, pp.sg_tok[h2])
        sA5.inc(ac.activation(wpb_, wb_, ACTF.Sign,
                              bias=chalf[:, 0:1], scale=rgam[:, idx:idx + 1]))
        sA5.inc(ac.activation(wmb_, wb_, ACTF.Sign,
                              bias=chalf[:, 1:2], scale=rgam[:, idx:idx + 1]))
        wq_us[b] = sA5.n
        gp.wait_ge(sA5.h, sA5.n)
        if pp.st_tok[h2]:
            gp.wait_ge(sP6[h2].h, pp.st_tok[h2])
        pp.sg_tok[h2] = sG.inc(gp.tensor_tensor(wib_, wpb_, wmb_, op=ALU.add))
        gp.wait_ge(sG.h, pp.sg_tok[h2])
        store_fn(wib[:, h2 * WCW:(h2 + 1) * WCW], sP6[h2])
        pp.st_tok[h2] = sP6[h2].n

    def wquant_bcast(idx):
        n_elems = D * H if idx < 2 else H * D
        te.wait_ge(sV5.h, sV5.n)
        gps = palloc("gwps" + str(idx), [1, 1])
        mm_tok = sT.inc(te.matmul(gps[:1, :1], gwsum[:, idx:idx + 1], ones[:],
                                  start=True, stop=True))
        ve.wait_ge(sT.h, mm_tok)
        ve.tensor_scalar(gwtot_sb[:1, idx:idx + 1], gps[:1, :1], 1.0 / float(n_elems),
                         1e-5, op0=ALU.mult, op1=ALU.max)
        ve.drain()
        ve.reciprocal(gwrec_sb[:1, idx:idx + 1], gwtot_sb[:1, idx:idx + 1])
        sV5.inc(ve.drain())
        pfree("gwps" + str(idx))
        bc_psw = palloc("bc_psw" + str(idx), [128, 512])
        te.wait_ge(sV5.h, sV5.n)
        cell = [0]
        bcast(rgam[:, idx:idx + 1], gwrec_sb[:1, idx:idx + 1], 1, bc_psw,
              war_cell=cell)
        bcast(gamw[:, idx:idx + 1], gwtot_sb[:1, idx:idx + 1], 1, bc_psw,
              war_cell=cell)
        ac.drain()
        pfree("bc_psw" + str(idx))

    wpump = Pump([
        (gamma_compute, 0, wchunks(gwT_hbm, wgi_sp)),
        (wquant_bcast, 0, wquant_bcast),
        (gamma_compute, 1, wchunks(vwT_hbm, wvi_sp)),
        (wquant_bcast, 1, wquant_bcast),
        (gamma_compute, 2, owt_chunks()),
        (wquant_bcast, 2, wquant_bcast),
        (quant_compute, 0, wchunks(gwT_hbm, wgi_sp)),
        (quant_compute, 1, wchunks(vwT_hbm, wvi_sp)),
        (quant_compute, 2, owt_chunks()),
    ])

    gam_b = alloc("gam_b", [128, D])
    bet_b = alloc("bet_b", [128, D])
    sL.inc(sy.dma_start(out=eye32[:], in_=eye32_hbm), 16)
    sL.inc(sy.dma_start(out=eye16[:], in_=eye16_hbm), 16)
    sL.inc(sy.dma_start(out=mask_own[:], in_=mko_hbm), 16)
    sL.inc(sy.dma_start(out=mask_peer[:], in_=mkp_hbm), 16)
    sL.inc(sy.dma_start(out=gam_b[:1, :], in_=gamma_hbm[None, :]), 16)
    sL.inc(sy.dma_start(out=bet_b[:1, :], in_=beta_hbm[None, :]), 16)
    ve.memset(ones[:], 1.0)
    ve.memset(ones16[:], 1.0)
    ve.memset(ones_row[:], 1.0)
    ve.memset(chalf[:, 0:1], -0.5)
    ve.memset(chalf[:, 1:2], 0.5)
    sV.inc(ve.drain())
    bc_ps0 = palloc("bc_ps0", [128, 512])
    te.wait_ge(sL.h, sL.n)
    te.wait_ge(sV.h, sV.n)
    bcast(gam_b, gam_b[:1, :], D, bc_ps0)
    bcast(bet_b, bet_b[:1, :], D, bc_ps0)
    pfree("bc_ps0")

    hnf = alloc("hnf", [128, DC * S])
    xt = alloc("xt", [128, 2 * D])
    ptr = palloc("ptr", [128, 1024])
    ve.wait_ge(sA.h, sA.n)
    evac_tok = [0, 0]
    tile_done = [0, 0]
    for i in range(KT):
        b = i % 2
        xb = xt[:, b * D:b * D + D]
        sy.wait_ge(sT.h, tile_done[b])
        ld = sX[b].inc(sy.dma_start(out=xb, in_=x_hbm[i * 128:(i + 1) * 128, :]), 16)
        ve.wait_ge(sX[b].h, ld)
        ln_stats(xb, D)
        ln_rstd()
        ln_apply(xb, xb)
        if i < QT:
            sy.wait_ge(sA.h, sA.n)
            sS.inc(sy.dma_start(out=h_sp[i * 128:(i + 1) * 128, :], in_=xb), 16)
        ve.wait_ge(sA.h, sA.n)
        ln_stats(xb, D)
        ln_rstd()
        ac.wait_ge(sS.h, sS.n)
        ln_apply(xb, xb)
        ve.wait_ge(sA.h, sA.n)
        ve.tensor_tensor(xb, xb, gam_b[:], op=ALU.mult)
        ve.drain()
        ve.tensor_tensor(xb, xb, bet_b[:], op=ALU.add)
        sV.inc(ve.drain())
        te.wait_ge(sV.h, sV.n)
        for dc in range(DC):
            pb = ptr[:, (dc % 2) * 512:(dc % 2) * 512 + 128]
            te.wait_ge(sA.h, evac_tok[dc % 2])
            sT.inc(te.transpose(pb, xb[:, dc * 128:(dc + 1) * 128], eye32[:]))
            ac.wait_ge(sT.h, sT.n)
            dst = hnf[:, dc * S + i * 128: dc * S + i * 128 + 128]
            evac_tok[dc % 2] = sA.inc(ac.copy(dst, pb))
            gp.wait_ge(sA.h, sA.n)
            sKV.inc(gp.dma_start(out=hn_sp[i][:, dc * 128:(dc + 1) * 128],
                                 in_=dst), 16)
        tile_done[b] = sT.n
    free("xt")
    pfree("ptr")
    barrier()

    wb = alloc("wb", [128, 2 * DC * 128])
    ksb = alloc("ksb", [128, S])
    ksq = alloc("ksq", [128, SLAB])
    nsb = alloc("nsb", [1, S], BF16)
    kp = palloc("kp", [128, S])
    npp = palloc("npsum", [1, 2 * SLAB])

    def proj(base_col, ntok, store, pump_hook=lambda: None):
        NG = ntok // SLAB
        wtok = [0, 0]
        copy_tok = [0]
        for o in range(DC):
            ob = base_col // 128 + o
            bb = o % 2
            wbo = wb[:, bb * DC * 128:(bb + 1) * DC * 128]
            sy.wait_ge(sT.h, wtok[bb])
            ld = sW[bb].inc(sy.dma_start(
                out=wbo, in_=qkvc_hbm[:, ob * DC * 128:(ob + 1) * DC * 128]), 16)
            te.wait_ge(sW[bb].h, ld)
            te.wait_ge(sV.h, copy_tok[0])
            for dc in range(DC):
                wbb = wbo[:, dc * 128:(dc + 1) * 128]
                for g in range(NG):
                    hsrc = hnf[:, dc * S + g * SLAB: dc * S + (g + 1) * SLAB]
                    sT.inc(te.matmul(kp[:, g * SLAB:(g + 1) * SLAB], wbb, hsrc,
                                     start=(dc == 0), stop=(dc == DC - 1)))
            wtok[bb] = sT.n
            ve.wait_ge(sT.h, sT.n)
            ve.wait_ge(sS.h, sS.n)
            ve.wait_ge(sKV.h, sKV.n)
            ve.tensor_copy(ksb[:, :ntok], kp[:, :ntok])
            copy_tok[0] = sV.inc(ve.drain())
            sy.wait_ge(sV.h, sV.n)
            gp.wait_ge(sV.h, sV.n)
            store(o)
            if o == 0:
                ve.memset(nsb[:1, :ntok], 0.0)
                sV.inc(ve.drain())
            for g in range(NG):
                gb_ = g % 2
                ve.wait_ge(sT.h, sT.n)
                ve.tensor_tensor(ksq[:], ksb[:, g * SLAB:(g + 1) * SLAB],
                                 ksb[:, g * SLAB:(g + 1) * SLAB], op=ALU.mult)
                sV.inc(ve.drain())
                te.wait_ge(sV.h, sV.n)
                nmm = sT.inc(te.matmul(npp[:1, gb_ * SLAB:(gb_ + 1) * SLAB],
                                       ones[:], ksq[:], start=True, stop=True))
                ve.wait_ge(sT.h, nmm)
                ve.tensor_tensor(nsb[:1, g * SLAB:(g + 1) * SLAB],
                                 nsb[:1, g * SLAB:(g + 1) * SLAB],
                                 npp[:1, gb_ * SLAB:(gb_ + 1) * SLAB], op=ALU.add)
                sV.inc(ve.drain())
            pump_hook()

    def store_k(o):
        sKV.inc(gp.dma_start(out=k_sp[:, :, o * 128:(o + 1) * 128],
                             in_=ksb[:, :S]), 16)

    proj(D, S, store_k, lambda: wpump.pump(25))
    ve.wait_ge(sT.h, sT.n)
    ve.tensor_reduce(maxkn[:], nsb[:1, :S], axis=AX.X, op=ALU.max)
    ve.drain()
    ve.tensor_scalar(maxkn[:], maxkn[:], 1e-30, None, op0=ALU.max)
    sV.inc(ve.drain())

    def store_q(o):
        sS.inc(sy.dma_start(out=q_sp[o], in_=ksb[:, :SO]), 16)

    proj(0, SO, store_q, lambda: wpump.pump(25))
    ve.wait_ge(sT.h, sT.n)
    ve.tensor_copy(qn_row[:], nsb[:1, :SO])
    ve.drain()
    ve.tensor_scalar(qn_row[:], qn_row[:], 1e-30, None, op0=ALU.max)
    sV.inc(ve.drain())
    ac.wait_ge(sV.h, sV.n)
    sA.inc(ac.activation(qn_row[:], qn_row[:], ACTF.Sqrt))
    sA.inc(ac.activation(maxkn[:], maxkn[:], ACTF.Sqrt))
    ve.wait_ge(sA.h, sA.n)
    ve.tensor_scalar(negMr[:], qn_row[:], maxkn[:1, :1], -1.0, op0=ALU.mult, op1=ALU.mult)
    sV.inc(ve.drain())
    pfree("npsum", "kp")
    free("nsb")
    bc_ps1 = palloc("bc_ps1", [128, 512])
    te.wait_ge(sV.h, sV.n)
    bcast(negM, negMr, SO, bc_ps1)
    pfree("bc_ps1")
    free("ksq", "ksb", "wb")
    free("hnf", "bet_b", "gam_b")
    free("negMr", "maxkn", "qn_row")
    b1_t, b1_v, b1_a = sT.n, sV.n, sA.n

    wv_sb = alloc("wv_sb", [128, DC * D])
    hbuf = alloc("hbuf", [128, 2 * DC * 128])
    vsb = alloc("vsb", [128, D])
    vp = palloc("vp", [128, 1024])
    sy.wait_ge(sT.h, b1_t)
    sy.wait_ge(sV.h, b1_v)
    ac.wait_ge(sT.h, b1_t)
    ac.wait_ge(sV.h, b1_v)
    for dc in range(DC):
        if dc % 2 == 0:
            sL.inc(sy.dma_start(out=wv_sb[:, dc * D:(dc + 1) * D],
                                in_=qkv_hbm[dc * 128:(dc + 1) * 128, 2 * D:3 * D]), 16)
        else:
            sWS.inc(ac.dma_start(out=wv_sb[:, dc * D:(dc + 1) * D],
                                 in_=qkv_hbm[dc * 128:(dc + 1) * 128, 2 * D:3 * D]), 16)
    te.wait_ge(sL.h, sL.n)
    te.wait_ge(sWS.h, sWS.n)
    te.wait_ge(sA.h, b1_a)
    te.wait_ge(sV.h, b1_v)
    huse = [0, 0]
    vevac = [0, 0]
    for m in range(KT):
        b = m % 2
        hb = hbuf[:, b * DC * 128:(b + 1) * DC * 128]
        sy.wait_ge(sT.h, huse[b])
        sH[b].inc(sy.dma_start(out=hb, in_=hn_sp[m]), 16)
        te.wait_ge(sH[b].h, sH[b].n)
        if m > 0:
            te.wait_ge(sKV.h, sKV.n)
        for ds in range(DSL):
            pbk = vp[:, (ds % 2) * 512:(ds % 2) * 512 + SLAB]
            te.wait_ge(sV.h, vevac[ds % 2])
            for dc in range(DC):
                sT.inc(te.matmul(pbk, hb[:, dc * 128:(dc + 1) * 128],
                                 wv_sb[:, dc * D + ds * SLAB: dc * D + (ds + 1) * SLAB],
                                 start=(dc == 0), stop=(dc == DC - 1)))
            ve.wait_ge(sT.h, sT.n)
            ve.tensor_copy(vsb[:, ds * SLAB:(ds + 1) * SLAB], pbk)
            vevac[ds % 2] = sV.inc(ve.drain())
        huse[b] = sT.n
        gp.wait_ge(sV.h, sV.n)
        sKV.inc(gp.dma_start(out=v_sp[:, :, m * 128:(m + 1) * 128], in_=vsb[:]), 16)
        wpump.pump(24)
    free("vsb", "hbuf", "wv_sb")
    pfree("vp")
    b2_t, b2_v = sT.n, sV.n

    ofm = alloc("ofm", [128, DC * SO])
    _qa_list = [SO - 128 * (m % QT) for m in range(KT)]
    ptp = alloc("ptp", [128, int(sum(_qa_list))])
    qres = alloc("qres", [128, DC * SO])
    kbuf = alloc("kbuf", [128, 2 * DC * 128])
    qa_list = _qa_list
    PTW = int(sum(qa_list))
    poff = np.cumsum([0] + qa_list)[:-1].tolist()
    sp_ps = palloc("sp_ps", [128, 2 * SO])
    l_ps = palloc("l_ps", [128, QT])
    sy.wait_ge(sT.h, b2_t)
    sy.wait_ge(sV.h, b2_v)
    sy.wait_ge(sS.h, sS.n)
    sy.wait_ge(sKV.h, sKV.n)
    for o in range(DC):
        sL.inc(sy.dma_start(out=qres[:, o * SO:(o + 1) * SO], in_=q_sp[o]), 16)
    te.wait_ge(sL.h, sL.n)
    te.wait_ge(sV.h, b2_v)
    kuse = [0, 0]
    pset_tok = [0, 0]
    ldk_tok = [0, 0]
    exp_tok = [0]
    first_l = True
    ldk_tok[0] = sX[0].inc(sy.dma_start(out=kbuf[:, :DC * 128], in_=k_sp[:, 0, :]), 16)
    for m in range(KT):
        b = m % 2
        kb = kbuf[:, b * DC * 128:(b + 1) * DC * 128]
        if m + 1 < KT:
            b2 = (m + 1) % 2
            sy.wait_ge(sT.h, kuse[b2])
            ldk_tok[b2] = sX[b2].inc(sy.dma_start(
                out=kbuf[:, b2 * DC * 128:(b2 + 1) * DC * 128],
                in_=k_sp[:, m + 1, :]), 16)
        qoff = 128 * (m % QT)
        qa = SO - qoff
        te.wait_ge(sX[b].h, ldk_tok[b])
        te.wait_ge(sV.h, pset_tok[b])
        base = b * SO
        nseg = (qa + SLAB - 1) // SLAB
        for seg in range(nseg):
            lo = seg * SLAB
            hi = min(qa, lo + SLAB)
            for dc in range(DC):
                sT.inc(te.matmul(
                    sp_ps[:, base + lo:base + hi],
                    kb[:, dc * 128:(dc + 1) * 128],
                    qres[:, dc * SO + qoff + lo: dc * SO + qoff + hi],
                    start=(dc == 0), stop=(dc == DC - 1)))
        kuse[b] = sT.n
        wpump.pump(5)
        pm = ptp[:, poff[m]:poff[m] + qa]
        ve.wait_ge(sT.h, sT.n)
        ve.tensor_tensor(pm, sp_ps[:, base:base + qa], negM[:, qoff:SO], op=ALU.add)
        sV.inc(ve.drain())
        pset_tok[b] = sV.n
        mk = mask_own if m < QT else mask_peer
        ve.tensor_tensor(ptp[:, poff[m]:poff[m] + 128],
                         ptp[:, poff[m]:poff[m] + 128], mk[:], op=ALU.add)
        sV.inc(ve.drain())
        ac.wait_ge(sV.h, sV.n)
        exp_tok[0] = sA.inc(ac.activation(pm, pm, ACTF.Exp, scale=c.scale))
        te.wait_ge(sA.h, sA.n)
        for jj in range(qa // 128):
            col = qoff // 128 + jj
            sT.inc(te.matmul(l_ps[:, col:col + 1],
                             ptp[:, poff[m] + jj * 128: poff[m] + (jj + 1) * 128],
                             ones[:], start=first_l, stop=(m == KT - 1),
                             skip_group_check=True))
            first_l = False
        wpump.pump(5)
    ve.wait_ge(sT.h, sT.n)
    ve.reciprocal(recip_l[:], l_ps[:])
    sV.inc(ve.drain())
    if c.debug:
        sy.wait_ge(sV.h, sV.n)
        sy.wait_ge(sA.h, sA.n)
        sS.inc(sy.dma_start(out=dbg_ptp[:], in_=ptp[:]), 16)
        sS.inc(sy.dma_start(out=dbg_l[:], in_=recip_l[:]), 16)
        sS.inc(sy.dma_start(out=dbg_negM[:], in_=negM[:]), 16)
    free("kbuf", "qres")
    pfree("l_ps", "sp_ps")
    b3_t, b3_v, b3_a = sT.n, sV.n, exp_tok[0]

    vbuf = alloc("vbuf", [128, 2 * KT * 128])
    o_ps = palloc("o_ps", [128, 2 * SO])
    vuse = [0, 0]
    oset_tok = [0, 0]
    sy.wait_ge(sT.h, b3_t)
    te.wait_ge(sV.h, b3_v)
    te.wait_ge(sA.h, b3_a)
    for cc in range(DC):
        b = cc % 2
        vb = vbuf[:, b * KT * 128:(b + 1) * KT * 128]
        sy.wait_ge(sT.h, vuse[b])
        ld = sU[b].inc(sy.dma_start(out=vb, in_=v_sp[:, cc, :]), 16)
        te.wait_ge(sU[b].h, ld)
        te.wait_ge(sV.h, oset_tok[b])
        base = b * SO
        for m in range(KT):
            qoff = 128 * (m % QT)
            for seg in range(OSL):
                lo = max(seg * SLAB, qoff)
                hi = (seg + 1) * SLAB
                if lo >= hi:
                    continue
                sT.inc(te.matmul(
                    o_ps[:, base + lo:base + hi],
                    vb[:, m * 128:(m + 1) * 128],
                    ptp[:, poff[m] + lo - qoff: poff[m] + hi - qoff],
                    start=(m == 0), stop=(m == KT - 1), skip_group_check=True))
        vuse[b] = sT.n
        ve.wait_ge(sT.h, sT.n)
        ve.tensor_copy(ofm[:, cc * SO:(cc + 1) * SO], o_ps[:, base:base + SO])
        oset_tok[b] = sV.inc(ve.drain())
        wpump.pump(20)
    if c.debug:
        sy.wait_ge(sA.h, sA.n)
        sS.inc(sy.dma_start(out=dbg_ofm[:], in_=ofm[:]), 16)
    free("vbuf")
    free("ptp")
    pfree("o_ps")
    b4_t, b4_v = sT.n, sV.n

    ao = alloc("ao", [128, QT * D])
    wos = alloc("wos", [128, DC * SLAB])
    ao_ps = palloc("ao_ps", [128, 1024])
    aevac = [0, 0]
    sy.wait_ge(sT.h, b4_t)
    te.wait_ge(sV.h, b4_v)
    for ds in range(DSL):
        te.wait_ge(sT.h, sT.n)
        sy.wait_ge(sT.h, sT.n)
        sWS.inc(sy.dma_start(
            out=wos[:], in_=wob_hbm[:, ds * DC * SLAB:(ds + 1) * DC * SLAB]), 16)
        te.wait_ge(sWS.h, sWS.n)
        for qc in range(QT):
            bk = qc % 2
            pbk = ao_ps[:, bk * 512:bk * 512 + SLAB]
            te.wait_ge(sV.h, aevac[bk])
            for dc in range(DC):
                sT.inc(te.matmul(pbk, ofm[:, dc * SO + qc * 128: dc * SO + (qc + 1) * 128],
                                 wos[:, dc * SLAB:(dc + 1) * SLAB],
                                 start=(dc == 0), stop=(dc == DC - 1)))
            ve.wait_ge(sT.h, sT.n)
            ve.tensor_scalar(ao[:, qc * D + ds * SLAB: qc * D + (ds + 1) * SLAB],
                             pbk, recip_l[:, qc:qc + 1], None, op0=ALU.mult)
            aevac[bk] = sV.inc(ve.drain())
    if c.debug:
        sy.wait_ge(sV.h, sV.n)
        sS.inc(sy.dma_start(out=dbg_ao[:], in_=ao[:]), 16)
        barrier()
    free("wos")
    pfree("ao_ps")
    hbt = alloc("hbt", [128, D])
    xbt = alloc("xbt", [128, D])
    z2t = alloc("z2t", [128, D])
    for qc in range(QT):
        sy.wait_ge(sV.h, sV.n)
        sL.inc(sy.dma_start(out=hbt[:], in_=h_sp[qc * 128:(qc + 1) * 128, :]), 16)
        ld = sL.inc(sy.dma_start(out=xbt[:], in_=x_hbm[qc * 128:(qc + 1) * 128, :]), 16)
        at = ao[:, qc * D:(qc + 1) * D]
        ve.wait_ge(sL.h, ld)
        ve.tensor_tensor(at, at, hbt[:], op=ALU.add)
        ve.drain()
        ln_stats(at, D)
        ln_rstd()
        ac.wait_ge(sS.h, sS.n)
        ln_apply(z2t[:], at)
        ve.wait_ge(sA.h, sA.n)
        ve.tensor_tensor(z2t[:], z2t[:], xbt[:], op=ALU.add)
        sV.inc(ve.drain())
        sy.wait_ge(sV.h, sV.n)
        sS.inc(sy.dma_start(out=x1_sp[qc * 128:(qc + 1) * 128, :], in_=z2t[:]), 16)
    wpump.drain()
    free("z2t", "xbt", "hbt", "ao", "ofm")
    free("wib", "wmb", "wpb", "wqb")
    barrier()

    free("negM")
    NSLOT = 3
    NH4 = c.NH4
    xiT = alloc("xiT", [128, DC * SO], BF16)
    ft = alloc("ft", [128, D])
    rs1 = alloc("rs1", [128, 1024])
    xit = alloc("xit", [128, D], BF16)
    wgsl = alloc("wgsl", [128, 2 * DC * SLAB], FP8)
    wvsl = alloc("wvsl", [128, 2 * DC * SLAB], FP8)
    gbb = alloc("gbb", [128, SLAB])
    vbb = alloc("vbb", [128, SLAB])
    gsl = alloc("gsl", [128, 2 * SLAB])
    vsl = alloc("vsl", [128, SLAB])
    sgn = alloc("sgn", [128, 4 * c.W6])
    ubuf = alloc("ubuf", [128, 4 * c.W6])
    scr16 = alloc("scr16", [128, 2 * c.W6], BF16)
    cm1275 = alloc("cm1275", [128, 1])
    c1285 = alloc("c1285", [128, 1])
    dsc = alloc("dsc", [128, 128])
    gvb = alloc("gvb", [128, NSLOT * H])
    pt16 = palloc("pt16", [128, 2048], BF16)
    gps2 = palloc("gps2", [128, 1024])
    vps2 = palloc("vps2", [128, 1024])
    ptq = palloc("ptq", [128, 1024])
    ve.memset(cm1275[:], -127.5)
    ve.memset(c1285[:], 128.5)
    ve.memset(ggvcol[:], 0.0)
    sV.inc(ve.drain())
    ac.wait_ge(sV.h, sV.n)

    slot_free = [None] * NSLOT
    wu = [0, 0]
    bias_rd = [0]
    silu_tok = [0, 0]
    pfree_tok = [0, 0]
    dsc_war = [0]
    evt6 = [0, 0]
    sgnwar = [0, 0, 0, 0]
    uwar = [0, 0, 0, 0]
    scr_tok = [0, 0]
    evt = [0, 0]
    xa_tok = [0] * QT
    ft_free = [0]

    def p6aA(qc):
        sy.wait_ge(sV.h, ft_free[0])
        ld = sX[0].inc(sy.dma_start(out=ft[:], in_=x1_sp[qc * 128:(qc + 1) * 128, :]), 16)
        ve.wait_ge(sX[0].h, ld)
        ln_stats(ft[:], D)
        ln_rstd()
        ln_apply(ft[:], ft[:])
        ve.wait_ge(sA.h, sA.n)
        ve.tensor_reduce(gxcol[:, qc:qc + 1], ft[:], axis=AX.X, op=ALU.max,
                         apply_absolute_value=True)
        ve.drain()
        ve.tensor_scalar(gxcol[:, qc:qc + 1], gxcol[:, qc:qc + 1], 1e-5, None, op0=ALU.max)
        ve.drain()
        ve.reciprocal(tt1[:], gxcol[:, qc:qc + 1])
        ve.drain()
        ve.tensor_scalar(sc_x[:, qc:qc + 1], tt1[:], 127.0, None, op0=ALU.mult)
        ve.drain()
        ve.tensor_scalar(tt0[:], gxcol[:, qc:qc + 1], 0.5 / 127.0, None, op0=ALU.mult)
        ve.drain()
        ve.tensor_tensor(scg_t[:, qc:qc + 1], tt0[:], gamw[:, 0:1], op=ALU.mult)
        ve.tensor_tensor(scv_t[:, qc:qc + 1], tt0[:], gamw[:, 1:2], op=ALU.mult)
        sV.inc(ve.drain())

    def p6aB(qc):
        HD = min(512, D // 2)
        wtok = sV.inc(ve.tensor_scalar(ft[:], ft[:], sc_x[:, qc:qc + 1], 128.5,
                                       op0=ALU.mult, op1=ALU.add))
        fin_tok = wtok
        for ps_ in range(D // (2 * HD)):
            ac.wait_ge(sV.h, fin_tok)
            ac.drain()
            up = {}
            for ci in range(2):
                sl_ = slice(ps_ * 2 * HD + ci * HD, ps_ * 2 * HD + (ci + 1) * HD)
                sA.inc(ac.activation(rs1[:, ci * HD:(ci + 1) * HD], ft[:, sl_],
                                     ACTF.Identity, bias=cm1275[:]))
                up[ci] = 0
            ac.drain()
            for j in range(8):
                kpow = -float(1 << (6 - j)) if j <= 6 else -0.5
                for ci in range(2):
                    sl_ = slice(ps_ * 2 * HD + ci * HD, ps_ * 2 * HD + (ci + 1) * HD)
                    uch = rs1[:, ci * HD:(ci + 1) * HD]
                    sch = xit[:, sl_]
                    if up[ci]:
                        ac.wait_ge(sV.h, up[ci])
                    sA.inc(ac.activation(sch, uch, ACTF.Sign, bias=chalf[:, 0:1]))
                    ve.wait_ge(sA.h, sA.n)
                    up[ci] = sV.inc(ve.scalar_tensor_tensor(
                        uch, sch, kpow, uch, op0=ALU.mult, op1=ALU.add))
            for ci in range(2):
                sl_ = slice(ps_ * 2 * HD + ci * HD, ps_ * 2 * HD + (ci + 1) * HD)
                ve.wait_ge(sV.h, up[ci])
                ve.scalar_tensor_tensor(xit[:, sl_], rs1[:, ci * HD:(ci + 1) * HD],
                                        -1.0, ft[:, sl_], op0=ALU.mult, op1=ALU.add)
                ve.drain()
                ve.tensor_scalar(xit[:, sl_], xit[:, sl_], -128.0, None, op0=ALU.add)
                fin_tok = sV.inc(ve.drain())
        ft_free[0] = fin_tok
        te.wait_ge(sV.h, ft_free[0])
        for dc in range(DC):
            pb = pt16[:, (dc % 2) * 1024:(dc % 2) * 1024 + 128]
            te.wait_ge(sA.h, evt[dc % 2])
            sT.inc(te.transpose(pb, xit[:, dc * 128:(dc + 1) * 128], eye16[:]))
            ac.wait_ge(sT.h, sT.n)
            evt[dc % 2] = sA.inc(ac.copy(
                xiT[:, dc * SO + qc * 128: dc * SO + (qc + 1) * 128], pb))
        xa_tok[qc] = sA.n

    def produce_gen(qg):
        for s in range(HSL):
            b = s % 2
            wgb = wgsl[:, b * DC * SLAB:(b + 1) * DC * SLAB]
            wvb2 = wvsl[:, b * DC * SLAB:(b + 1) * DC * SLAB]
            if s == 0:
                te.wait_ge(sA.h, max(xa_tok[qg * c.QG], xa_tok[qg * c.QG + 1]))
            sy.wait_ge(sT.h, wu[b])
            sW[b].inc(sy.dma_start(out=wgb[:], in_=wgi_sp[s]), 16)
            sW[b].inc(sy.dma_start(out=wvb2[:], in_=wvi_sp[s]), 16)
            ldw = sW[b].n
            sy.wait_ge(sV.h, bias_rd[0])
            sH[0].inc(sy.dma_start(out=gbb[:],
                                   in_=gbbt_hbm[:, s * SLAB:(s + 1) * SLAB]), 16)
            ld2 = sH[0].inc(sy.dma_start(out=vbb[:],
                                         in_=vbbt_hbm[:, s * SLAB:(s + 1) * SLAB]), 16)
            te.wait_ge(sW[b].h, ldw)
            for j in range(c.QG):
                qc = qg * c.QG + j
                slot = qc % NSLOT
                gvs = gvb[:, slot * H + s * SLAB: slot * H + (s + 1) * SLAB]
                gps_j = gps2[:, j * SLAB:(j + 1) * SLAB]
                vps_j = vps2[:, j * SLAB:(j + 1) * SLAB]
                gsl_j = gsl[:, j * SLAB:(j + 1) * SLAB]
                te.wait_ge(sV.h, pfree_tok[j])
                for dc in range(DC):
                    sT.inc(te.matmul(gps_j,
                                     xiT[:, dc * SO + qc * 128: dc * SO + (qc + 1) * 128],
                                     wgb[:, dc * SLAB:(dc + 1) * SLAB],
                                     start=(dc == 0), stop=(dc == DC - 1)))
                tok_g = sT.n
                for dc in range(DC):
                    sT.inc(te.matmul(vps_j,
                                     xiT[:, dc * SO + qc * 128: dc * SO + (qc + 1) * 128],
                                     wvb2[:, dc * SLAB:(dc + 1) * SLAB],
                                     start=(dc == 0), stop=(dc == DC - 1)))
                tok_v = sT.n
                ve.wait_ge(sT.h, tok_g)
                ve.wait_ge(sH[0].h, ld2)
                ve.wait_ge(sA.h, silu_tok[j])
                ve.scalar_tensor_tensor(gsl_j, gps_j, scg_t[:, qc:qc + 1],
                                        gbb[:], op0=ALU.mult, op1=ALU.add)
                sV.inc(ve.drain())
                ac.wait_ge(sV.h, sV.n)
                silu_tok[j] = sA.inc(ac.activation(gsl_j, gsl_j, ACTF.Silu))
                ve.wait_ge(sT.h, tok_v)
                ve.scalar_tensor_tensor(vsl[:], vps_j, scv_t[:, qc:qc + 1],
                                        vbb[:], op0=ALU.mult, op1=ALU.add)
                pfree_tok[j] = sV.inc(ve.drain())
                ve.wait_ge(sA.h, silu_tok[j])
                if slot_free[slot] is not None:
                    si, stok = slot_free[slot][s]
                    ve.wait_ge(sP6[si].h, stok)
                ve.tensor_tensor(gvs, gsl_j, vsl[:], op=ALU.mult)
                ve.drain()
                ve.tensor_reduce(tt0[:], gvs, axis=AX.X, op=ALU.max,
                                 apply_absolute_value=True)
                ve.drain()
                ve.tensor_tensor(ggvcol[:, qc:qc + 1], ggvcol[:, qc:qc + 1], tt0[:],
                                 op=ALU.max)
                sV.inc(ve.drain())
            bias_rd[0] = sV.n
            wu[b] = sT.n
            yield

    def qA(qc):
        slot = qc % NSLOT
        base = slot * H
        ve.tensor_scalar(ggvcol[:, qc:qc + 1], ggvcol[:, qc:qc + 1], 1e-5, None,
                         op0=ALU.max)
        ve.drain()
        ve.reciprocal(tt1[:], ggvcol[:, qc:qc + 1])
        ve.drain()
        ve.tensor_scalar(sc_gv[:, qc:qc + 1], tt1[:], 127.0, None, op0=ALU.mult)
        ve.drain()
        ve.tensor_scalar(tt0[:], ggvcol[:, qc:qc + 1], 0.5 / 127.0, None, op0=ALU.mult)
        ve.drain()
        ve.tensor_tensor(sc3[:, qc:qc + 1], tt0[:], gamw[:, 2:3], op=ALU.mult)
        ve.drain()
        ve.wait_ge(sT.h, dsc_war[0])
        ve.tensor_scalar(dsc[:], eye32[:], sc_gv[:, qc:qc + 1], None, op0=ALU.mult)
        dsc_tok = sV.inc(ve.drain())
        te.wait_ge(sV.h, dsc_tok)
        for ch in range(HC):
            p = ch % 2
            pb = ptq[:, p * 512:p * 512 + 128]
            blk = gvb[:, base + ch * 128: base + (ch + 1) * 128]
            te.wait_ge(sA.h, evt6[p])
            sT.inc(te.matmul(pb, blk, dsc[:], start=True, stop=True))
            ac.wait_ge(sT.h, sT.n)
            evt6[p] = sA.inc(ac.activation(blk, pb, ACTF.Identity, bias=c1285[:]))
        dsc_war[0] = sT.n
        slot_free[slot] = {}

    def pair_gen(qc):
        slot = qc % NSLOT
        base = slot * H
        W6 = c.W6
        for k0 in range(0, c.NHW, 4):
            pair = (k0, k0 + 1, k0 + 2, k0 + 3)
            upd = {}
            ac.drain()
            for ki in pair:
                i = ki % 4
                wch = gvb[:, base + ki * W6: base + (ki + 1) * W6]
                uch = ubuf[:, i * W6:(i + 1) * W6]
                if uwar[i]:
                    ac.wait_ge(sV.h, uwar[i])
                sA.inc(ac.activation(uch, wch, ACTF.Identity, bias=cm1275[:]))
                upd[ki] = sgnwar[i]
            ac.drain()
            for j in range(8):
                kpow = -float(1 << (6 - j)) if j <= 6 else -0.5
                for ki in pair:
                    i = ki % 4
                    uch = ubuf[:, i * W6:(i + 1) * W6]
                    sch = sgn[:, i * W6:(i + 1) * W6]
                    if upd[ki]:
                        ac.wait_ge(sV.h, upd[ki])
                    sA.inc(ac.activation(sch, uch, ACTF.Sign, bias=chalf[:, 0:1]))
                    ve.wait_ge(sA.h, sA.n)
                    upd[ki] = sV.inc(ve.scalar_tensor_tensor(
                        uch, sch, kpow, uch, op0=ALU.mult, op1=ALU.add))
            for ki in pair:
                i = ki % 4
                p2 = ki % 2
                sgnwar[i] = upd[ki]
                wch = gvb[:, base + ki * W6: base + (ki + 1) * W6]
                ve.wait_ge(sV.h, upd[ki])
                uwar[i] = sV.inc(ve.scalar_tensor_tensor(
                    wch, ubuf[:, i * W6:(i + 1) * W6], -1.0, wch,
                    op0=ALU.mult, op1=ALU.add))
                ac.wait_ge(sV.h, uwar[i])
                if scr_tok[p2]:
                    ac.wait_ge(sP6[p2].h, scr_tok[p2])
                ct = sA.inc(ac.copy(scr16[:, p2 * W6:(p2 + 1) * W6], wch))
                gp.wait_ge(sA.h, ct)
                scr_tok[p2] = sP6[p2].inc(gp.dma_start(
                    out=x2i_sp[qc][:, ki * W6:(ki + 1) * W6],
                    in_=scr16[:, p2 * W6:(p2 + 1) * W6]), 16)
                slot_free[slot][ki] = (p2, scr_tok[p2])
            yield

    assert c.QG == 2 and HSL % 4 == 0 and c.NHW == HSL
    p6a_pos = [3 * HSL // 16, 5 * HSL // 16, 11 * HSL // 16, 13 * HSL // 16]
    p6aA(0)
    p6aB(0)
    p6aA(1)
    p6aB(1)
    for qg in range(c.NQG):
        pairs = []
        if qg > 0:
            for q in (qg * 2 - 2, qg * 2 - 1):
                qA(q)
                pairs.append(pair_gen(q))
        nxt = [qg * 2 + 2, qg * 2 + 3]
        pg = produce_gen(qg)
        for s in range(HSL):
            if pairs:
                if s % 4 == 0:
                    next(pairs[0], None)
                elif s % 4 == 2:
                    next(pairs[1], None)
            if s == p6a_pos[0] and nxt[0] < QT:
                p6aA(nxt[0])
            if s == p6a_pos[1] and nxt[0] < QT:
                p6aB(nxt[0])
            if s == p6a_pos[2] and nxt[1] < QT:
                p6aA(nxt[1])
            if s == p6a_pos[3] and nxt[1] < QT:
                p6aB(nxt[1])
            next(pg)
        for g_ in pairs:
            for _ in g_:
                pass
    for q in (QT - 2, QT - 1):
        qA(q)
    for _ in zip(pair_gen(QT - 2), pair_gen(QT - 1)):
        pass
    free("gvb", "dsc", "c1285", "cm1275", "scr16", "ubuf", "sgn", "vsl",
         "gsl", "vbb", "gbb", "wvsl", "wgsl",
         "xit", "rs1", "ft", "xiT")
    pfree("ptq", "vps2", "gps2", "pt16")
    barrier()

    wo3 = alloc("wo3", [128, HC * D], FP8)
    xb2 = alloc("xb2", [128, 2 * HC * 128], BF16)
    o3sb = alloc("o3sb", [128, D])
    x1b = alloc("x1b", [128, D])
    z4t = alloc("z4t", [128, D])
    obb = alloc("obb", [128, D])
    csb = alloc("csb", [128, D])
    sL.inc(sy.dma_start(out=obb[:], in_=obbt_hbm[:]), 16)
    NW7 = max(1, HC // 16)
    for k7 in range(NW7):
        w7 = HC * D // NW7
        sL.inc(sy.dma_start(out=wo3[:, k7 * w7:(k7 + 1) * w7],
                            in_=woi_sp[:, k7 * w7:(k7 + 1) * w7]), 16)
    te.wait_ge(sL.h, sL.n)
    cs_ps = palloc("cs_ps", [1, D])
    for hc in range(HC):
        for dsl_ in range(DSL):
            sT.inc(te.matmul(cs_ps[:1, dsl_ * SLAB:(dsl_ + 1) * SLAB], ones16[:],
                             wo3[:, hc * D + dsl_ * SLAB: hc * D + (dsl_ + 1) * SLAB],
                             start=(hc == 0), stop=(hc == HC - 1)))
    ve.wait_ge(sT.h, sT.n)
    ve.tensor_scalar(csb[:1, :], cs_ps[:1, :], 128.0, None, op0=ALU.mult)
    sV.inc(ve.drain())
    pfree("cs_ps")
    bc_ps7 = palloc("bc_ps7", [128, 512])
    te.wait_ge(sV.h, sV.n)
    bcast(csb, csb[:1, :], D, bc_ps7)
    pfree("bc_ps7")
    o3ps = palloc("o3ps", [128, 2048])
    te.wait_ge(sA.h, sA.n)
    xu = [0, 0]
    ev7 = [0, 0]
    ld7 = [0, 0]
    ld7[0] = sX[0].inc(sy.dma_start(out=xb2[:, :HC * 128], in_=x2i_sp[0]), 16)
    for qc in range(QT):
        b7 = qc % 2
        xbb = xb2[:, b7 * HC * 128:(b7 + 1) * HC * 128]
        sy.wait_ge(sV.h, sV.n)
        lx7 = sK.inc(sy.dma_start(out=x1b[:], in_=x1_sp[qc * 128:(qc + 1) * 128, :]), 16)
        if qc + 1 < QT:
            b2 = (qc + 1) % 2
            sy.wait_ge(sT.h, xu[b2])
            ld7[b2] = sX[b2].inc(sy.dma_start(
                out=xb2[:, b2 * HC * 128:(b2 + 1) * HC * 128], in_=x2i_sp[qc + 1]), 16)
        te.wait_ge(sX[b7].h, ld7[b7])
        for dg in range((DSL + 1) // 2):
            dss = range(2 * dg, min(2 * dg + 2, DSL))
            te.wait_ge(sV.h, ev7[dg % 2])
            for hc in range(HC):
                for ds in dss:
                    sT.inc(te.matmul(o3ps[:, ds * SLAB:(ds + 1) * SLAB],
                                     xbb[:, hc * 128:(hc + 1) * 128],
                                     wo3[:, hc * D + ds * SLAB: hc * D + (ds + 1) * SLAB],
                                     start=(hc == 0), stop=(hc == HC - 1)))
            xu[b7] = sT.n
            ve.wait_ge(sT.h, sT.n)
            ve.wait_ge(sA.h, sA.n)
            for ds in dss:
                ve.tensor_tensor(o3sb[:, ds * SLAB:(ds + 1) * SLAB],
                                 o3ps[:, ds * SLAB:(ds + 1) * SLAB],
                                 csb[:, ds * SLAB:(ds + 1) * SLAB], op=ALU.subtract)
                ve.drain()
                ve.scalar_tensor_tensor(o3sb[:, ds * SLAB:(ds + 1) * SLAB],
                                        o3sb[:, ds * SLAB:(ds + 1) * SLAB],
                                        sc3[:, qc:qc + 1], obb[:, ds * SLAB:(ds + 1) * SLAB],
                                        op0=ALU.mult, op1=ALU.add)
                ve.drain()
            ev7[dg % 2] = sV.inc(ve.drain())
        ln_stats(o3sb[:], D)
        ln_rstd()
        ac.wait_ge(sO.h, sO.n)
        ln_apply(z4t[:], o3sb[:])
        ve.wait_ge(sA.h, sA.n)
        ve.wait_ge(sK.h, lx7)
        ve.tensor_tensor(z4t[:], z4t[:], x1b[:], op=ALU.add)
        sV.inc(ve.drain())
        sy.wait_ge(sV.h, sV.n)
        sO.inc(sy.dma_start(out=out_hbm[qc * 128:(qc + 1) * 128, :], in_=z4t[:]), 16)
    sy.wait_ge(sO.h, sO.n)
    barrier()

    for cm in reversed(list(live.values())):
        cm.__exit__(None, None, None)
    for cm in reversed(list(psum_live.values())):
        cm.__exit__(None, None, None)
    for cm in reversed(sem_cms):
        cm.__exit__(None, None, None)
    return nc


def _bf16():
    import ml_dtypes
    return ml_dtypes.bfloat16


def reshape_blocked(w, out_blk=128):
    Dd, N = w.shape
    dcn = Dd // 128
    nb = N // out_blk
    return np.ascontiguousarray(
        w.reshape(dcn, 128, nb, out_blk).transpose(1, 2, 0, 3).reshape(128, -1))


def make_core_inputs(cfg, x_b, parity, shared):
    own = np.ascontiguousarray(x_b[parity::2])
    peer = np.ascontiguousarray(x_b[1 - parity::2])
    xr = np.concatenate([own, peer], 0)
    ii = np.arange(128)[:, None]
    jj = np.arange(128)[None, :]
    mask_own = np.where(ii <= jj, 0.0, NEG_MASK).astype(np.float32)
    if parity == 0:
        mask_peer = np.where(ii < jj, 0.0, NEG_MASK).astype(np.float32)
    else:
        mask_peer = np.where(ii <= jj, 0.0, NEG_MASK).astype(np.float32)
    d = {"x": xr, "mask_own": mask_own, "mask_peer": mask_peer}
    d.update(shared)
    return d


def kernel(L0_attn_x, L0_attn_qkv, L0_attn_o_proj, L0_attn_gamma, L0_attn_beta,
           L0_ffn_gate_w, L0_ffn_gate_b, L0_ffn_val_w, L0_ffn_val_b,
           L0_ffn_out_w, L0_ffn_out_b):
    cfg = Cfg(D=2048, S=2048)
    x = np.asarray(L0_attn_x, dtype=np.float32)
    qkv_f = np.asarray(L0_attn_qkv, dtype=np.float32)
    wo_f = np.asarray(L0_attn_o_proj, dtype=np.float32)
    shared = {
        "qkv": np.ascontiguousarray(qkv_f),
        "qkvc": reshape_blocked(qkv_f, 128),
        "wob": reshape_blocked(wo_f, 512),
        "gamma": np.asarray(L0_attn_gamma, dtype=np.float32),
        "beta": np.asarray(L0_attn_beta, dtype=np.float32),
        "gwT": np.ascontiguousarray(np.asarray(L0_ffn_gate_w, dtype=np.float32).T),
        "vwT": np.ascontiguousarray(np.asarray(L0_ffn_val_w, dtype=np.float32).T),
        "owT": np.ascontiguousarray(np.asarray(L0_ffn_out_w, dtype=np.float32).T),
        "gate_b": np.asarray(L0_ffn_gate_b, dtype=np.float32),
        "gbbt": np.ascontiguousarray(np.broadcast_to(
            np.asarray(L0_ffn_gate_b, dtype=np.float32), (128, len(L0_ffn_gate_b)))),
        "vbbt": np.ascontiguousarray(np.broadcast_to(
            np.asarray(L0_ffn_val_b, dtype=np.float32), (128, len(L0_ffn_val_b)))),
        "obbt": np.ascontiguousarray(np.broadcast_to(
            np.asarray(L0_ffn_out_b, dtype=np.float32), (128, len(L0_ffn_out_b)))),
        "val_b": np.asarray(L0_ffn_val_b, dtype=np.float32),
        "out_b": np.asarray(L0_ffn_out_b, dtype=np.float32),
        "eye32": np.eye(128, dtype=np.float32),
        "eye16": np.eye(128, dtype=np.float32).astype(_bf16()),
    }
    nc = build_program(cfg)
    in_maps = []
    for core in range(8):
        b, p = divmod(core, 2)
        in_maps.append(make_core_inputs(cfg, x[b], p, shared))
    res = run_bass_kernel_spmd(nc, in_maps, core_ids=list(range(8)))
    out = np.empty((4, cfg.S, cfg.D), dtype=np.float32)
    for core in range(8):
        b, p = divmod(core, 2)
        out[b, p::2] = res.results[core]["out"]
    return out


# version 31
# speedup vs baseline: 1.1467x; 1.0036x over previous
import math
import numpy as np

import concourse.bass as bass
import concourse.mybir as mybir
from concourse.bass_utils import run_bass_kernel_spmd

F32 = mybir.dt.float32
BF16 = mybir.dt.bfloat16
FP8 = mybir.dt.float8e4
AX = mybir.AxisListType
ALU = mybir.AluOpType
ACTF = mybir.ActivationFunctionType

NEG_MASK = -1.0e33
EPS = 1.0e-5


class Cfg:
    def __init__(self, D=2048, S=2048, H=None):
        self.D = D
        self.S = S
        self.H = H if H is not None else 4 * D
        self.SO = S // 2
        self.DC = D // 128
        self.QT = self.SO // 128
        self.KT = S // 128
        self.HC = self.H // 128
        self.SLAB = min(512, self.H, D, self.SO)
        self.DSL = D // self.SLAB
        self.HSL = self.H // self.SLAB
        self.SSL = S // self.SLAB
        self.OSL = self.SO // self.SLAB
        self.QG = min(2, self.QT)
        self.NQG = self.QT // self.QG
        self.NH4 = self.H // 512
        self.W6 = 512
        self.NHW = self.H // self.W6
        self.scale = 1.0 / math.sqrt(float(D))
        self.debug = False
        assert self.SLAB == 512, "psum accumulation slabs must be bank-aligned"
        assert D % 128 == 0 and self.SO % 128 == 0 and self.H % self.SLAB == 0
        assert D % self.SLAB == 0 and self.SO % self.SLAB == 0
        assert self.NH4 % 2 == 0 and self.QT % self.QG == 0
        assert self.NHW % 4 == 0 and self.W6 % 512 == 0


class Sem:
    def __init__(self, h):
        self.h = h
        self.n = 0

    def inc(self, instr, k=1):
        instr.then_inc(self.h, k)
        self.n += k
        return self.n


def build_program(c: Cfg):
    nc = bass.Bass("TRN2", target_bir_lowering=False)
    D, S, H, SO = c.D, c.S, c.H, c.SO
    DC, QT, KT, HC = c.DC, c.QT, c.KT, c.HC
    SLAB, DSL, HSL, OSL = c.SLAB, c.DSL, c.HSL, c.OSL

    din = lambda n, sh, dt=F32: nc.dram_tensor(n, sh, dt, kind="ExternalInput").ap()
    x_hbm = din("x", [S, D])
    qkv_hbm = din("qkv", [D, 3 * D])
    qkvc_hbm = din("qkvc", [128, (3 * D // 128) * DC * 128])
    wob_hbm = din("wob", [128, DSL * DC * SLAB])
    gamma_hbm = din("gamma", [D])
    beta_hbm = din("beta", [D])
    gwT_hbm = din("gwT", [D, H])
    vwT_hbm = din("vwT", [D, H])
    owT_hbm = din("owT", [H, D])
    gb_hbm = din("gate_b", [H])
    vb_hbm = din("val_b", [H])
    ob_hbm = din("out_b", [D])
    gbbt_hbm = din("gbbt", [128, H])
    vbbt_hbm = din("vbbt", [128, H])
    obbt_hbm = din("obbt", [128, D])
    mko_hbm = din("mask_own", [128, 128])
    mkp_hbm = din("mask_peer", [128, 128])
    eye32_hbm = din("eye32", [128, 128])
    eye16_hbm = din("eye16", [128, 128], BF16)
    out_hbm = nc.dram_tensor("out", [SO, D], F32, kind="ExternalOutput").ap()

    dint = lambda n, sh, dt=F32: nc.dram_tensor(n, sh, dt, kind="Internal").ap()
    hn_sp = dint("hn_sp", [KT, 128, DC * 128])
    k_sp = dint("k_sp", [128, KT, DC * 128])
    v_sp = dint("v_sp", [128, DC, KT * 128])
    q_sp = dint("q_sp", [DC, 128, SO])
    h_sp = dint("h_sp", [SO, D])
    x1_sp = dint("x1_sp", [SO, D])
    wgi_sp = dint("wgi_sp", [HSL, 128, DC * SLAB], FP8)
    wvi_sp = dint("wvi_sp", [HSL, 128, DC * SLAB], FP8)
    woi_sp = dint("woi_sp", [128, HC * D], FP8)
    x2i_sp = dint("x2i_sp", [QT, 128, H], BF16)
    if c.debug:
        qa_l = [SO - 128 * (m % QT) for m in range(KT)]
        dbg_ptp = dint("dbg_ptp", [128, int(sum(qa_l))])
        dbg_l = dint("dbg_l", [128, QT])
        dbg_negM = dint("dbg_negM", [128, SO])
        dbg_ofm = dint("dbg_ofm", [128, DC * SO])
        dbg_ao = dint("dbg_ao", [128, QT * D])

    te, ve, ac, gp, sy = nc.tensor, nc.vector, nc.scalar, nc.gpsimd, nc.sync

    sem_cms = []

    def mksem(name):
        cm = nc.semaphore(name)
        h = cm.__enter__()
        sem_cms.append(cm)
        return Sem(h)

    sL = mksem("sL")
    sX = [mksem("sX0"), mksem("sX1")]
    sW = [mksem("sW0"), mksem("sW1")]
    sH = [mksem("sH0"), mksem("sH1")]
    sU = [mksem("sU0"), mksem("sU1")]
    sK = mksem("sK")
    sWS = mksem("sWS")
    sQW = [mksem("sQW0"), mksem("sQW1"), mksem("sQW2"), mksem("sQW3")]
    sQZ = [mksem("sQZ0"), mksem("sQZ1"), mksem("sQZ2"), mksem("sQZ3")]
    sA5 = mksem("sA5")
    sV5 = mksem("sV5")
    sS = mksem("sS")
    sP = mksem("sP")
    sP6 = [mksem("sP6a"), mksem("sP6b")]
    sKV = mksem("sKV")
    sT = mksem("sT")
    sV = mksem("sV")
    sA = mksem("sA")
    sG = mksem("sG")
    sO = mksem("sO")
    sB = mksem("sB")

    def barrier(full=True):
        drains = [sS, sL, sO, sK, sWS, sKV, sX[0], sX[1], sW[0], sW[1],
                  sH[0], sH[1], sU[0], sU[1]]
        if full:
            drains += [sP] + sP6 + sQW + sQZ
        for s_ in drains:
            sy.wait_ge(s_.h, s_.n)
        gp.wait_ge(sG.h, sG.n)
        engs = [te, ve, ac, gp, sy]
        for e in engs:
            sB.inc(e.drain() if e is not sy else e.nop())
        for e in engs:
            e.wait_ge(sB.h, sB.n)

    live = {}

    def alloc(name, sh, dt=F32):
        cm = nc.sbuf_tensor(name, sh, dt)
        h = cm.__enter__()
        live[name] = cm
        return h

    def alloc_r(name, sh, dt=F32):
        cm = nc.sbuf_tensor(name, sh, dt, side="right")
        h = cm.__enter__()
        live[name] = cm
        return h

    def free(*names):
        for n in names:
            live.pop(n).__exit__(None, None, None)

    psum_live = {}

    def palloc(name, sh, dt=F32):
        cm = nc.psum_tensor(name, sh, dt)
        h = cm.__enter__()
        psum_live[name] = cm
        return h

    def pfree(*names):
        for n in names:
            psum_live.pop(n).__exit__(None, None, None)

    ones = alloc("ones", [128, 1])
    ones16 = alloc("ones16", [128, 1], BF16)
    ones_row = alloc("ones_row", [1, 128])
    eye32 = alloc("eye32s", [128, 128])
    eye16 = alloc("eye16s", [128, 128], BF16)
    mask_own = alloc("mask_own_s", [128, 128])
    mask_peer = alloc("mask_peer_s", [128, 128])
    nstat = 6 * ((D + 511) // 512)
    stat = alloc("stat", [128, nstat])
    agg = alloc("agg", [128, 2])
    vep = alloc("vep", [128, 1])
    rr0 = alloc("rr0", [128, 1])
    tt0 = alloc("tt0", [128, 1])
    tt1 = alloc("tt1", [128, 1])
    recip_l = alloc("recip_l", [128, QT])
    gxcol = alloc("gxcol", [128, QT])
    ggvcol = alloc("ggvcol", [128, QT])
    sc_x = alloc("sc_x", [128, QT])
    sc_gv = alloc("sc_gv", [128, QT])
    scg_t = alloc("scg_t", [128, QT])
    scv_t = alloc("scv_t", [128, QT])
    sc3 = alloc("sc3", [128, QT])
    gwpart = alloc("gwpart", [128, 2])
    chalf = alloc("chalf", [128, 2])
    gwsum = alloc("gwsum", [128, 3])
    gwtot_sb = alloc("gwtot_sb", [1, 3])
    gwrec_sb = alloc("gwrec_sb", [1, 3])
    rgam = alloc("rgam", [128, 3])
    gamw = alloc("gamw", [128, 3])
    negM = alloc("negM", [128, SO])
    qn_row = alloc("qn_row", [1, SO])
    maxkn = alloc("maxkn", [1, 1])
    negMr = alloc("negMr", [1, SO])

    def ln_stats(x_ap, nfree):
        nchunks = (nfree + 511) // 512
        for i in range(nchunks):
            lo = i * 512
            hi = min(nfree, lo + 512)
            ve.bn_stats(stat[:, 6 * i:6 * (i + 1)], x_ap[:, lo:hi])
        ve.drain()
        ve.bn_aggr(agg[:], stat[:, :6 * nchunks])
        ve.drain()

    def ln_rstd():
        ve.tensor_scalar(vep[:], agg[:, 1:2], EPS, None, op0=ALU.add)
        sV.inc(ve.drain())
        ac.wait_ge(sV.h, sV.n)
        sA.inc(ac.activation(tt0[:], vep[:], ACTF.Sqrt))
        ve.wait_ge(sA.h, sA.n)
        ve.reciprocal(rr0[:], tt0[:])
        for _ in range(2):
            ve.drain()
            ve.tensor_tensor(tt0[:], rr0[:], rr0[:], op=ALU.mult)
            ve.drain()
            ve.tensor_tensor(tt0[:], tt0[:], vep[:], op=ALU.mult)
            ve.drain()
            ve.tensor_scalar(tt0[:], tt0[:], -0.5, 1.5, op0=ALU.mult, op1=ALU.add)
            ve.drain()
            ve.tensor_tensor(rr0[:], rr0[:], tt0[:], op=ALU.mult)
        ve.drain()
        ve.tensor_scalar(tt1[:], agg[:, 0:1], -1.0, rr0[:], op0=ALU.mult, op1=ALU.mult)
        sV.inc(ve.drain())

    def ln_apply(dst, src):
        ac.wait_ge(sV.h, sV.n)
        ac.drain()
        sA.inc(ac.activation(dst, src, ACTF.Identity, bias=tt1[:], scale=rr0[:]))

    def bcast(dst, src_row, n, ps, war_cell=None):
        for lo in range(0, n, 512):
            w = min(512, n - lo)
            te.wait_ge(sA.h, sA.n if war_cell is None else war_cell[0])
            sT.inc(te.matmul(ps[:, :w], ones_row[:1, :], src_row[:1, lo:lo + w],
                             start=True, stop=True))
            ac.wait_ge(sT.h, sT.n)
            sA.inc(ac.copy(dst[:, lo:lo + w], ps[:, :w]))
            if war_cell is not None:
                war_cell[0] = sA.n
        return sT.n

    def round_ladder(f_ap, sc_ap, out_ap, w_scr, u_scr, p_scr):
        ve.tensor_scalar(w_scr, f_ap, sc_ap, 128.5, op0=ALU.mult, op1=ALU.add)
        ve.drain()
        ve.tensor_copy(u_scr, w_scr)
        ve.drain()
        for j in range(7, -1, -1):
            k = float(1 << j)
            ve.tensor_scalar(p_scr, u_scr, k, None, op0=ALU.is_ge)
            ve.drain()
            ve.scalar_tensor_tensor(u_scr, p_scr, -k, u_scr, op0=ALU.mult, op1=ALU.add)
            ve.drain()
        ve.scalar_tensor_tensor(p_scr, u_scr, -1.0, w_scr, op0=ALU.mult, op1=ALU.add)
        ve.drain()
        ve.tensor_scalar(out_ap, p_scr, -128.0, None, op0=ALU.add)
        ve.drain()

    WCW = 512
    wqb = alloc_r("wqb", [128, 4 * WCW])
    wpb = alloc_r("wpb", [128, 2 * WCW])
    wmb = alloc_r("wmb", [128, 2 * WCW])
    wib = alloc_r("wib", [128, 2 * WCW], FP8)
    NCG = H // WCW

    def wchunks(src_hbm, mat_sp):
        out = []
        for dc in range(DC):
            for cg in range(NCG):
                def store(wib_, sem, dc=dc, cg=cg):
                    s_lo = (cg * WCW) // SLAB
                    s_hi = ((cg + 1) * WCW) // SLAB
                    for s in range(s_lo, s_hi):
                        off = s * SLAB - cg * WCW
                        sem.inc(gp.dma_start(
                            out=mat_sp[s][:, dc * SLAB:(dc + 1) * SLAB],
                            in_=wib_[:, off:off + SLAB]), 16)
                out.append((src_hbm[dc * 128:(dc + 1) * 128, cg * WCW:(cg + 1) * WCW],
                            WCW, store))
        return out

    def owt_chunks():
        out = []
        for ch in range(HC):
            for hf in range(D // WCW):
                def store(wib_, sem, ch=ch, hf=hf):
                    sem.inc(gp.dma_start(
                        out=woi_sp[:, ch * D + hf * WCW: ch * D + (hf + 1) * WCW],
                        in_=wib_[:, :WCW]), 16)
                out.append((owT_hbm[ch * 128:(ch + 1) * 128,
                                    hf * WCW:(hf + 1) * WCW], WCW, store))
        return out

    wq_us = [0, 0, 0, 0]

    class Pump:
        DEPTH = 4

        def __init__(self, stages):
            self.items = []
            self.nslot = 0
            for compute, idx, chunks in stages:
                if callable(chunks):
                    self.items.append((compute, idx, False, None, None))
                    continue
                for ci, ck in enumerate(chunks):
                    self.items.append((compute, idx, ci == 0, ck, self.nslot))
                    self.nslot += 1
            self.n = len(self.items)
            self.cursor = 0
            self.loaded = 0
            self.lds = {}

        def _load(self):
            i = self.loaded
            item = self.items[i]
            if item[3] is not None:
                compute, idx, first, (src_, ncols, store_fn), sl = item
                b = sl % 4
                if compute is gamma_compute:
                    qeng, sem = ac, sQW[b]
                else:
                    qeng, sem = gp, sQZ[b]
                qeng.wait_ge(sA5.h, wq_us[b])
                tok = sem.inc(
                    qeng.dma_start(out=wqb[:, b * WCW:b * WCW + ncols], in_=src_), 16)
                self.lds[i] = (sem, tok)
            self.loaded += 1

        def pump(self, k):
            while k > 0 and self.cursor < self.n:
                while self.loaded < min(self.cursor + self.DEPTH, self.n):
                    self._load()
                compute, idx, first, ck, sl = self.items[self.cursor]
                if ck is None:
                    compute(idx)
                else:
                    self.compute_chunk(compute, self.cursor, idx, first, ck, sl)
                self.cursor += 1
                k -= 1

        def compute_chunk(self, compute, i, idx, first, ck, sl):
            src_, ncols, store_fn = ck
            compute(self, i, idx, first, ncols, store_fn, sl)

        def drain(self):
            self.pump(self.n)

    def gamma_compute(pp, i, idx, first, ncols, store_fn, sl):
        b = sl % 4
        wb_ = wqb[:, b * WCW:b * WCW + ncols]
        if first:
            ve.wait_ge(sA5.h, sA5.n)
            ve.memset(gwsum[:, idx:idx + 1], 0.0)
            sV5.inc(ve.drain())
            ac.wait_ge(sV5.h, sV5.n)
        if not hasattr(pp, "gv_tok"):
            pp.gv_tok = [0, 0]
        h2 = sl % 2
        ldsem, ldtok = pp.lds[i]
        ac.wait_ge(ldsem.h, ldtok)
        if pp.gv_tok[h2]:
            ac.wait_ge(sV5.h, pp.gv_tok[h2])
        if getattr(pp, "sg_tok", None) and pp.sg_tok[h2]:
            ac.wait_ge(sG.h, pp.sg_tok[h2])
        ac.drain()
        sA5.inc(ac.activation(wpb[:, h2 * WCW:h2 * WCW + ncols], wb_,
                              ACTF.Abs, accum_out=gwpart[:, h2:h2 + 1]))
        wq_us[b] = sA5.n
        ve.wait_ge(sA5.h, sA5.n)
        ve.tensor_tensor(gwsum[:, idx:idx + 1], gwsum[:, idx:idx + 1],
                         gwpart[:, h2:h2 + 1], op=ALU.add)
        pp.gv_tok[h2] = sV5.inc(ve.drain())

    def quant_compute(pp, i, idx, first, ncols, store_fn, sl):
        b = sl % 4
        h2 = sl % 2
        wb_ = wqb[:, b * WCW:b * WCW + ncols]
        wpb_ = wpb[:, h2 * WCW:h2 * WCW + ncols]
        wmb_ = wmb[:, h2 * WCW:h2 * WCW + ncols]
        wib_ = wib[:, h2 * WCW:h2 * WCW + ncols]
        if not hasattr(pp, "sg_tok"):
            pp.sg_tok = [0, 0]
            pp.st_tok = [0, 0]
        ldsem, ldtok = pp.lds[i]
        ac.wait_ge(ldsem.h, ldtok)
        if pp.sg_tok[h2]:
            ac.wait_ge(sG.h, pp.sg_tok[h2])
        sA5.inc(ac.activation(wpb_, wb_, ACTF.Sign,
                              bias=chalf[:, 0:1], scale=rgam[:, idx:idx + 1]))
        sA5.inc(ac.activation(wmb_, wb_, ACTF.Sign,
                              bias=chalf[:, 1:2], scale=rgam[:, idx:idx + 1]))
        wq_us[b] = sA5.n
        gp.wait_ge(sA5.h, sA5.n)
        if pp.st_tok[h2]:
            gp.wait_ge(sP6[h2].h, pp.st_tok[h2])
        pp.sg_tok[h2] = sG.inc(gp.tensor_tensor(wib_, wpb_, wmb_, op=ALU.add))
        gp.wait_ge(sG.h, pp.sg_tok[h2])
        store_fn(wib[:, h2 * WCW:(h2 + 1) * WCW], sP6[h2])
        pp.st_tok[h2] = sP6[h2].n

    def wquant_bcast(idx):
        n_elems = D * H if idx < 2 else H * D
        te.wait_ge(sV5.h, sV5.n)
        gps = palloc("gwps" + str(idx), [1, 1])
        mm_tok = sT.inc(te.matmul(gps[:1, :1], gwsum[:, idx:idx + 1], ones[:],
                                  start=True, stop=True))
        ve.wait_ge(sT.h, mm_tok)
        ve.tensor_scalar(gwtot_sb[:1, idx:idx + 1], gps[:1, :1], 1.0 / float(n_elems),
                         1e-5, op0=ALU.mult, op1=ALU.max)
        ve.drain()
        ve.reciprocal(gwrec_sb[:1, idx:idx + 1], gwtot_sb[:1, idx:idx + 1])
        sV5.inc(ve.drain())
        pfree("gwps" + str(idx))
        bc_psw = palloc("bc_psw" + str(idx), [128, 512])
        te.wait_ge(sV5.h, sV5.n)
        cell = [0]
        bcast(rgam[:, idx:idx + 1], gwrec_sb[:1, idx:idx + 1], 1, bc_psw,
              war_cell=cell)
        bcast(gamw[:, idx:idx + 1], gwtot_sb[:1, idx:idx + 1], 1, bc_psw,
              war_cell=cell)
        ac.drain()
        pfree("bc_psw" + str(idx))

    wpump = Pump([
        (gamma_compute, 0, wchunks(gwT_hbm, wgi_sp)),
        (wquant_bcast, 0, wquant_bcast),
        (gamma_compute, 1, wchunks(vwT_hbm, wvi_sp)),
        (wquant_bcast, 1, wquant_bcast),
        (gamma_compute, 2, owt_chunks()),
        (wquant_bcast, 2, wquant_bcast),
        (quant_compute, 0, wchunks(gwT_hbm, wgi_sp)),
        (quant_compute, 1, wchunks(vwT_hbm, wvi_sp)),
        (quant_compute, 2, owt_chunks()),
    ])

    gam_b = alloc("gam_b", [128, D])
    bet_b = alloc("bet_b", [128, D])
    sL.inc(sy.dma_start(out=eye32[:], in_=eye32_hbm), 16)
    sL.inc(sy.dma_start(out=eye16[:], in_=eye16_hbm), 16)
    sL.inc(sy.dma_start(out=mask_own[:], in_=mko_hbm), 16)
    sL.inc(sy.dma_start(out=mask_peer[:], in_=mkp_hbm), 16)
    sL.inc(sy.dma_start(out=gam_b[:1, :], in_=gamma_hbm[None, :]), 16)
    sL.inc(sy.dma_start(out=bet_b[:1, :], in_=beta_hbm[None, :]), 16)
    ve.memset(ones[:], 1.0)
    ve.memset(ones16[:], 1.0)
    ve.memset(ones_row[:], 1.0)
    ve.memset(chalf[:, 0:1], -0.5)
    ve.memset(chalf[:, 1:2], 0.5)
    sV.inc(ve.drain())
    bc_ps0 = palloc("bc_ps0", [128, 512])
    te.wait_ge(sL.h, sL.n)
    te.wait_ge(sV.h, sV.n)
    bcast(gam_b, gam_b[:1, :], D, bc_ps0)
    bcast(bet_b, bet_b[:1, :], D, bc_ps0)
    pfree("bc_ps0")

    hnf = alloc("hnf", [128, DC * S])
    xt = alloc("xt", [128, 2 * D])
    ptr = palloc("ptr", [128, 1024])
    ve.wait_ge(sA.h, sA.n)
    evac_tok = [0, 0]
    tile_done = [0, 0]
    for i in range(KT):
        b = i % 2
        xb = xt[:, b * D:b * D + D]
        sy.wait_ge(sT.h, tile_done[b])
        ld = sX[b].inc(sy.dma_start(out=xb, in_=x_hbm[i * 128:(i + 1) * 128, :]), 16)
        ve.wait_ge(sX[b].h, ld)
        ln_stats(xb, D)
        ln_rstd()
        ln_apply(xb, xb)
        if i < QT:
            sy.wait_ge(sA.h, sA.n)
            sS.inc(sy.dma_start(out=h_sp[i * 128:(i + 1) * 128, :], in_=xb), 16)
        ve.wait_ge(sA.h, sA.n)
        ln_stats(xb, D)
        ln_rstd()
        ac.wait_ge(sS.h, sS.n)
        ln_apply(xb, xb)
        ve.wait_ge(sA.h, sA.n)
        ve.tensor_tensor(xb, xb, gam_b[:], op=ALU.mult)
        ve.drain()
        ve.tensor_tensor(xb, xb, bet_b[:], op=ALU.add)
        sV.inc(ve.drain())
        te.wait_ge(sV.h, sV.n)
        for dc in range(DC):
            pb = ptr[:, (dc % 2) * 512:(dc % 2) * 512 + 128]
            te.wait_ge(sA.h, evac_tok[dc % 2])
            sT.inc(te.transpose(pb, xb[:, dc * 128:(dc + 1) * 128], eye32[:]))
            ac.wait_ge(sT.h, sT.n)
            dst = hnf[:, dc * S + i * 128: dc * S + i * 128 + 128]
            evac_tok[dc % 2] = sA.inc(ac.copy(dst, pb))
            gp.wait_ge(sA.h, sA.n)
            sKV.inc(gp.dma_start(out=hn_sp[i][:, dc * 128:(dc + 1) * 128],
                                 in_=dst), 16)
        tile_done[b] = sT.n
    free("xt")
    pfree("ptr")
    barrier()

    wb = alloc("wb", [128, 2 * DC * 128])
    ksb = alloc("ksb", [128, S])
    ksq = alloc("ksq", [128, SLAB])
    nsb = alloc("nsb", [1, S], BF16)
    kp = palloc("kp", [128, S])
    npp = palloc("npsum", [1, 2 * SLAB])

    def proj(base_col, ntok, store, pump_hook=lambda: None):
        NG = ntok // SLAB
        wtok = [0, 0]
        copy_tok = [0]
        for o in range(DC):
            ob = base_col // 128 + o
            bb = o % 2
            wbo = wb[:, bb * DC * 128:(bb + 1) * DC * 128]
            sy.wait_ge(sT.h, wtok[bb])
            ld = sW[bb].inc(sy.dma_start(
                out=wbo, in_=qkvc_hbm[:, ob * DC * 128:(ob + 1) * DC * 128]), 16)
            te.wait_ge(sW[bb].h, ld)
            te.wait_ge(sV.h, copy_tok[0])
            for dc in range(DC):
                wbb = wbo[:, dc * 128:(dc + 1) * 128]
                for g in range(NG):
                    hsrc = hnf[:, dc * S + g * SLAB: dc * S + (g + 1) * SLAB]
                    sT.inc(te.matmul(kp[:, g * SLAB:(g + 1) * SLAB], wbb, hsrc,
                                     start=(dc == 0), stop=(dc == DC - 1)))
            wtok[bb] = sT.n
            ve.wait_ge(sT.h, sT.n)
            ve.wait_ge(sS.h, sS.n)
            ve.wait_ge(sKV.h, sKV.n)
            ve.tensor_copy(ksb[:, :ntok], kp[:, :ntok])
            copy_tok[0] = sV.inc(ve.drain())
            sy.wait_ge(sV.h, sV.n)
            gp.wait_ge(sV.h, sV.n)
            store(o)
            if o == 0:
                ve.memset(nsb[:1, :ntok], 0.0)
                sV.inc(ve.drain())
            for g in range(NG):
                gb_ = g % 2
                ve.wait_ge(sT.h, sT.n)
                ve.tensor_tensor(ksq[:], ksb[:, g * SLAB:(g + 1) * SLAB],
                                 ksb[:, g * SLAB:(g + 1) * SLAB], op=ALU.mult)
                sV.inc(ve.drain())
                te.wait_ge(sV.h, sV.n)
                nmm = sT.inc(te.matmul(npp[:1, gb_ * SLAB:(gb_ + 1) * SLAB],
                                       ones[:], ksq[:], start=True, stop=True))
                ve.wait_ge(sT.h, nmm)
                ve.tensor_tensor(nsb[:1, g * SLAB:(g + 1) * SLAB],
                                 nsb[:1, g * SLAB:(g + 1) * SLAB],
                                 npp[:1, gb_ * SLAB:(gb_ + 1) * SLAB], op=ALU.add)
                sV.inc(ve.drain())
            pump_hook()

    def store_k(o):
        sKV.inc(gp.dma_start(out=k_sp[:, :, o * 128:(o + 1) * 128],
                             in_=ksb[:, :S]), 16)

    proj(D, S, store_k, lambda: wpump.pump(25))
    ve.wait_ge(sT.h, sT.n)
    ve.tensor_reduce(maxkn[:], nsb[:1, :S], axis=AX.X, op=ALU.max)
    ve.drain()
    ve.tensor_scalar(maxkn[:], maxkn[:], 1e-30, None, op0=ALU.max)
    sV.inc(ve.drain())

    def store_q(o):
        sS.inc(sy.dma_start(out=q_sp[o], in_=ksb[:, :SO]), 16)

    proj(0, SO, store_q, lambda: wpump.pump(25))
    ve.wait_ge(sT.h, sT.n)
    ve.tensor_copy(qn_row[:], nsb[:1, :SO])
    ve.drain()
    ve.tensor_scalar(qn_row[:], qn_row[:], 1e-30, None, op0=ALU.max)
    sV.inc(ve.drain())
    ac.wait_ge(sV.h, sV.n)
    sA.inc(ac.activation(qn_row[:], qn_row[:], ACTF.Sqrt))
    sA.inc(ac.activation(maxkn[:], maxkn[:], ACTF.Sqrt))
    ve.wait_ge(sA.h, sA.n)
    ve.tensor_scalar(negMr[:], qn_row[:], maxkn[:1, :1], -1.0, op0=ALU.mult, op1=ALU.mult)
    sV.inc(ve.drain())
    pfree("npsum", "kp")
    free("nsb")
    bc_ps1 = palloc("bc_ps1", [128, 512])
    te.wait_ge(sV.h, sV.n)
    bcast(negM, negMr, SO, bc_ps1)
    pfree("bc_ps1")
    free("ksq", "ksb", "wb")
    free("hnf", "bet_b", "gam_b")
    free("negMr", "maxkn", "qn_row")
    b1_t, b1_v, b1_a = sT.n, sV.n, sA.n

    wv_sb = alloc("wv_sb", [128, DC * D])
    hbuf = alloc("hbuf", [128, 2 * DC * 128])
    vsb = alloc("vsb", [128, D])
    vp = palloc("vp", [128, 1024])
    sy.wait_ge(sT.h, b1_t)
    sy.wait_ge(sV.h, b1_v)
    ac.wait_ge(sT.h, b1_t)
    ac.wait_ge(sV.h, b1_v)
    for dc in range(DC):
        if dc % 2 == 0:
            sL.inc(sy.dma_start(out=wv_sb[:, dc * D:(dc + 1) * D],
                                in_=qkv_hbm[dc * 128:(dc + 1) * 128, 2 * D:3 * D]), 16)
        else:
            sWS.inc(ac.dma_start(out=wv_sb[:, dc * D:(dc + 1) * D],
                                 in_=qkv_hbm[dc * 128:(dc + 1) * 128, 2 * D:3 * D]), 16)
    te.wait_ge(sL.h, sL.n)
    te.wait_ge(sWS.h, sWS.n)
    te.wait_ge(sA.h, b1_a)
    te.wait_ge(sV.h, b1_v)
    huse = [0, 0]
    vevac = [0, 0]
    for m in range(KT):
        b = m % 2
        hb = hbuf[:, b * DC * 128:(b + 1) * DC * 128]
        sy.wait_ge(sT.h, huse[b])
        sH[b].inc(sy.dma_start(out=hb, in_=hn_sp[m]), 16)
        te.wait_ge(sH[b].h, sH[b].n)
        if m > 0:
            te.wait_ge(sKV.h, sKV.n)
        for ds in range(DSL):
            pbk = vp[:, (ds % 2) * 512:(ds % 2) * 512 + SLAB]
            te.wait_ge(sV.h, vevac[ds % 2])
            for dc in range(DC):
                sT.inc(te.matmul(pbk, hb[:, dc * 128:(dc + 1) * 128],
                                 wv_sb[:, dc * D + ds * SLAB: dc * D + (ds + 1) * SLAB],
                                 start=(dc == 0), stop=(dc == DC - 1)))
            ve.wait_ge(sT.h, sT.n)
            ve.tensor_copy(vsb[:, ds * SLAB:(ds + 1) * SLAB], pbk)
            vevac[ds % 2] = sV.inc(ve.drain())
        huse[b] = sT.n
        gp.wait_ge(sV.h, sV.n)
        sKV.inc(gp.dma_start(out=v_sp[:, :, m * 128:(m + 1) * 128], in_=vsb[:]), 16)
        wpump.pump(24)
    free("vsb", "hbuf", "wv_sb")
    pfree("vp")
    b2_t, b2_v = sT.n, sV.n

    ofm = alloc("ofm", [128, DC * SO])
    _qa_list = [SO - 128 * (m % QT) for m in range(KT)]
    ptp = alloc("ptp", [128, int(sum(_qa_list))])
    qres = alloc("qres", [128, DC * SO])
    kbuf = alloc("kbuf", [128, 2 * DC * 128])
    qa_list = _qa_list
    PTW = int(sum(qa_list))
    poff = np.cumsum([0] + qa_list)[:-1].tolist()
    sp_ps = palloc("sp_ps", [128, 2 * SO])
    l_ps = palloc("l_ps", [128, QT])
    sy.wait_ge(sT.h, b2_t)
    sy.wait_ge(sV.h, b2_v)
    sy.wait_ge(sS.h, sS.n)
    sy.wait_ge(sKV.h, sKV.n)
    for o in range(DC):
        sL.inc(sy.dma_start(out=qres[:, o * SO:(o + 1) * SO], in_=q_sp[o]), 16)
    te.wait_ge(sL.h, sL.n)
    te.wait_ge(sV.h, b2_v)
    kuse = [0, 0]
    pset_tok = [0, 0]
    ldk_tok = [0, 0]
    exp_tok = [0]
    first_l = True
    ldk_tok[0] = sX[0].inc(sy.dma_start(out=kbuf[:, :DC * 128], in_=k_sp[:, 0, :]), 16)
    for m in range(KT):
        b = m % 2
        kb = kbuf[:, b * DC * 128:(b + 1) * DC * 128]
        if m + 1 < KT:
            b2 = (m + 1) % 2
            sy.wait_ge(sT.h, kuse[b2])
            ldk_tok[b2] = sX[b2].inc(sy.dma_start(
                out=kbuf[:, b2 * DC * 128:(b2 + 1) * DC * 128],
                in_=k_sp[:, m + 1, :]), 16)
        qoff = 128 * (m % QT)
        qa = SO - qoff
        te.wait_ge(sX[b].h, ldk_tok[b])
        te.wait_ge(sV.h, pset_tok[b])
        base = b * SO
        nseg = (qa + SLAB - 1) // SLAB
        for seg in range(nseg):
            lo = seg * SLAB
            hi = min(qa, lo + SLAB)
            for dc in range(DC):
                sT.inc(te.matmul(
                    sp_ps[:, base + lo:base + hi],
                    kb[:, dc * 128:(dc + 1) * 128],
                    qres[:, dc * SO + qoff + lo: dc * SO + qoff + hi],
                    start=(dc == 0), stop=(dc == DC - 1)))
        kuse[b] = sT.n
        wpump.pump(5)
        pm = ptp[:, poff[m]:poff[m] + qa]
        ve.wait_ge(sT.h, sT.n)
        ve.tensor_tensor(pm, sp_ps[:, base:base + qa], negM[:, qoff:SO], op=ALU.add)
        sV.inc(ve.drain())
        pset_tok[b] = sV.n
        mk = mask_own if m < QT else mask_peer
        ve.tensor_tensor(ptp[:, poff[m]:poff[m] + 128],
                         ptp[:, poff[m]:poff[m] + 128], mk[:], op=ALU.add)
        sV.inc(ve.drain())
        ac.wait_ge(sV.h, sV.n)
        exp_tok[0] = sA.inc(ac.activation(pm, pm, ACTF.Exp, scale=c.scale))
        te.wait_ge(sA.h, sA.n)
        for jj in range(qa // 128):
            col = qoff // 128 + jj
            sT.inc(te.matmul(l_ps[:, col:col + 1],
                             ptp[:, poff[m] + jj * 128: poff[m] + (jj + 1) * 128],
                             ones[:], start=first_l, stop=(m == KT - 1),
                             skip_group_check=True))
            first_l = False
        wpump.pump(5)
    ve.wait_ge(sT.h, sT.n)
    ve.reciprocal(recip_l[:], l_ps[:])
    sV.inc(ve.drain())
    if c.debug:
        sy.wait_ge(sV.h, sV.n)
        sy.wait_ge(sA.h, sA.n)
        sS.inc(sy.dma_start(out=dbg_ptp[:], in_=ptp[:]), 16)
        sS.inc(sy.dma_start(out=dbg_l[:], in_=recip_l[:]), 16)
        sS.inc(sy.dma_start(out=dbg_negM[:], in_=negM[:]), 16)
    free("kbuf", "qres")
    pfree("l_ps", "sp_ps")
    b3_t, b3_v, b3_a = sT.n, sV.n, exp_tok[0]

    vbuf = alloc("vbuf", [128, 2 * KT * 128])
    o_ps = palloc("o_ps", [128, 2 * SO])
    vuse = [0, 0]
    oset_tok = [0, 0]
    sy.wait_ge(sT.h, b3_t)
    te.wait_ge(sV.h, b3_v)
    te.wait_ge(sA.h, b3_a)
    for cc in range(DC):
        b = cc % 2
        vb = vbuf[:, b * KT * 128:(b + 1) * KT * 128]
        sy.wait_ge(sT.h, vuse[b])
        ld = sU[b].inc(sy.dma_start(out=vb, in_=v_sp[:, cc, :]), 16)
        te.wait_ge(sU[b].h, ld)
        te.wait_ge(sV.h, oset_tok[b])
        base = b * SO
        for m in range(KT):
            qoff = 128 * (m % QT)
            for seg in range(OSL):
                lo = max(seg * SLAB, qoff)
                hi = (seg + 1) * SLAB
                if lo >= hi:
                    continue
                sT.inc(te.matmul(
                    o_ps[:, base + lo:base + hi],
                    vb[:, m * 128:(m + 1) * 128],
                    ptp[:, poff[m] + lo - qoff: poff[m] + hi - qoff],
                    start=(m == 0), stop=(m == KT - 1), skip_group_check=True))
        vuse[b] = sT.n
        ve.wait_ge(sT.h, sT.n)
        ve.tensor_copy(ofm[:, cc * SO:(cc + 1) * SO], o_ps[:, base:base + SO])
        oset_tok[b] = sV.inc(ve.drain())
        wpump.pump(20)
    if c.debug:
        sy.wait_ge(sA.h, sA.n)
        sS.inc(sy.dma_start(out=dbg_ofm[:], in_=ofm[:]), 16)
    free("vbuf")
    free("ptp")
    pfree("o_ps")
    b4_t, b4_v = sT.n, sV.n

    ao = alloc("ao", [128, QT * D])
    wos = alloc("wos", [128, DC * SLAB])
    ao_ps = palloc("ao_ps", [128, 1024])
    aevac = [0, 0]
    sy.wait_ge(sT.h, b4_t)
    te.wait_ge(sV.h, b4_v)
    for ds in range(DSL):
        te.wait_ge(sT.h, sT.n)
        sy.wait_ge(sT.h, sT.n)
        sWS.inc(sy.dma_start(
            out=wos[:], in_=wob_hbm[:, ds * DC * SLAB:(ds + 1) * DC * SLAB]), 16)
        te.wait_ge(sWS.h, sWS.n)
        for qc in range(QT):
            bk = qc % 2
            pbk = ao_ps[:, bk * 512:bk * 512 + SLAB]
            te.wait_ge(sV.h, aevac[bk])
            for dc in range(DC):
                sT.inc(te.matmul(pbk, ofm[:, dc * SO + qc * 128: dc * SO + (qc + 1) * 128],
                                 wos[:, dc * SLAB:(dc + 1) * SLAB],
                                 start=(dc == 0), stop=(dc == DC - 1)))
            ve.wait_ge(sT.h, sT.n)
            ve.tensor_scalar(ao[:, qc * D + ds * SLAB: qc * D + (ds + 1) * SLAB],
                             pbk, recip_l[:, qc:qc + 1], None, op0=ALU.mult)
            aevac[bk] = sV.inc(ve.drain())
    if c.debug:
        sy.wait_ge(sV.h, sV.n)
        sS.inc(sy.dma_start(out=dbg_ao[:], in_=ao[:]), 16)
        barrier()
    free("wos")
    pfree("ao_ps")
    hbt = alloc("hbt", [128, D])
    xbt = alloc("xbt", [128, D])
    z2t = alloc("z2t", [128, D])
    for qc in range(QT):
        sy.wait_ge(sV.h, sV.n)
        sL.inc(sy.dma_start(out=hbt[:], in_=h_sp[qc * 128:(qc + 1) * 128, :]), 16)
        ld = sL.inc(sy.dma_start(out=xbt[:], in_=x_hbm[qc * 128:(qc + 1) * 128, :]), 16)
        at = ao[:, qc * D:(qc + 1) * D]
        ve.wait_ge(sL.h, ld)
        ve.tensor_tensor(at, at, hbt[:], op=ALU.add)
        ve.drain()
        ln_stats(at, D)
        ln_rstd()
        ac.wait_ge(sS.h, sS.n)
        ln_apply(z2t[:], at)
        ve.wait_ge(sA.h, sA.n)
        ve.tensor_tensor(z2t[:], z2t[:], xbt[:], op=ALU.add)
        sV.inc(ve.drain())
        sy.wait_ge(sV.h, sV.n)
        sS.inc(sy.dma_start(out=x1_sp[qc * 128:(qc + 1) * 128, :], in_=z2t[:]), 16)
    wpump.drain()
    free("z2t", "xbt", "hbt", "ao", "ofm")
    free("wib", "wmb", "wpb", "wqb")
    barrier()

    free("negM")
    NSLOT = 3
    NH4 = c.NH4
    xiT = alloc("xiT", [128, DC * SO], BF16)
    ft = alloc("ft", [128, D])
    rs1 = alloc("rs1", [128, 1024])
    xit = alloc("xit", [128, D], BF16)
    wgsl = alloc("wgsl", [128, 2 * DC * SLAB], FP8)
    wvsl = alloc("wvsl", [128, 2 * DC * SLAB], FP8)
    gbb = alloc("gbb", [128, SLAB])
    vbb = alloc("vbb", [128, SLAB])
    gsl = alloc("gsl", [128, 2 * SLAB])
    vsl = alloc("vsl", [128, SLAB])
    sgn = alloc("sgn", [128, 4 * c.W6])
    ubuf = alloc("ubuf", [128, 4 * c.W6])
    scr16 = alloc("scr16", [128, 2 * c.W6], BF16)
    cm1275 = alloc("cm1275", [128, 1])
    c1285 = alloc("c1285", [128, 1])
    dsc = alloc("dsc", [128, 128])
    gvb = alloc("gvb", [128, NSLOT * H])
    pt16 = palloc("pt16", [128, 2048], BF16)
    gps2 = palloc("gps2", [128, 1024])
    vps2 = palloc("vps2", [128, 1024])
    ptq = palloc("ptq", [128, 1024])
    ve.memset(cm1275[:], -127.5)
    ve.memset(c1285[:], 128.5)
    ve.memset(ggvcol[:], 0.0)
    sV.inc(ve.drain())
    ac.wait_ge(sV.h, sV.n)

    slot_free = [None] * NSLOT
    wu = [0, 0]
    bias_rd = [0]
    silu_tok = [0, 0]
    pfree_tok = [0, 0]
    dsc_war = [0]
    evt6 = [0, 0]
    sgnwar = [0, 0, 0, 0]
    uwar = [0, 0, 0, 0]
    scr_tok = [0, 0]
    evt = [0, 0]
    xa_tok = [0] * QT
    ft_free = [0]

    def p6aA(qc):
        sy.wait_ge(sV.h, ft_free[0])
        sy.wait_ge(sS.h, sS.n)
        ld = sX[0].inc(sy.dma_start(out=ft[:], in_=x1_sp[qc * 128:(qc + 1) * 128, :]), 16)
        ve.wait_ge(sX[0].h, ld)
        ln_stats(ft[:], D)
        ln_rstd()
        ln_apply(ft[:], ft[:])
        ve.wait_ge(sA.h, sA.n)
        ve.tensor_reduce(gxcol[:, qc:qc + 1], ft[:], axis=AX.X, op=ALU.max,
                         apply_absolute_value=True)
        ve.drain()
        ve.tensor_scalar(gxcol[:, qc:qc + 1], gxcol[:, qc:qc + 1], 1e-5, None, op0=ALU.max)
        ve.drain()
        ve.reciprocal(tt1[:], gxcol[:, qc:qc + 1])
        ve.drain()
        ve.tensor_scalar(sc_x[:, qc:qc + 1], tt1[:], 127.0, None, op0=ALU.mult)
        ve.drain()
        ve.tensor_scalar(tt0[:], gxcol[:, qc:qc + 1], 0.5 / 127.0, None, op0=ALU.mult)
        ve.drain()
        ve.tensor_tensor(scg_t[:, qc:qc + 1], tt0[:], gamw[:, 0:1], op=ALU.mult)
        ve.tensor_tensor(scv_t[:, qc:qc + 1], tt0[:], gamw[:, 1:2], op=ALU.mult)
        sV.inc(ve.drain())

    def p6aB(qc, deep=False):
        HD = min(512, D // 2)
        wtok = sV.inc(ve.tensor_scalar(ft[:], ft[:], sc_x[:, qc:qc + 1], 128.5,
                                       op0=ALU.mult, op1=ALU.add))
        fin_tok = wtok
        if deep:
            nch = D // HD
            ac.wait_ge(sV.h, wtok)
            ac.drain()
            up = {}
            for ci in range(nch):
                sl_ = slice(ci * HD, (ci + 1) * HD)
                if uwar[ci % 4]:
                    ac.wait_ge(sV.h, uwar[ci % 4])
                sA.inc(ac.activation(ubuf[:, sl_], ft[:, sl_],
                                     ACTF.Identity, bias=cm1275[:]))
                up[ci] = sgnwar[ci % 4]
            ac.drain()
            for j in range(8):
                kpow = -float(1 << (6 - j)) if j <= 6 else -0.5
                for ci in range(nch):
                    sl_ = slice(ci * HD, (ci + 1) * HD)
                    if up[ci]:
                        ac.wait_ge(sV.h, up[ci])
                    sA.inc(ac.activation(sgn[:, sl_], ubuf[:, sl_], ACTF.Sign,
                                         bias=chalf[:, 0:1]))
                    ve.wait_ge(sA.h, sA.n)
                    up[ci] = sV.inc(ve.scalar_tensor_tensor(
                        ubuf[:, sl_], sgn[:, sl_], kpow, ubuf[:, sl_],
                        op0=ALU.mult, op1=ALU.add))
            for ci in range(nch):
                sl_ = slice(ci * HD, (ci + 1) * HD)
                sgnwar[ci % 4] = up[ci]
                ve.wait_ge(sV.h, up[ci])
                ve.scalar_tensor_tensor(xit[:, sl_], ubuf[:, sl_], -1.0, ft[:, sl_],
                                        op0=ALU.mult, op1=ALU.add)
                ve.drain()
                ve.tensor_scalar(xit[:, sl_], xit[:, sl_], -128.0, None, op0=ALU.add)
                fin_tok = sV.inc(ve.drain())
                uwar[ci % 4] = fin_tok
            passes = []
        else:
            passes = list(range(D // (2 * HD)))
        for ps_ in passes:
            ac.wait_ge(sV.h, fin_tok)
            ac.drain()
            up = {}
            for ci in range(2):
                sl_ = slice(ps_ * 2 * HD + ci * HD, ps_ * 2 * HD + (ci + 1) * HD)
                sA.inc(ac.activation(rs1[:, ci * HD:(ci + 1) * HD], ft[:, sl_],
                                     ACTF.Identity, bias=cm1275[:]))
                up[ci] = 0
            ac.drain()
            for j in range(8):
                kpow = -float(1 << (6 - j)) if j <= 6 else -0.5
                for ci in range(2):
                    sl_ = slice(ps_ * 2 * HD + ci * HD, ps_ * 2 * HD + (ci + 1) * HD)
                    uch = rs1[:, ci * HD:(ci + 1) * HD]
                    sch = xit[:, sl_]
                    if up[ci]:
                        ac.wait_ge(sV.h, up[ci])
                    sA.inc(ac.activation(sch, uch, ACTF.Sign, bias=chalf[:, 0:1]))
                    ve.wait_ge(sA.h, sA.n)
                    up[ci] = sV.inc(ve.scalar_tensor_tensor(
                        uch, sch, kpow, uch, op0=ALU.mult, op1=ALU.add))
            for ci in range(2):
                sl_ = slice(ps_ * 2 * HD + ci * HD, ps_ * 2 * HD + (ci + 1) * HD)
                ve.wait_ge(sV.h, up[ci])
                ve.scalar_tensor_tensor(xit[:, sl_], rs1[:, ci * HD:(ci + 1) * HD],
                                        -1.0, ft[:, sl_], op0=ALU.mult, op1=ALU.add)
                ve.drain()
                ve.tensor_scalar(xit[:, sl_], xit[:, sl_], -128.0, None, op0=ALU.add)
                fin_tok = sV.inc(ve.drain())
        ft_free[0] = fin_tok
        te.wait_ge(sV.h, ft_free[0])
        for dc in range(DC):
            pb = pt16[:, (dc % 2) * 1024:(dc % 2) * 1024 + 128]
            te.wait_ge(sA.h, evt[dc % 2])
            sT.inc(te.transpose(pb, xit[:, dc * 128:(dc + 1) * 128], eye16[:]))
            ac.wait_ge(sT.h, sT.n)
            evt[dc % 2] = sA.inc(ac.copy(
                xiT[:, dc * SO + qc * 128: dc * SO + (qc + 1) * 128], pb))
        xa_tok[qc] = sA.n

    def produce_gen(qg):
        for s in range(HSL):
            b = s % 2
            wgb = wgsl[:, b * DC * SLAB:(b + 1) * DC * SLAB]
            wvb2 = wvsl[:, b * DC * SLAB:(b + 1) * DC * SLAB]
            if s == 0:
                te.wait_ge(sA.h, max(xa_tok[qg * c.QG], xa_tok[qg * c.QG + 1]))
            sy.wait_ge(sT.h, wu[b])
            sW[b].inc(sy.dma_start(out=wgb[:], in_=wgi_sp[s]), 16)
            sW[b].inc(sy.dma_start(out=wvb2[:], in_=wvi_sp[s]), 16)
            ldw = sW[b].n
            sy.wait_ge(sV.h, bias_rd[0])
            sH[0].inc(sy.dma_start(out=gbb[:],
                                   in_=gbbt_hbm[:, s * SLAB:(s + 1) * SLAB]), 16)
            ld2 = sH[0].inc(sy.dma_start(out=vbb[:],
                                         in_=vbbt_hbm[:, s * SLAB:(s + 1) * SLAB]), 16)
            te.wait_ge(sW[b].h, ldw)
            for j in range(c.QG):
                qc = qg * c.QG + j
                slot = qc % NSLOT
                gvs = gvb[:, slot * H + s * SLAB: slot * H + (s + 1) * SLAB]
                gps_j = gps2[:, j * SLAB:(j + 1) * SLAB]
                vps_j = vps2[:, j * SLAB:(j + 1) * SLAB]
                gsl_j = gsl[:, j * SLAB:(j + 1) * SLAB]
                te.wait_ge(sV.h, pfree_tok[j])
                for dc in range(DC):
                    sT.inc(te.matmul(gps_j,
                                     xiT[:, dc * SO + qc * 128: dc * SO + (qc + 1) * 128],
                                     wgb[:, dc * SLAB:(dc + 1) * SLAB],
                                     start=(dc == 0), stop=(dc == DC - 1)))
                tok_g = sT.n
                for dc in range(DC):
                    sT.inc(te.matmul(vps_j,
                                     xiT[:, dc * SO + qc * 128: dc * SO + (qc + 1) * 128],
                                     wvb2[:, dc * SLAB:(dc + 1) * SLAB],
                                     start=(dc == 0), stop=(dc == DC - 1)))
                tok_v = sT.n
                ve.wait_ge(sT.h, tok_g)
                ve.wait_ge(sH[0].h, ld2)
                ve.wait_ge(sA.h, silu_tok[j])
                ve.scalar_tensor_tensor(gsl_j, gps_j, scg_t[:, qc:qc + 1],
                                        gbb[:], op0=ALU.mult, op1=ALU.add)
                sV.inc(ve.drain())
                ac.wait_ge(sV.h, sV.n)
                silu_tok[j] = sA.inc(ac.activation(gsl_j, gsl_j, ACTF.Silu))
                ve.wait_ge(sT.h, tok_v)
                ve.scalar_tensor_tensor(vsl[:], vps_j, scv_t[:, qc:qc + 1],
                                        vbb[:], op0=ALU.mult, op1=ALU.add)
                pfree_tok[j] = sV.inc(ve.drain())
                ve.wait_ge(sA.h, silu_tok[j])
                if slot_free[slot] is not None:
                    si, stok = slot_free[slot][s]
                    ve.wait_ge(sP6[si].h, stok)
                ve.tensor_tensor(gvs, gsl_j, vsl[:], op=ALU.mult)
                ve.drain()
                ve.tensor_reduce(tt0[:], gvs, axis=AX.X, op=ALU.max,
                                 apply_absolute_value=True)
                ve.drain()
                ve.tensor_tensor(ggvcol[:, qc:qc + 1], ggvcol[:, qc:qc + 1], tt0[:],
                                 op=ALU.max)
                sV.inc(ve.drain())
            bias_rd[0] = sV.n
            wu[b] = sT.n
            yield

    def qA(qc):
        slot = qc % NSLOT
        base = slot * H
        ve.tensor_scalar(ggvcol[:, qc:qc + 1], ggvcol[:, qc:qc + 1], 1e-5, None,
                         op0=ALU.max)
        ve.drain()
        ve.reciprocal(tt1[:], ggvcol[:, qc:qc + 1])
        ve.drain()
        ve.tensor_scalar(sc_gv[:, qc:qc + 1], tt1[:], 127.0, None, op0=ALU.mult)
        ve.drain()
        ve.tensor_scalar(tt0[:], ggvcol[:, qc:qc + 1], 0.5 / 127.0, None, op0=ALU.mult)
        ve.drain()
        ve.tensor_tensor(sc3[:, qc:qc + 1], tt0[:], gamw[:, 2:3], op=ALU.mult)
        ve.drain()
        ve.wait_ge(sT.h, dsc_war[0])
        ve.tensor_scalar(dsc[:], eye32[:], sc_gv[:, qc:qc + 1], None, op0=ALU.mult)
        dsc_tok = sV.inc(ve.drain())
        te.wait_ge(sV.h, dsc_tok)
        for ch in range(HC):
            p = ch % 2
            pb = ptq[:, p * 512:p * 512 + 128]
            blk = gvb[:, base + ch * 128: base + (ch + 1) * 128]
            te.wait_ge(sA.h, evt6[p])
            sT.inc(te.matmul(pb, blk, dsc[:], start=True, stop=True))
            ac.wait_ge(sT.h, sT.n)
            evt6[p] = sA.inc(ac.activation(blk, pb, ACTF.Identity, bias=c1285[:]))
        dsc_war[0] = sT.n
        slot_free[slot] = {}

    def pair_gen(qc):
        slot = qc % NSLOT
        base = slot * H
        W6 = c.W6
        for k0 in range(0, c.NHW, 4):
            pair = (k0, k0 + 1, k0 + 2, k0 + 3)
            upd = {}
            ac.drain()
            for ki in pair:
                i = ki % 4
                wch = gvb[:, base + ki * W6: base + (ki + 1) * W6]
                uch = ubuf[:, i * W6:(i + 1) * W6]
                if uwar[i]:
                    ac.wait_ge(sV.h, uwar[i])
                sA.inc(ac.activation(uch, wch, ACTF.Identity, bias=cm1275[:]))
                upd[ki] = sgnwar[i]
            ac.drain()
            for j in range(8):
                kpow = -float(1 << (6 - j)) if j <= 6 else -0.5
                for ki in pair:
                    i = ki % 4
                    uch = ubuf[:, i * W6:(i + 1) * W6]
                    sch = sgn[:, i * W6:(i + 1) * W6]
                    if upd[ki]:
                        ac.wait_ge(sV.h, upd[ki])
                    sA.inc(ac.activation(sch, uch, ACTF.Sign, bias=chalf[:, 0:1]))
                    ve.wait_ge(sA.h, sA.n)
                    upd[ki] = sV.inc(ve.scalar_tensor_tensor(
                        uch, sch, kpow, uch, op0=ALU.mult, op1=ALU.add))
            for ki in pair:
                i = ki % 4
                p2 = ki % 2
                sgnwar[i] = upd[ki]
                wch = gvb[:, base + ki * W6: base + (ki + 1) * W6]
                ve.wait_ge(sV.h, upd[ki])
                uwar[i] = sV.inc(ve.scalar_tensor_tensor(
                    wch, ubuf[:, i * W6:(i + 1) * W6], -1.0, wch,
                    op0=ALU.mult, op1=ALU.add))
                ac.wait_ge(sV.h, uwar[i])
                if scr_tok[p2]:
                    ac.wait_ge(sP6[p2].h, scr_tok[p2])
                ct = sA.inc(ac.copy(scr16[:, p2 * W6:(p2 + 1) * W6], wch))
                gp.wait_ge(sA.h, ct)
                scr_tok[p2] = sP6[p2].inc(gp.dma_start(
                    out=x2i_sp[qc][:, ki * W6:(ki + 1) * W6],
                    in_=scr16[:, p2 * W6:(p2 + 1) * W6]), 16)
                slot_free[slot][ki] = (p2, scr_tok[p2])
            yield

    assert c.QG == 2 and HSL % 4 == 0 and c.NHW == HSL
    p6a_pos = [3 * HSL // 16, 5 * HSL // 16, 9 * HSL // 16, 11 * HSL // 16]
    p6aA(0)
    p6aB(0, deep=True)
    p6aA(1)
    p6aB(1, deep=True)
    for qg in range(c.NQG):
        pairs = []
        if qg > 0:
            for q in (qg * 2 - 2, qg * 2 - 1):
                qA(q)
                pairs.append(pair_gen(q))
        nxt = [qg * 2 + 2, qg * 2 + 3]
        pg = produce_gen(qg)
        for s in range(HSL):
            if pairs:
                if s % 4 == 0:
                    next(pairs[0], None)
                elif s % 4 == 2:
                    next(pairs[1], None)
            if s == p6a_pos[0] and nxt[0] < QT:
                p6aA(nxt[0])
            if s == p6a_pos[1] and nxt[0] < QT:
                p6aB(nxt[0])
            if s == p6a_pos[2] and nxt[1] < QT:
                p6aA(nxt[1])
            if s == p6a_pos[3] and nxt[1] < QT:
                p6aB(nxt[1])
            next(pg)
        for g_ in pairs:
            for _ in g_:
                pass
    for q in (QT - 2, QT - 1):
        qA(q)
    for _ in zip(pair_gen(QT - 2), pair_gen(QT - 1)):
        pass
    free("gvb", "dsc", "c1285", "cm1275", "scr16", "ubuf", "sgn", "vsl",
         "gsl", "vbb", "gbb", "wvsl", "wgsl",
         "xit", "rs1", "ft", "xiT")
    pfree("ptq", "vps2", "gps2", "pt16")
    barrier()

    wo3 = alloc("wo3", [128, HC * D], FP8)
    xb2 = alloc("xb2", [128, 2 * HC * 128], BF16)
    o3sb = alloc("o3sb", [128, D])
    x1b = alloc("x1b", [128, D])
    z4t = alloc("z4t", [128, D])
    obb = alloc("obb", [128, D])
    csb = alloc("csb", [128, D])
    sL.inc(sy.dma_start(out=obb[:], in_=obbt_hbm[:]), 16)
    NW7 = max(1, HC // 16)
    for k7 in range(NW7):
        w7 = HC * D // NW7
        sL.inc(sy.dma_start(out=wo3[:, k7 * w7:(k7 + 1) * w7],
                            in_=woi_sp[:, k7 * w7:(k7 + 1) * w7]), 16)
    te.wait_ge(sL.h, sL.n)
    cs_ps = palloc("cs_ps", [1, D])
    for hc in range(HC):
        for dsl_ in range(DSL):
            sT.inc(te.matmul(cs_ps[:1, dsl_ * SLAB:(dsl_ + 1) * SLAB], ones16[:],
                             wo3[:, hc * D + dsl_ * SLAB: hc * D + (dsl_ + 1) * SLAB],
                             start=(hc == 0), stop=(hc == HC - 1)))
    ve.wait_ge(sT.h, sT.n)
    ve.tensor_scalar(csb[:1, :], cs_ps[:1, :], 128.0, None, op0=ALU.mult)
    sV.inc(ve.drain())
    pfree("cs_ps")
    bc_ps7 = palloc("bc_ps7", [128, 512])
    te.wait_ge(sV.h, sV.n)
    bcast(csb, csb[:1, :], D, bc_ps7)
    pfree("bc_ps7")
    o3ps = palloc("o3ps", [128, 2048])
    te.wait_ge(sA.h, sA.n)
    xu = [0, 0]
    ev7 = [0, 0]
    ld7 = [0, 0]
    ld7[0] = sX[0].inc(sy.dma_start(out=xb2[:, :HC * 128], in_=x2i_sp[0]), 16)
    for qc in range(QT):
        b7 = qc % 2
        xbb = xb2[:, b7 * HC * 128:(b7 + 1) * HC * 128]
        sy.wait_ge(sV.h, sV.n)
        lx7 = sK.inc(sy.dma_start(out=x1b[:], in_=x1_sp[qc * 128:(qc + 1) * 128, :]), 16)
        if qc + 1 < QT:
            b2 = (qc + 1) % 2
            sy.wait_ge(sT.h, xu[b2])
            ld7[b2] = sX[b2].inc(sy.dma_start(
                out=xb2[:, b2 * HC * 128:(b2 + 1) * HC * 128], in_=x2i_sp[qc + 1]), 16)
        te.wait_ge(sX[b7].h, ld7[b7])
        for dg in range((DSL + 1) // 2):
            dss = range(2 * dg, min(2 * dg + 2, DSL))
            te.wait_ge(sV.h, ev7[dg % 2])
            for hc in range(HC):
                for ds in dss:
                    sT.inc(te.matmul(o3ps[:, ds * SLAB:(ds + 1) * SLAB],
                                     xbb[:, hc * 128:(hc + 1) * 128],
                                     wo3[:, hc * D + ds * SLAB: hc * D + (ds + 1) * SLAB],
                                     start=(hc == 0), stop=(hc == HC - 1)))
            xu[b7] = sT.n
            ve.wait_ge(sT.h, sT.n)
            ve.wait_ge(sA.h, sA.n)
            for ds in dss:
                ve.tensor_tensor(o3sb[:, ds * SLAB:(ds + 1) * SLAB],
                                 o3ps[:, ds * SLAB:(ds + 1) * SLAB],
                                 csb[:, ds * SLAB:(ds + 1) * SLAB], op=ALU.subtract)
                ve.drain()
                ve.scalar_tensor_tensor(o3sb[:, ds * SLAB:(ds + 1) * SLAB],
                                        o3sb[:, ds * SLAB:(ds + 1) * SLAB],
                                        sc3[:, qc:qc + 1], obb[:, ds * SLAB:(ds + 1) * SLAB],
                                        op0=ALU.mult, op1=ALU.add)
                ve.drain()
            ev7[dg % 2] = sV.inc(ve.drain())
        ln_stats(o3sb[:], D)
        ln_rstd()
        ac.wait_ge(sO.h, sO.n)
        ln_apply(z4t[:], o3sb[:])
        ve.wait_ge(sA.h, sA.n)
        ve.wait_ge(sK.h, lx7)
        ve.tensor_tensor(z4t[:], z4t[:], x1b[:], op=ALU.add)
        sV.inc(ve.drain())
        sy.wait_ge(sV.h, sV.n)
        sO.inc(sy.dma_start(out=out_hbm[qc * 128:(qc + 1) * 128, :], in_=z4t[:]), 16)
    sy.wait_ge(sO.h, sO.n)
    barrier()

    for cm in reversed(list(live.values())):
        cm.__exit__(None, None, None)
    for cm in reversed(list(psum_live.values())):
        cm.__exit__(None, None, None)
    for cm in reversed(sem_cms):
        cm.__exit__(None, None, None)
    return nc


def _bf16():
    import ml_dtypes
    return ml_dtypes.bfloat16


def reshape_blocked(w, out_blk=128):
    Dd, N = w.shape
    dcn = Dd // 128
    nb = N // out_blk
    return np.ascontiguousarray(
        w.reshape(dcn, 128, nb, out_blk).transpose(1, 2, 0, 3).reshape(128, -1))


def make_core_inputs(cfg, x_b, parity, shared):
    own = np.ascontiguousarray(x_b[parity::2])
    peer = np.ascontiguousarray(x_b[1 - parity::2])
    xr = np.concatenate([own, peer], 0)
    ii = np.arange(128)[:, None]
    jj = np.arange(128)[None, :]
    mask_own = np.where(ii <= jj, 0.0, NEG_MASK).astype(np.float32)
    if parity == 0:
        mask_peer = np.where(ii < jj, 0.0, NEG_MASK).astype(np.float32)
    else:
        mask_peer = np.where(ii <= jj, 0.0, NEG_MASK).astype(np.float32)
    d = {"x": xr, "mask_own": mask_own, "mask_peer": mask_peer}
    d.update(shared)
    return d


def kernel(L0_attn_x, L0_attn_qkv, L0_attn_o_proj, L0_attn_gamma, L0_attn_beta,
           L0_ffn_gate_w, L0_ffn_gate_b, L0_ffn_val_w, L0_ffn_val_b,
           L0_ffn_out_w, L0_ffn_out_b):
    cfg = Cfg(D=2048, S=2048)
    x = np.asarray(L0_attn_x, dtype=np.float32)
    qkv_f = np.asarray(L0_attn_qkv, dtype=np.float32)
    wo_f = np.asarray(L0_attn_o_proj, dtype=np.float32)
    shared = {
        "qkv": np.ascontiguousarray(qkv_f),
        "qkvc": reshape_blocked(qkv_f, 128),
        "wob": reshape_blocked(wo_f, 512),
        "gamma": np.asarray(L0_attn_gamma, dtype=np.float32),
        "beta": np.asarray(L0_attn_beta, dtype=np.float32),
        "gwT": np.ascontiguousarray(np.asarray(L0_ffn_gate_w, dtype=np.float32).T),
        "vwT": np.ascontiguousarray(np.asarray(L0_ffn_val_w, dtype=np.float32).T),
        "owT": np.ascontiguousarray(np.asarray(L0_ffn_out_w, dtype=np.float32).T),
        "gate_b": np.asarray(L0_ffn_gate_b, dtype=np.float32),
        "gbbt": np.ascontiguousarray(np.broadcast_to(
            np.asarray(L0_ffn_gate_b, dtype=np.float32), (128, len(L0_ffn_gate_b)))),
        "vbbt": np.ascontiguousarray(np.broadcast_to(
            np.asarray(L0_ffn_val_b, dtype=np.float32), (128, len(L0_ffn_val_b)))),
        "obbt": np.ascontiguousarray(np.broadcast_to(
            np.asarray(L0_ffn_out_b, dtype=np.float32), (128, len(L0_ffn_out_b)))),
        "val_b": np.asarray(L0_ffn_val_b, dtype=np.float32),
        "out_b": np.asarray(L0_ffn_out_b, dtype=np.float32),
        "eye32": np.eye(128, dtype=np.float32),
        "eye16": np.eye(128, dtype=np.float32).astype(_bf16()),
    }
    nc = build_program(cfg)
    in_maps = []
    for core in range(8):
        b, p = divmod(core, 2)
        in_maps.append(make_core_inputs(cfg, x[b], p, shared))
    res = run_bass_kernel_spmd(nc, in_maps, core_ids=list(range(8)))
    out = np.empty((4, cfg.S, cfg.D), dtype=np.float32)
    for core in range(8):
        b, p = divmod(core, 2)
        out[b, p::2] = res.results[core]["out"]
    return out


# version 32
# speedup vs baseline: 1.1789x; 1.0280x over previous
import math
import numpy as np

import concourse.bass as bass
import concourse.mybir as mybir
from concourse.bass_utils import run_bass_kernel_spmd

F32 = mybir.dt.float32
BF16 = mybir.dt.bfloat16
FP8 = mybir.dt.float8e4
AX = mybir.AxisListType
ALU = mybir.AluOpType
ACTF = mybir.ActivationFunctionType

NEG_MASK = -1.0e33
EPS = 1.0e-5


class Cfg:
    def __init__(self, D=2048, S=2048, H=None):
        self.D = D
        self.S = S
        self.H = H if H is not None else 4 * D
        self.SO = S // 2
        self.DC = D // 128
        self.QT = self.SO // 128
        self.KT = S // 128
        self.HC = self.H // 128
        self.SLAB = min(512, self.H, D, self.SO)
        self.DSL = D // self.SLAB
        self.HSL = self.H // self.SLAB
        self.SSL = S // self.SLAB
        self.OSL = self.SO // self.SLAB
        self.QG = min(2, self.QT)
        self.NQG = self.QT // self.QG
        self.NH4 = self.H // 512
        self.W6 = 512
        self.NHW = self.H // self.W6
        self.scale = 1.0 / math.sqrt(float(D))
        self.debug = False
        assert self.SLAB == 512, "psum accumulation slabs must be bank-aligned"
        assert D % 128 == 0 and self.SO % 128 == 0 and self.H % self.SLAB == 0
        assert D % self.SLAB == 0 and self.SO % self.SLAB == 0
        assert self.NH4 % 2 == 0 and self.QT % self.QG == 0
        assert self.NHW % 4 == 0 and self.W6 % 512 == 0


class Sem:
    def __init__(self, h):
        self.h = h
        self.n = 0

    def inc(self, instr, k=1):
        instr.then_inc(self.h, k)
        self.n += k
        return self.n


def build_program(c: Cfg):
    nc = bass.Bass("TRN2", target_bir_lowering=False)
    D, S, H, SO = c.D, c.S, c.H, c.SO
    DC, QT, KT, HC = c.DC, c.QT, c.KT, c.HC
    SLAB, DSL, HSL, OSL = c.SLAB, c.DSL, c.HSL, c.OSL

    din = lambda n, sh, dt=F32: nc.dram_tensor(n, sh, dt, kind="ExternalInput").ap()
    x_hbm = din("x", [S, D])
    qkv_hbm = din("qkv", [D, 3 * D])
    qkvc_hbm = din("qkvc", [128, (3 * D // 128) * DC * 128])
    wob_hbm = din("wob", [128, DSL * DC * SLAB])
    gamma_hbm = din("gamma", [D])
    beta_hbm = din("beta", [D])
    gwT_hbm = din("gwT", [D, H])
    vwT_hbm = din("vwT", [D, H])
    owT_hbm = din("owT", [H, D])
    gb_hbm = din("gate_b", [H])
    vb_hbm = din("val_b", [H])
    ob_hbm = din("out_b", [D])
    gbbt_hbm = din("gbbt", [128, H])
    vbbt_hbm = din("vbbt", [128, H])
    obbt_hbm = din("obbt", [128, D])
    mko_hbm = din("mask_own", [128, 128])
    mkp_hbm = din("mask_peer", [128, 128])
    eye32_hbm = din("eye32", [128, 128])
    eye16_hbm = din("eye16", [128, 128], BF16)
    out_hbm = nc.dram_tensor("out", [SO, D], F32, kind="ExternalOutput").ap()

    dint = lambda n, sh, dt=F32: nc.dram_tensor(n, sh, dt, kind="Internal").ap()
    hn_sp = dint("hn_sp", [KT, 128, DC * 128])
    k_sp = dint("k_sp", [128, KT, DC * 128])
    v_sp = dint("v_sp", [128, DC, KT * 128])
    q_sp = dint("q_sp", [DC, 128, SO])
    h_sp = dint("h_sp", [SO, D])
    x1_sp = dint("x1_sp", [SO, D])
    wgi_sp = dint("wgi_sp", [HSL, 128, DC * SLAB], FP8)
    wvi_sp = dint("wvi_sp", [HSL, 128, DC * SLAB], FP8)
    woi_sp = dint("woi_sp", [128, HC * D], FP8)
    x2i_sp = dint("x2i_sp", [QT, 128, H], BF16)
    if c.debug:
        qa_l = [SO - 128 * (m % QT) for m in range(KT)]
        dbg_ptp = dint("dbg_ptp", [128, int(sum(qa_l))])
        dbg_l = dint("dbg_l", [128, QT])
        dbg_negM = dint("dbg_negM", [128, SO])
        dbg_ofm = dint("dbg_ofm", [128, DC * SO])
        dbg_ao = dint("dbg_ao", [128, QT * D])

    te, ve, ac, gp, sy = nc.tensor, nc.vector, nc.scalar, nc.gpsimd, nc.sync

    sem_cms = []

    def mksem(name):
        cm = nc.semaphore(name)
        h = cm.__enter__()
        sem_cms.append(cm)
        return Sem(h)

    sL = mksem("sL")
    sX = [mksem("sX0"), mksem("sX1")]
    sW = [mksem("sW0"), mksem("sW1")]
    sH = [mksem("sH0"), mksem("sH1")]
    sU = [mksem("sU0"), mksem("sU1")]
    sK = mksem("sK")
    sWS = mksem("sWS")
    sQW = [mksem("sQW0"), mksem("sQW1"), mksem("sQW2"), mksem("sQW3")]
    sQZ = [mksem("sQZ0"), mksem("sQZ1"), mksem("sQZ2"), mksem("sQZ3")]
    sA5 = mksem("sA5")
    sV5 = mksem("sV5")
    sS = mksem("sS")
    sP = mksem("sP")
    sP6 = [mksem("sP6a"), mksem("sP6b")]
    sKV = mksem("sKV")
    sT = mksem("sT")
    sV = mksem("sV")
    sA = mksem("sA")
    sG = mksem("sG")
    sO = mksem("sO")
    sB = mksem("sB")

    def barrier(full=True):
        drains = [sS, sL, sO, sK, sWS, sKV, sX[0], sX[1], sW[0], sW[1],
                  sH[0], sH[1], sU[0], sU[1]]
        if full:
            drains += [sP] + sP6 + sQW + sQZ
        for s_ in drains:
            sy.wait_ge(s_.h, s_.n)
        gp.wait_ge(sG.h, sG.n)
        engs = [te, ve, ac, gp, sy]
        for e in engs:
            sB.inc(e.drain() if e is not sy else e.nop())
        for e in engs:
            e.wait_ge(sB.h, sB.n)

    live = {}

    def alloc(name, sh, dt=F32):
        cm = nc.sbuf_tensor(name, sh, dt)
        h = cm.__enter__()
        live[name] = cm
        return h

    def alloc_r(name, sh, dt=F32):
        cm = nc.sbuf_tensor(name, sh, dt, side="right")
        h = cm.__enter__()
        live[name] = cm
        return h

    def free(*names):
        for n in names:
            live.pop(n).__exit__(None, None, None)

    psum_live = {}

    def palloc(name, sh, dt=F32):
        cm = nc.psum_tensor(name, sh, dt)
        h = cm.__enter__()
        psum_live[name] = cm
        return h

    def pfree(*names):
        for n in names:
            psum_live.pop(n).__exit__(None, None, None)

    ones = alloc("ones", [128, 1])
    ones16 = alloc("ones16", [128, 1], BF16)
    ones_row = alloc("ones_row", [1, 128])
    eye32 = alloc("eye32s", [128, 128])
    eye16 = alloc("eye16s", [128, 128], BF16)
    mask_own = alloc("mask_own_s", [128, 128])
    mask_peer = alloc("mask_peer_s", [128, 128])
    nstat = 6 * ((D + 511) // 512)
    stat = alloc("stat", [128, nstat])
    agg = alloc("agg", [128, 2])
    vep = alloc("vep", [128, 1])
    rr0 = alloc("rr0", [128, 1])
    tt0 = alloc("tt0", [128, 1])
    tt1 = alloc("tt1", [128, 1])
    recip_l = alloc("recip_l", [128, QT])
    gxcol = alloc("gxcol", [128, QT])
    ggvcol = alloc("ggvcol", [128, QT])
    sc_x = alloc("sc_x", [128, QT])
    sc_gv = alloc("sc_gv", [128, QT])
    scg_t = alloc("scg_t", [128, QT])
    scv_t = alloc("scv_t", [128, QT])
    sc3 = alloc("sc3", [128, QT])
    gwpart = alloc("gwpart", [128, 2])
    chalf = alloc("chalf", [128, 2])
    gwsum = alloc("gwsum", [128, 3])
    gwtot_sb = alloc("gwtot_sb", [1, 3])
    gwrec_sb = alloc("gwrec_sb", [1, 3])
    rgam = alloc("rgam", [128, 3])
    gamw = alloc("gamw", [128, 3])
    negM = alloc("negM", [128, SO])
    qn_row = alloc("qn_row", [1, SO])
    maxkn = alloc("maxkn", [1, 1])
    negMr = alloc("negMr", [1, SO])

    def ln_stats(x_ap, nfree):
        nchunks = (nfree + 511) // 512
        for i in range(nchunks):
            lo = i * 512
            hi = min(nfree, lo + 512)
            ve.bn_stats(stat[:, 6 * i:6 * (i + 1)], x_ap[:, lo:hi])
        ve.drain()
        ve.bn_aggr(agg[:], stat[:, :6 * nchunks])
        ve.drain()

    def ln_rstd():
        ve.tensor_scalar(vep[:], agg[:, 1:2], EPS, None, op0=ALU.add)
        sV.inc(ve.drain())
        ac.wait_ge(sV.h, sV.n)
        sA.inc(ac.activation(tt0[:], vep[:], ACTF.Sqrt))
        ve.wait_ge(sA.h, sA.n)
        ve.reciprocal(rr0[:], tt0[:])
        for _ in range(2):
            ve.drain()
            ve.tensor_tensor(tt0[:], rr0[:], rr0[:], op=ALU.mult)
            ve.drain()
            ve.tensor_tensor(tt0[:], tt0[:], vep[:], op=ALU.mult)
            ve.drain()
            ve.tensor_scalar(tt0[:], tt0[:], -0.5, 1.5, op0=ALU.mult, op1=ALU.add)
            ve.drain()
            ve.tensor_tensor(rr0[:], rr0[:], tt0[:], op=ALU.mult)
        ve.drain()
        ve.tensor_scalar(tt1[:], agg[:, 0:1], -1.0, rr0[:], op0=ALU.mult, op1=ALU.mult)
        sV.inc(ve.drain())

    def ln_apply(dst, src):
        ac.wait_ge(sV.h, sV.n)
        ac.drain()
        sA.inc(ac.activation(dst, src, ACTF.Identity, bias=tt1[:], scale=rr0[:]))

    def bcast(dst, src_row, n, ps, war_cell=None):
        for lo in range(0, n, 512):
            w = min(512, n - lo)
            te.wait_ge(sA.h, sA.n if war_cell is None else war_cell[0])
            sT.inc(te.matmul(ps[:, :w], ones_row[:1, :], src_row[:1, lo:lo + w],
                             start=True, stop=True))
            ac.wait_ge(sT.h, sT.n)
            sA.inc(ac.copy(dst[:, lo:lo + w], ps[:, :w]))
            if war_cell is not None:
                war_cell[0] = sA.n
        return sT.n

    def round_ladder(f_ap, sc_ap, out_ap, w_scr, u_scr, p_scr):
        ve.tensor_scalar(w_scr, f_ap, sc_ap, 128.5, op0=ALU.mult, op1=ALU.add)
        ve.drain()
        ve.tensor_copy(u_scr, w_scr)
        ve.drain()
        for j in range(7, -1, -1):
            k = float(1 << j)
            ve.tensor_scalar(p_scr, u_scr, k, None, op0=ALU.is_ge)
            ve.drain()
            ve.scalar_tensor_tensor(u_scr, p_scr, -k, u_scr, op0=ALU.mult, op1=ALU.add)
            ve.drain()
        ve.scalar_tensor_tensor(p_scr, u_scr, -1.0, w_scr, op0=ALU.mult, op1=ALU.add)
        ve.drain()
        ve.tensor_scalar(out_ap, p_scr, -128.0, None, op0=ALU.add)
        ve.drain()

    WCW = 512
    wqb = alloc_r("wqb", [128, 4 * WCW])
    wpb = alloc_r("wpb", [128, 2 * WCW])
    wmb = alloc_r("wmb", [128, 2 * WCW])
    wib = alloc_r("wib", [128, 2 * WCW], FP8)
    NCG = H // WCW

    def wchunks(src_hbm, mat_sp):
        out = []
        for dc in range(DC):
            for cg in range(NCG):
                def store(wib_, sem, dc=dc, cg=cg):
                    s_lo = (cg * WCW) // SLAB
                    s_hi = ((cg + 1) * WCW) // SLAB
                    for s in range(s_lo, s_hi):
                        off = s * SLAB - cg * WCW
                        sem.inc(gp.dma_start(
                            out=mat_sp[s][:, dc * SLAB:(dc + 1) * SLAB],
                            in_=wib_[:, off:off + SLAB]), 16)
                out.append((src_hbm[dc * 128:(dc + 1) * 128, cg * WCW:(cg + 1) * WCW],
                            WCW, store))
        return out

    def owt_chunks():
        out = []
        for ch in range(HC):
            for hf in range(D // WCW):
                def store(wib_, sem, ch=ch, hf=hf):
                    sem.inc(gp.dma_start(
                        out=woi_sp[:, ch * D + hf * WCW: ch * D + (hf + 1) * WCW],
                        in_=wib_[:, :WCW]), 16)
                out.append((owT_hbm[ch * 128:(ch + 1) * 128,
                                    hf * WCW:(hf + 1) * WCW], WCW, store))
        return out

    wq_us = [0, 0, 0, 0]

    class Pump:
        DEPTH = 4

        def __init__(self, stages):
            self.items = []
            self.nslot = 0
            for compute, idx, chunks in stages:
                if callable(chunks):
                    self.items.append((compute, idx, False, None, None))
                    continue
                for ci, ck in enumerate(chunks):
                    self.items.append((compute, idx, ci == 0, ck, self.nslot))
                    self.nslot += 1
            self.n = len(self.items)
            self.cursor = 0
            self.loaded = 0
            self.lds = {}

        def _load(self):
            i = self.loaded
            item = self.items[i]
            if item[3] is not None:
                compute, idx, first, (src_, ncols, store_fn), sl = item
                b = sl % 4
                if compute is gamma_compute:
                    qeng, sem = ac, sQW[b]
                else:
                    qeng, sem = gp, sQZ[b]
                qeng.wait_ge(sA5.h, wq_us[b])
                tok = sem.inc(
                    qeng.dma_start(out=wqb[:, b * WCW:b * WCW + ncols], in_=src_), 16)
                self.lds[i] = (sem, tok)
            self.loaded += 1

        def pump(self, k):
            while k > 0 and self.cursor < self.n:
                while self.loaded < min(self.cursor + self.DEPTH, self.n):
                    self._load()
                compute, idx, first, ck, sl = self.items[self.cursor]
                if ck is None:
                    compute(idx)
                else:
                    self.compute_chunk(compute, self.cursor, idx, first, ck, sl)
                self.cursor += 1
                k -= 1

        def compute_chunk(self, compute, i, idx, first, ck, sl):
            src_, ncols, store_fn = ck
            compute(self, i, idx, first, ncols, store_fn, sl)

        def drain(self):
            self.pump(self.n)

    def gamma_compute(pp, i, idx, first, ncols, store_fn, sl):
        b = sl % 4
        wb_ = wqb[:, b * WCW:b * WCW + ncols]
        if first:
            ve.wait_ge(sA5.h, sA5.n)
            ve.memset(gwsum[:, idx:idx + 1], 0.0)
            sV5.inc(ve.drain())
            ac.wait_ge(sV5.h, sV5.n)
        if not hasattr(pp, "gv_tok"):
            pp.gv_tok = [0, 0]
        h2 = sl % 2
        ldsem, ldtok = pp.lds[i]
        ac.wait_ge(ldsem.h, ldtok)
        if pp.gv_tok[h2]:
            ac.wait_ge(sV5.h, pp.gv_tok[h2])
        if getattr(pp, "sg_tok", None) and pp.sg_tok[h2]:
            ac.wait_ge(sG.h, pp.sg_tok[h2])
        ac.drain()
        sA5.inc(ac.activation(wpb[:, h2 * WCW:h2 * WCW + ncols], wb_,
                              ACTF.Abs, accum_out=gwpart[:, h2:h2 + 1]))
        wq_us[b] = sA5.n
        ve.wait_ge(sA5.h, sA5.n)
        ve.tensor_tensor(gwsum[:, idx:idx + 1], gwsum[:, idx:idx + 1],
                         gwpart[:, h2:h2 + 1], op=ALU.add)
        pp.gv_tok[h2] = sV5.inc(ve.drain())

    def quant_compute(pp, i, idx, first, ncols, store_fn, sl):
        b = sl % 4
        h2 = sl % 2
        wb_ = wqb[:, b * WCW:b * WCW + ncols]
        wpb_ = wpb[:, h2 * WCW:h2 * WCW + ncols]
        wmb_ = wmb[:, h2 * WCW:h2 * WCW + ncols]
        wib_ = wib[:, h2 * WCW:h2 * WCW + ncols]
        if not hasattr(pp, "sg_tok"):
            pp.sg_tok = [0, 0]
            pp.st_tok = [0, 0]
        ldsem, ldtok = pp.lds[i]
        ac.wait_ge(ldsem.h, ldtok)
        if pp.sg_tok[h2]:
            ac.wait_ge(sG.h, pp.sg_tok[h2])
        sA5.inc(ac.activation(wpb_, wb_, ACTF.Sign,
                              bias=chalf[:, 0:1], scale=rgam[:, idx:idx + 1]))
        sA5.inc(ac.activation(wmb_, wb_, ACTF.Sign,
                              bias=chalf[:, 1:2], scale=rgam[:, idx:idx + 1]))
        wq_us[b] = sA5.n
        gp.wait_ge(sA5.h, sA5.n)
        if pp.st_tok[h2]:
            gp.wait_ge(sP6[h2].h, pp.st_tok[h2])
        pp.sg_tok[h2] = sG.inc(gp.tensor_tensor(wib_, wpb_, wmb_, op=ALU.add))
        gp.wait_ge(sG.h, pp.sg_tok[h2])
        store_fn(wib[:, h2 * WCW:(h2 + 1) * WCW], sP6[h2])
        pp.st_tok[h2] = sP6[h2].n

    def wquant_bcast(idx):
        n_elems = D * H if idx < 2 else H * D
        te.wait_ge(sV5.h, sV5.n)
        gps = palloc("gwps" + str(idx), [1, 1])
        mm_tok = sT.inc(te.matmul(gps[:1, :1], gwsum[:, idx:idx + 1], ones[:],
                                  start=True, stop=True))
        ve.wait_ge(sT.h, mm_tok)
        ve.tensor_scalar(gwtot_sb[:1, idx:idx + 1], gps[:1, :1], 1.0 / float(n_elems),
                         1e-5, op0=ALU.mult, op1=ALU.max)
        ve.drain()
        ve.reciprocal(gwrec_sb[:1, idx:idx + 1], gwtot_sb[:1, idx:idx + 1])
        sV5.inc(ve.drain())
        pfree("gwps" + str(idx))
        bc_psw = palloc("bc_psw" + str(idx), [128, 512])
        te.wait_ge(sV5.h, sV5.n)
        cell = [0]
        bcast(rgam[:, idx:idx + 1], gwrec_sb[:1, idx:idx + 1], 1, bc_psw,
              war_cell=cell)
        bcast(gamw[:, idx:idx + 1], gwtot_sb[:1, idx:idx + 1], 1, bc_psw,
              war_cell=cell)
        ac.drain()
        pfree("bc_psw" + str(idx))

    wpump = Pump([
        (gamma_compute, 0, wchunks(gwT_hbm, wgi_sp)),
        (wquant_bcast, 0, wquant_bcast),
        (gamma_compute, 1, wchunks(vwT_hbm, wvi_sp)),
        (wquant_bcast, 1, wquant_bcast),
        (gamma_compute, 2, owt_chunks()),
        (wquant_bcast, 2, wquant_bcast),
        (quant_compute, 0, wchunks(gwT_hbm, wgi_sp)),
        (quant_compute, 1, wchunks(vwT_hbm, wvi_sp)),
        (quant_compute, 2, owt_chunks()),
    ])

    gam_b = alloc("gam_b", [128, D])
    bet_b = alloc("bet_b", [128, D])
    sL.inc(sy.dma_start(out=eye32[:], in_=eye32_hbm), 16)
    sL.inc(sy.dma_start(out=eye16[:], in_=eye16_hbm), 16)
    sL.inc(sy.dma_start(out=mask_own[:], in_=mko_hbm), 16)
    sL.inc(sy.dma_start(out=mask_peer[:], in_=mkp_hbm), 16)
    sL.inc(sy.dma_start(out=gam_b[:1, :], in_=gamma_hbm[None, :]), 16)
    sL.inc(sy.dma_start(out=bet_b[:1, :], in_=beta_hbm[None, :]), 16)
    ve.memset(ones[:], 1.0)
    ve.memset(ones16[:], 1.0)
    ve.memset(ones_row[:], 1.0)
    ve.memset(chalf[:, 0:1], -0.5)
    ve.memset(chalf[:, 1:2], 0.5)
    sV.inc(ve.drain())
    bc_ps0 = palloc("bc_ps0", [128, 512])
    te.wait_ge(sL.h, sL.n)
    te.wait_ge(sV.h, sV.n)
    bcast(gam_b, gam_b[:1, :], D, bc_ps0)
    bcast(bet_b, bet_b[:1, :], D, bc_ps0)
    pfree("bc_ps0")

    hnf = alloc("hnf", [128, DC * S])
    xt = alloc("xt", [128, 2 * D])
    ptr = palloc("ptr", [128, 1024])
    ve.wait_ge(sA.h, sA.n)
    evac_tok = [0, 0]
    tile_done = [0, 0]
    for i in range(KT):
        b = i % 2
        xb = xt[:, b * D:b * D + D]
        sy.wait_ge(sT.h, tile_done[b])
        ld = sX[b].inc(sy.dma_start(out=xb, in_=x_hbm[i * 128:(i + 1) * 128, :]), 16)
        ve.wait_ge(sX[b].h, ld)
        ln_stats(xb, D)
        ln_rstd()
        ln_apply(xb, xb)
        if i < QT:
            sy.wait_ge(sA.h, sA.n)
            sS.inc(sy.dma_start(out=h_sp[i * 128:(i + 1) * 128, :], in_=xb), 16)
        ve.wait_ge(sA.h, sA.n)
        ln_stats(xb, D)
        ln_rstd()
        ac.wait_ge(sS.h, sS.n)
        ln_apply(xb, xb)
        ve.wait_ge(sA.h, sA.n)
        ve.tensor_tensor(xb, xb, gam_b[:], op=ALU.mult)
        ve.drain()
        ve.tensor_tensor(xb, xb, bet_b[:], op=ALU.add)
        sV.inc(ve.drain())
        te.wait_ge(sV.h, sV.n)
        for dc in range(DC):
            pb = ptr[:, (dc % 2) * 512:(dc % 2) * 512 + 128]
            te.wait_ge(sA.h, evac_tok[dc % 2])
            sT.inc(te.transpose(pb, xb[:, dc * 128:(dc + 1) * 128], eye32[:]))
            ac.wait_ge(sT.h, sT.n)
            dst = hnf[:, dc * S + i * 128: dc * S + i * 128 + 128]
            evac_tok[dc % 2] = sA.inc(ac.copy(dst, pb))
            gp.wait_ge(sA.h, sA.n)
            sKV.inc(gp.dma_start(out=hn_sp[i][:, dc * 128:(dc + 1) * 128],
                                 in_=dst), 16)
        tile_done[b] = sT.n
    free("xt")
    pfree("ptr")
    barrier()

    wb = alloc("wb", [128, 2 * DC * 128])
    ksb = alloc("ksb", [128, S])
    ksq = alloc("ksq", [128, SLAB])
    nsb = alloc("nsb", [1, S], BF16)
    kp = palloc("kp", [128, S])
    npp = palloc("npsum", [1, 2 * SLAB])

    def proj(base_col, ntok, store, pump_hook=lambda: None):
        NG = ntok // SLAB
        wtok = [0, 0]
        copy_tok = [0]
        for o in range(DC):
            ob = base_col // 128 + o
            bb = o % 2
            wbo = wb[:, bb * DC * 128:(bb + 1) * DC * 128]
            sy.wait_ge(sT.h, wtok[bb])
            ld = sW[bb].inc(sy.dma_start(
                out=wbo, in_=qkvc_hbm[:, ob * DC * 128:(ob + 1) * DC * 128]), 16)
            te.wait_ge(sW[bb].h, ld)
            te.wait_ge(sV.h, copy_tok[0])
            for dc in range(DC):
                wbb = wbo[:, dc * 128:(dc + 1) * 128]
                for g in range(NG):
                    hsrc = hnf[:, dc * S + g * SLAB: dc * S + (g + 1) * SLAB]
                    sT.inc(te.matmul(kp[:, g * SLAB:(g + 1) * SLAB], wbb, hsrc,
                                     start=(dc == 0), stop=(dc == DC - 1)))
            wtok[bb] = sT.n
            ve.wait_ge(sT.h, sT.n)
            ve.wait_ge(sS.h, sS.n)
            ve.wait_ge(sKV.h, sKV.n)
            ve.tensor_copy(ksb[:, :ntok], kp[:, :ntok])
            copy_tok[0] = sV.inc(ve.drain())
            sy.wait_ge(sV.h, sV.n)
            gp.wait_ge(sV.h, sV.n)
            store(o)
            if o == 0:
                ve.memset(nsb[:1, :ntok], 0.0)
                sV.inc(ve.drain())
            for g in range(NG):
                gb_ = g % 2
                ve.wait_ge(sT.h, sT.n)
                ve.tensor_tensor(ksq[:], ksb[:, g * SLAB:(g + 1) * SLAB],
                                 ksb[:, g * SLAB:(g + 1) * SLAB], op=ALU.mult)
                sV.inc(ve.drain())
                te.wait_ge(sV.h, sV.n)
                nmm = sT.inc(te.matmul(npp[:1, gb_ * SLAB:(gb_ + 1) * SLAB],
                                       ones[:], ksq[:], start=True, stop=True))
                ve.wait_ge(sT.h, nmm)
                ve.tensor_tensor(nsb[:1, g * SLAB:(g + 1) * SLAB],
                                 nsb[:1, g * SLAB:(g + 1) * SLAB],
                                 npp[:1, gb_ * SLAB:(gb_ + 1) * SLAB], op=ALU.add)
                sV.inc(ve.drain())
            pump_hook()

    def store_k(o):
        sKV.inc(gp.dma_start(out=k_sp[:, :, o * 128:(o + 1) * 128],
                             in_=ksb[:, :S]), 16)

    proj(D, S, store_k, lambda: wpump.pump(25))
    ve.wait_ge(sT.h, sT.n)
    ve.tensor_reduce(maxkn[:], nsb[:1, :S], axis=AX.X, op=ALU.max)
    ve.drain()
    ve.tensor_scalar(maxkn[:], maxkn[:], 1e-30, None, op0=ALU.max)
    sV.inc(ve.drain())

    def store_q(o):
        sS.inc(sy.dma_start(out=q_sp[o], in_=ksb[:, :SO]), 16)

    proj(0, SO, store_q, lambda: wpump.pump(25))
    ve.wait_ge(sT.h, sT.n)
    ve.tensor_copy(qn_row[:], nsb[:1, :SO])
    ve.drain()
    ve.tensor_scalar(qn_row[:], qn_row[:], 1e-30, None, op0=ALU.max)
    sV.inc(ve.drain())
    ac.wait_ge(sV.h, sV.n)
    sA.inc(ac.activation(qn_row[:], qn_row[:], ACTF.Sqrt))
    sA.inc(ac.activation(maxkn[:], maxkn[:], ACTF.Sqrt))
    ve.wait_ge(sA.h, sA.n)
    ve.tensor_scalar(negMr[:], qn_row[:], maxkn[:1, :1], -1.0, op0=ALU.mult, op1=ALU.mult)
    sV.inc(ve.drain())
    pfree("npsum", "kp")
    free("nsb")
    bc_ps1 = palloc("bc_ps1", [128, 512])
    te.wait_ge(sV.h, sV.n)
    bcast(negM, negMr, SO, bc_ps1)
    pfree("bc_ps1")
    free("ksq", "ksb", "wb")
    free("hnf", "bet_b", "gam_b")
    free("negMr", "maxkn", "qn_row")
    b1_t, b1_v, b1_a = sT.n, sV.n, sA.n

    wv_sb = alloc("wv_sb", [128, DC * D])
    hbuf = alloc("hbuf", [128, 2 * DC * 128])
    vsb = alloc("vsb", [128, D])
    vp = palloc("vp", [128, 1024])
    sy.wait_ge(sT.h, b1_t)
    sy.wait_ge(sV.h, b1_v)
    ac.wait_ge(sT.h, b1_t)
    ac.wait_ge(sV.h, b1_v)
    for dc in range(DC):
        if dc % 2 == 0:
            sL.inc(sy.dma_start(out=wv_sb[:, dc * D:(dc + 1) * D],
                                in_=qkv_hbm[dc * 128:(dc + 1) * 128, 2 * D:3 * D]), 16)
        else:
            sWS.inc(ac.dma_start(out=wv_sb[:, dc * D:(dc + 1) * D],
                                 in_=qkv_hbm[dc * 128:(dc + 1) * 128, 2 * D:3 * D]), 16)
    te.wait_ge(sL.h, sL.n)
    te.wait_ge(sWS.h, sWS.n)
    te.wait_ge(sA.h, b1_a)
    te.wait_ge(sV.h, b1_v)
    huse = [0, 0]
    vevac = [0, 0]
    for m in range(KT):
        b = m % 2
        hb = hbuf[:, b * DC * 128:(b + 1) * DC * 128]
        sy.wait_ge(sT.h, huse[b])
        sH[b].inc(sy.dma_start(out=hb, in_=hn_sp[m]), 16)
        te.wait_ge(sH[b].h, sH[b].n)
        if m > 0:
            te.wait_ge(sKV.h, sKV.n)
        for ds in range(DSL):
            pbk = vp[:, (ds % 2) * 512:(ds % 2) * 512 + SLAB]
            te.wait_ge(sV.h, vevac[ds % 2])
            for dc in range(DC):
                sT.inc(te.matmul(pbk, hb[:, dc * 128:(dc + 1) * 128],
                                 wv_sb[:, dc * D + ds * SLAB: dc * D + (ds + 1) * SLAB],
                                 start=(dc == 0), stop=(dc == DC - 1)))
            ve.wait_ge(sT.h, sT.n)
            ve.tensor_copy(vsb[:, ds * SLAB:(ds + 1) * SLAB], pbk)
            vevac[ds % 2] = sV.inc(ve.drain())
        huse[b] = sT.n
        gp.wait_ge(sV.h, sV.n)
        sKV.inc(gp.dma_start(out=v_sp[:, :, m * 128:(m + 1) * 128], in_=vsb[:]), 16)
        wpump.pump(24)
    free("vsb", "hbuf", "wv_sb")
    pfree("vp")
    b2_t, b2_v = sT.n, sV.n

    ofm = alloc("ofm", [128, DC * SO])
    _qa_list = [SO - 128 * (m % QT) for m in range(KT)]
    ptp = alloc("ptp", [128, int(sum(_qa_list))])
    qres = alloc("qres", [128, DC * SO])
    kbuf = alloc("kbuf", [128, 2 * DC * 128])
    qa_list = _qa_list
    PTW = int(sum(qa_list))
    poff = np.cumsum([0] + qa_list)[:-1].tolist()
    sp_ps = palloc("sp_ps", [128, 2 * SO])
    l_ps = palloc("l_ps", [128, QT])
    sy.wait_ge(sT.h, b2_t)
    sy.wait_ge(sV.h, b2_v)
    sy.wait_ge(sS.h, sS.n)
    sy.wait_ge(sKV.h, sKV.n)
    for o in range(DC):
        sL.inc(sy.dma_start(out=qres[:, o * SO:(o + 1) * SO], in_=q_sp[o]), 16)
    te.wait_ge(sL.h, sL.n)
    te.wait_ge(sV.h, b2_v)
    kuse = [0, 0]
    pset_tok = [0, 0]
    ldk_tok = [0, 0]
    exp_tok = [0]
    first_l = True
    ldk_tok[0] = sX[0].inc(sy.dma_start(out=kbuf[:, :DC * 128], in_=k_sp[:, 0, :]), 16)
    for m in range(KT):
        b = m % 2
        kb = kbuf[:, b * DC * 128:(b + 1) * DC * 128]
        if m + 1 < KT:
            b2 = (m + 1) % 2
            sy.wait_ge(sT.h, kuse[b2])
            ldk_tok[b2] = sX[b2].inc(sy.dma_start(
                out=kbuf[:, b2 * DC * 128:(b2 + 1) * DC * 128],
                in_=k_sp[:, m + 1, :]), 16)
        qoff = 128 * (m % QT)
        qa = SO - qoff
        te.wait_ge(sX[b].h, ldk_tok[b])
        te.wait_ge(sV.h, pset_tok[b])
        base = b * SO
        nseg = (qa + SLAB - 1) // SLAB
        for seg in range(nseg):
            lo = seg * SLAB
            hi = min(qa, lo + SLAB)
            for dc in range(DC):
                sT.inc(te.matmul(
                    sp_ps[:, base + lo:base + hi],
                    kb[:, dc * 128:(dc + 1) * 128],
                    qres[:, dc * SO + qoff + lo: dc * SO + qoff + hi],
                    start=(dc == 0), stop=(dc == DC - 1)))
        kuse[b] = sT.n
        wpump.pump(5)
        pm = ptp[:, poff[m]:poff[m] + qa]
        ve.wait_ge(sT.h, sT.n)
        ve.tensor_tensor(pm, sp_ps[:, base:base + qa], negM[:, qoff:SO], op=ALU.add)
        sV.inc(ve.drain())
        pset_tok[b] = sV.n
        mk = mask_own if m < QT else mask_peer
        ve.tensor_tensor(ptp[:, poff[m]:poff[m] + 128],
                         ptp[:, poff[m]:poff[m] + 128], mk[:], op=ALU.add)
        sV.inc(ve.drain())
        ac.wait_ge(sV.h, sV.n)
        exp_tok[0] = sA.inc(ac.activation(pm, pm, ACTF.Exp, scale=c.scale))
        te.wait_ge(sA.h, sA.n)
        for jj in range(qa // 128):
            col = qoff // 128 + jj
            sT.inc(te.matmul(l_ps[:, col:col + 1],
                             ptp[:, poff[m] + jj * 128: poff[m] + (jj + 1) * 128],
                             ones[:], start=first_l, stop=(m == KT - 1),
                             skip_group_check=True))
            first_l = False
        wpump.pump(5)
    ve.wait_ge(sT.h, sT.n)
    ve.reciprocal(recip_l[:], l_ps[:])
    sV.inc(ve.drain())
    if c.debug:
        sy.wait_ge(sV.h, sV.n)
        sy.wait_ge(sA.h, sA.n)
        sS.inc(sy.dma_start(out=dbg_ptp[:], in_=ptp[:]), 16)
        sS.inc(sy.dma_start(out=dbg_l[:], in_=recip_l[:]), 16)
        sS.inc(sy.dma_start(out=dbg_negM[:], in_=negM[:]), 16)
    free("kbuf", "qres")
    pfree("l_ps", "sp_ps")
    b3_t, b3_v, b3_a = sT.n, sV.n, exp_tok[0]

    vbuf = alloc("vbuf", [128, 2 * KT * 128])
    o_ps = palloc("o_ps", [128, 2 * SO])
    vuse = [0, 0]
    oset_tok = [0, 0]
    sy.wait_ge(sT.h, b3_t)
    te.wait_ge(sV.h, b3_v)
    te.wait_ge(sA.h, b3_a)
    for cc in range(DC):
        b = cc % 2
        vb = vbuf[:, b * KT * 128:(b + 1) * KT * 128]
        sy.wait_ge(sT.h, vuse[b])
        ld = sU[b].inc(sy.dma_start(out=vb, in_=v_sp[:, cc, :]), 16)
        te.wait_ge(sU[b].h, ld)
        te.wait_ge(sV.h, oset_tok[b])
        base = b * SO
        for m in range(KT):
            qoff = 128 * (m % QT)
            for seg in range(OSL):
                lo = max(seg * SLAB, qoff)
                hi = (seg + 1) * SLAB
                if lo >= hi:
                    continue
                sT.inc(te.matmul(
                    o_ps[:, base + lo:base + hi],
                    vb[:, m * 128:(m + 1) * 128],
                    ptp[:, poff[m] + lo - qoff: poff[m] + hi - qoff],
                    start=(m == 0), stop=(m == KT - 1), skip_group_check=True))
        vuse[b] = sT.n
        ve.wait_ge(sT.h, sT.n)
        ve.tensor_copy(ofm[:, cc * SO:(cc + 1) * SO], o_ps[:, base:base + SO])
        oset_tok[b] = sV.inc(ve.drain())
        wpump.pump(20)
    if c.debug:
        sy.wait_ge(sA.h, sA.n)
        sS.inc(sy.dma_start(out=dbg_ofm[:], in_=ofm[:]), 16)
    free("vbuf")
    free("ptp")
    pfree("o_ps")
    b4_t, b4_v = sT.n, sV.n

    ao = alloc("ao", [128, QT * D])
    wos = alloc("wos", [128, DC * SLAB])
    ao_ps = palloc("ao_ps", [128, 1024])
    aevac = [0, 0]
    sy.wait_ge(sT.h, b4_t)
    te.wait_ge(sV.h, b4_v)
    for ds in range(DSL):
        te.wait_ge(sT.h, sT.n)
        sy.wait_ge(sT.h, sT.n)
        sWS.inc(sy.dma_start(
            out=wos[:], in_=wob_hbm[:, ds * DC * SLAB:(ds + 1) * DC * SLAB]), 16)
        te.wait_ge(sWS.h, sWS.n)
        for qc in range(QT):
            bk = qc % 2
            pbk = ao_ps[:, bk * 512:bk * 512 + SLAB]
            te.wait_ge(sV.h, aevac[bk])
            for dc in range(DC):
                sT.inc(te.matmul(pbk, ofm[:, dc * SO + qc * 128: dc * SO + (qc + 1) * 128],
                                 wos[:, dc * SLAB:(dc + 1) * SLAB],
                                 start=(dc == 0), stop=(dc == DC - 1)))
            ve.wait_ge(sT.h, sT.n)
            ve.tensor_scalar(ao[:, qc * D + ds * SLAB: qc * D + (ds + 1) * SLAB],
                             pbk, recip_l[:, qc:qc + 1], None, op0=ALU.mult)
            aevac[bk] = sV.inc(ve.drain())
    if c.debug:
        sy.wait_ge(sV.h, sV.n)
        sS.inc(sy.dma_start(out=dbg_ao[:], in_=ao[:]), 16)
        barrier()
    free("wos")
    pfree("ao_ps")
    hbt = alloc("hbt", [128, D])
    xbt = alloc("xbt", [128, D])
    z2t = alloc("z2t", [128, D])
    for qc in range(QT):
        sy.wait_ge(sV.h, sV.n)
        sL.inc(sy.dma_start(out=hbt[:], in_=h_sp[qc * 128:(qc + 1) * 128, :]), 16)
        ld = sL.inc(sy.dma_start(out=xbt[:], in_=x_hbm[qc * 128:(qc + 1) * 128, :]), 16)
        at = ao[:, qc * D:(qc + 1) * D]
        ve.wait_ge(sL.h, ld)
        ve.tensor_tensor(at, at, hbt[:], op=ALU.add)
        ve.drain()
        ln_stats(at, D)
        ln_rstd()
        ac.wait_ge(sS.h, sS.n)
        ln_apply(z2t[:], at)
        ve.wait_ge(sA.h, sA.n)
        ve.tensor_tensor(z2t[:], z2t[:], xbt[:], op=ALU.add)
        sV.inc(ve.drain())
        sy.wait_ge(sV.h, sV.n)
        sS.inc(sy.dma_start(out=x1_sp[qc * 128:(qc + 1) * 128, :], in_=z2t[:]), 16)
    wpump.drain()
    free("z2t", "xbt", "hbt", "ao", "ofm")
    free("wib", "wmb", "wpb", "wqb")
    b5_t, b5_v, b5_a = sT.n, sV.n, sA.n

    free("negM")
    NSLOT = 3
    NH4 = c.NH4
    xiT = alloc("xiT", [128, DC * SO], BF16)
    ft = alloc("ft", [128, D])
    rs1 = alloc("rs1", [128, 1024])
    xit = alloc("xit", [128, D], BF16)
    wgsl = alloc("wgsl", [128, 2 * DC * SLAB], FP8)
    wvsl = alloc("wvsl", [128, 2 * DC * SLAB], FP8)
    gbb = alloc("gbb", [128, SLAB])
    vbb = alloc("vbb", [128, SLAB])
    gsl = alloc("gsl", [128, 2 * SLAB])
    vsl = alloc("vsl", [128, SLAB])
    sgn = alloc("sgn", [128, 4 * c.W6])
    ubuf = alloc("ubuf", [128, 4 * c.W6])
    scr16 = alloc("scr16", [128, 2 * c.W6], BF16)
    cm1275 = alloc("cm1275", [128, 1])
    c1285 = alloc("c1285", [128, 1])
    dsc = alloc("dsc", [128, 128])
    gvb = alloc("gvb", [128, NSLOT * H])
    pt16 = palloc("pt16", [128, 2048], BF16)
    gps2 = palloc("gps2", [128, 1024])
    vps2 = palloc("vps2", [128, 1024])
    ptq = palloc("ptq", [128, 1024])
    ve.memset(cm1275[:], -127.5)
    ve.memset(c1285[:], 128.5)
    ve.memset(ggvcol[:], 0.0)
    sV.inc(ve.drain())
    ac.wait_ge(sV.h, sV.n)

    slot_free = [None] * NSLOT
    wu = [0, 0]
    bias_rd = [0]
    silu_tok = [0, 0]
    pfree_tok = [0, 0]
    dsc_war = [0]
    evt6 = [0, 0]
    sgnwar = [0, 0, 0, 0]
    uwar = [0, 0, 0, 0]
    scr_tok = [0, 0]
    evt = [0, 0]
    xa_tok = [0] * QT
    ft_free = [0]

    def p6aA(qc):
        sy.wait_ge(sV.h, ft_free[0])
        sy.wait_ge(sS.h, sS.n)
        ld = sX[0].inc(sy.dma_start(out=ft[:], in_=x1_sp[qc * 128:(qc + 1) * 128, :]), 16)
        ve.wait_ge(sX[0].h, ld)
        ln_stats(ft[:], D)
        ln_rstd()
        ln_apply(ft[:], ft[:])
        ve.wait_ge(sA.h, sA.n)
        ve.tensor_reduce(gxcol[:, qc:qc + 1], ft[:], axis=AX.X, op=ALU.max,
                         apply_absolute_value=True)
        ve.drain()
        ve.tensor_scalar(gxcol[:, qc:qc + 1], gxcol[:, qc:qc + 1], 1e-5, None, op0=ALU.max)
        ve.drain()
        ve.reciprocal(tt1[:], gxcol[:, qc:qc + 1])
        ve.drain()
        ve.tensor_scalar(sc_x[:, qc:qc + 1], tt1[:], 127.0, None, op0=ALU.mult)
        ve.drain()
        ve.tensor_scalar(tt0[:], gxcol[:, qc:qc + 1], 0.5 / 127.0, None, op0=ALU.mult)
        ve.drain()
        ve.tensor_tensor(scg_t[:, qc:qc + 1], tt0[:], gamw[:, 0:1], op=ALU.mult)
        ve.tensor_tensor(scv_t[:, qc:qc + 1], tt0[:], gamw[:, 1:2], op=ALU.mult)
        sV.inc(ve.drain())

    def p6aB(qc, deep=False):
        HD = min(512, D // 2)
        wtok = sV.inc(ve.tensor_scalar(ft[:], ft[:], sc_x[:, qc:qc + 1], 128.5,
                                       op0=ALU.mult, op1=ALU.add))
        fin_tok = wtok
        if deep:
            nch = D // HD
            ac.wait_ge(sV.h, wtok)
            ac.drain()
            up = {}
            for ci in range(nch):
                sl_ = slice(ci * HD, (ci + 1) * HD)
                if uwar[ci % 4]:
                    ac.wait_ge(sV.h, uwar[ci % 4])
                sA.inc(ac.activation(ubuf[:, sl_], ft[:, sl_],
                                     ACTF.Identity, bias=cm1275[:]))
                up[ci] = sgnwar[ci % 4]
            ac.drain()
            for j in range(8):
                kpow = -float(1 << (6 - j)) if j <= 6 else -0.5
                for ci in range(nch):
                    sl_ = slice(ci * HD, (ci + 1) * HD)
                    if up[ci]:
                        ac.wait_ge(sV.h, up[ci])
                    sA.inc(ac.activation(sgn[:, sl_], ubuf[:, sl_], ACTF.Sign,
                                         bias=chalf[:, 0:1]))
                    ve.wait_ge(sA.h, sA.n)
                    up[ci] = sV.inc(ve.scalar_tensor_tensor(
                        ubuf[:, sl_], sgn[:, sl_], kpow, ubuf[:, sl_],
                        op0=ALU.mult, op1=ALU.add))
            for ci in range(nch):
                sl_ = slice(ci * HD, (ci + 1) * HD)
                sgnwar[ci % 4] = up[ci]
                ve.wait_ge(sV.h, up[ci])
                ve.scalar_tensor_tensor(xit[:, sl_], ubuf[:, sl_], -1.0, ft[:, sl_],
                                        op0=ALU.mult, op1=ALU.add)
                ve.drain()
                ve.tensor_scalar(xit[:, sl_], xit[:, sl_], -128.0, None, op0=ALU.add)
                fin_tok = sV.inc(ve.drain())
                uwar[ci % 4] = fin_tok
            passes = []
        else:
            passes = list(range(D // (2 * HD)))
        for ps_ in passes:
            ac.wait_ge(sV.h, fin_tok)
            ac.drain()
            up = {}
            for ci in range(2):
                sl_ = slice(ps_ * 2 * HD + ci * HD, ps_ * 2 * HD + (ci + 1) * HD)
                sA.inc(ac.activation(rs1[:, ci * HD:(ci + 1) * HD], ft[:, sl_],
                                     ACTF.Identity, bias=cm1275[:]))
                up[ci] = 0
            ac.drain()
            for j in range(8):
                kpow = -float(1 << (6 - j)) if j <= 6 else -0.5
                for ci in range(2):
                    sl_ = slice(ps_ * 2 * HD + ci * HD, ps_ * 2 * HD + (ci + 1) * HD)
                    uch = rs1[:, ci * HD:(ci + 1) * HD]
                    sch = xit[:, sl_]
                    if up[ci]:
                        ac.wait_ge(sV.h, up[ci])
                    sA.inc(ac.activation(sch, uch, ACTF.Sign, bias=chalf[:, 0:1]))
                    ve.wait_ge(sA.h, sA.n)
                    up[ci] = sV.inc(ve.scalar_tensor_tensor(
                        uch, sch, kpow, uch, op0=ALU.mult, op1=ALU.add))
            for ci in range(2):
                sl_ = slice(ps_ * 2 * HD + ci * HD, ps_ * 2 * HD + (ci + 1) * HD)
                ve.wait_ge(sV.h, up[ci])
                ve.scalar_tensor_tensor(xit[:, sl_], rs1[:, ci * HD:(ci + 1) * HD],
                                        -1.0, ft[:, sl_], op0=ALU.mult, op1=ALU.add)
                ve.drain()
                ve.tensor_scalar(xit[:, sl_], xit[:, sl_], -128.0, None, op0=ALU.add)
                fin_tok = sV.inc(ve.drain())
        ft_free[0] = fin_tok
        te.wait_ge(sV.h, ft_free[0])
        for dc in range(DC):
            pb = pt16[:, (dc % 2) * 1024:(dc % 2) * 1024 + 128]
            te.wait_ge(sA.h, evt[dc % 2])
            sT.inc(te.transpose(pb, xit[:, dc * 128:(dc + 1) * 128], eye16[:]))
            ac.wait_ge(sT.h, sT.n)
            evt[dc % 2] = sA.inc(ac.copy(
                xiT[:, dc * SO + qc * 128: dc * SO + (qc + 1) * 128], pb))
        xa_tok[qc] = sA.n

    def produce_gen(qg):
        for s in range(HSL):
            b = s % 2
            wgb = wgsl[:, b * DC * SLAB:(b + 1) * DC * SLAB]
            wvb2 = wvsl[:, b * DC * SLAB:(b + 1) * DC * SLAB]
            if s == 0:
                te.wait_ge(sA.h, max(xa_tok[qg * c.QG], xa_tok[qg * c.QG + 1]))
            sy.wait_ge(sT.h, wu[b])
            sW[b].inc(sy.dma_start(out=wgb[:], in_=wgi_sp[s]), 16)
            sW[b].inc(sy.dma_start(out=wvb2[:], in_=wvi_sp[s]), 16)
            ldw = sW[b].n
            sy.wait_ge(sV.h, bias_rd[0])
            sH[0].inc(sy.dma_start(out=gbb[:],
                                   in_=gbbt_hbm[:, s * SLAB:(s + 1) * SLAB]), 16)
            ld2 = sH[0].inc(sy.dma_start(out=vbb[:],
                                         in_=vbbt_hbm[:, s * SLAB:(s + 1) * SLAB]), 16)
            te.wait_ge(sW[b].h, ldw)
            for j in range(c.QG):
                qc = qg * c.QG + j
                slot = qc % NSLOT
                gvs = gvb[:, slot * H + s * SLAB: slot * H + (s + 1) * SLAB]
                gps_j = gps2[:, j * SLAB:(j + 1) * SLAB]
                vps_j = vps2[:, j * SLAB:(j + 1) * SLAB]
                gsl_j = gsl[:, j * SLAB:(j + 1) * SLAB]
                te.wait_ge(sV.h, pfree_tok[j])
                for dc in range(DC):
                    sT.inc(te.matmul(gps_j,
                                     xiT[:, dc * SO + qc * 128: dc * SO + (qc + 1) * 128],
                                     wgb[:, dc * SLAB:(dc + 1) * SLAB],
                                     start=(dc == 0), stop=(dc == DC - 1)))
                tok_g = sT.n
                for dc in range(DC):
                    sT.inc(te.matmul(vps_j,
                                     xiT[:, dc * SO + qc * 128: dc * SO + (qc + 1) * 128],
                                     wvb2[:, dc * SLAB:(dc + 1) * SLAB],
                                     start=(dc == 0), stop=(dc == DC - 1)))
                tok_v = sT.n
                ve.wait_ge(sT.h, tok_g)
                ve.wait_ge(sH[0].h, ld2)
                ve.wait_ge(sA.h, silu_tok[j])
                ve.scalar_tensor_tensor(gsl_j, gps_j, scg_t[:, qc:qc + 1],
                                        gbb[:], op0=ALU.mult, op1=ALU.add)
                sV.inc(ve.drain())
                ac.wait_ge(sV.h, sV.n)
                silu_tok[j] = sA.inc(ac.activation(gsl_j, gsl_j, ACTF.Silu))
                ve.wait_ge(sT.h, tok_v)
                ve.scalar_tensor_tensor(vsl[:], vps_j, scv_t[:, qc:qc + 1],
                                        vbb[:], op0=ALU.mult, op1=ALU.add)
                pfree_tok[j] = sV.inc(ve.drain())
                ve.wait_ge(sA.h, silu_tok[j])
                if slot_free[slot] is not None:
                    si, stok = slot_free[slot][s]
                    ve.wait_ge(sP6[si].h, stok)
                ve.tensor_tensor(gvs, gsl_j, vsl[:], op=ALU.mult)
                ve.drain()
                ve.tensor_reduce(tt0[:], gvs, axis=AX.X, op=ALU.max,
                                 apply_absolute_value=True)
                ve.drain()
                ve.tensor_tensor(ggvcol[:, qc:qc + 1], ggvcol[:, qc:qc + 1], tt0[:],
                                 op=ALU.max)
                sV.inc(ve.drain())
            bias_rd[0] = sV.n
            wu[b] = sT.n
            yield

    def qA(qc):
        slot = qc % NSLOT
        base = slot * H
        ve.tensor_scalar(ggvcol[:, qc:qc + 1], ggvcol[:, qc:qc + 1], 1e-5, None,
                         op0=ALU.max)
        ve.drain()
        ve.reciprocal(tt1[:], ggvcol[:, qc:qc + 1])
        ve.drain()
        ve.tensor_scalar(sc_gv[:, qc:qc + 1], tt1[:], 127.0, None, op0=ALU.mult)
        ve.drain()
        ve.tensor_scalar(tt0[:], ggvcol[:, qc:qc + 1], 0.5 / 127.0, None, op0=ALU.mult)
        ve.drain()
        ve.tensor_tensor(sc3[:, qc:qc + 1], tt0[:], gamw[:, 2:3], op=ALU.mult)
        ve.drain()
        ve.wait_ge(sT.h, dsc_war[0])
        ve.tensor_scalar(dsc[:], eye32[:], sc_gv[:, qc:qc + 1], None, op0=ALU.mult)
        dsc_tok = sV.inc(ve.drain())
        te.wait_ge(sV.h, dsc_tok)
        for ch in range(HC):
            p = ch % 2
            pb = ptq[:, p * 512:p * 512 + 128]
            blk = gvb[:, base + ch * 128: base + (ch + 1) * 128]
            te.wait_ge(sA.h, evt6[p])
            sT.inc(te.matmul(pb, blk, dsc[:], start=True, stop=True))
            ac.wait_ge(sT.h, sT.n)
            evt6[p] = sA.inc(ac.activation(blk, pb, ACTF.Identity, bias=c1285[:]))
        dsc_war[0] = sT.n
        slot_free[slot] = {}

    def pair_gen(qc):
        slot = qc % NSLOT
        base = slot * H
        W6 = c.W6
        for k0 in range(0, c.NHW, 4):
            pair = (k0, k0 + 1, k0 + 2, k0 + 3)
            upd = {}
            ac.drain()
            for ki in pair:
                i = ki % 4
                wch = gvb[:, base + ki * W6: base + (ki + 1) * W6]
                uch = ubuf[:, i * W6:(i + 1) * W6]
                if uwar[i]:
                    ac.wait_ge(sV.h, uwar[i])
                sA.inc(ac.activation(uch, wch, ACTF.Identity, bias=cm1275[:]))
                upd[ki] = sgnwar[i]
            ac.drain()
            for j in range(8):
                kpow = -float(1 << (6 - j)) if j <= 6 else -0.5
                for ki in pair:
                    i = ki % 4
                    uch = ubuf[:, i * W6:(i + 1) * W6]
                    sch = sgn[:, i * W6:(i + 1) * W6]
                    if upd[ki]:
                        ac.wait_ge(sV.h, upd[ki])
                    sA.inc(ac.activation(sch, uch, ACTF.Sign, bias=chalf[:, 0:1]))
                    ve.wait_ge(sA.h, sA.n)
                    upd[ki] = sV.inc(ve.scalar_tensor_tensor(
                        uch, sch, kpow, uch, op0=ALU.mult, op1=ALU.add))
            for ki in pair:
                i = ki % 4
                p2 = ki % 2
                sgnwar[i] = upd[ki]
                wch = gvb[:, base + ki * W6: base + (ki + 1) * W6]
                ve.wait_ge(sV.h, upd[ki])
                uwar[i] = sV.inc(ve.scalar_tensor_tensor(
                    wch, ubuf[:, i * W6:(i + 1) * W6], -1.0, wch,
                    op0=ALU.mult, op1=ALU.add))
                ac.wait_ge(sV.h, uwar[i])
                if scr_tok[p2]:
                    ac.wait_ge(sP6[p2].h, scr_tok[p2])
                ct = sA.inc(ac.copy(scr16[:, p2 * W6:(p2 + 1) * W6], wch))
                gp.wait_ge(sA.h, ct)
                scr_tok[p2] = sP6[p2].inc(gp.dma_start(
                    out=x2i_sp[qc][:, ki * W6:(ki + 1) * W6],
                    in_=scr16[:, p2 * W6:(p2 + 1) * W6]), 16)
                slot_free[slot][ki] = (p2, scr_tok[p2])
            yield

    assert c.QG == 2 and HSL % 4 == 0 and c.NHW == HSL
    p6a_pos = [3 * HSL // 16, 5 * HSL // 16, 9 * HSL // 16, 11 * HSL // 16]
    sy.wait_ge(sT.h, b5_t)
    sy.wait_ge(sV.h, b5_v)
    sy.wait_ge(sA.h, b5_a)
    sy.wait_ge(sP6[0].h, sP6[0].n)
    sy.wait_ge(sP6[1].h, sP6[1].n)
    te.wait_ge(sV.h, b5_v)
    te.wait_ge(sA.h, b5_a)
    gp.wait_ge(sV.h, b5_v)
    p6aA(0)
    p6aB(0, deep=True)
    p6aA(1)
    p6aB(1, deep=True)
    for qg in range(c.NQG):
        pairs = []
        if qg > 0:
            for q in (qg * 2 - 2, qg * 2 - 1):
                qA(q)
                pairs.append(pair_gen(q))
        nxt = [qg * 2 + 2, qg * 2 + 3]
        pg = produce_gen(qg)
        for s in range(HSL):
            if pairs:
                if s % 4 == 0:
                    next(pairs[0], None)
                elif s % 4 == 2:
                    next(pairs[1], None)
            if s == p6a_pos[0] and nxt[0] < QT:
                p6aA(nxt[0])
            if s == p6a_pos[1] and nxt[0] < QT:
                p6aB(nxt[0])
            if s == p6a_pos[2] and nxt[1] < QT:
                p6aA(nxt[1])
            if s == p6a_pos[3] and nxt[1] < QT:
                p6aB(nxt[1])
            next(pg)
        for g_ in pairs:
            for _ in g_:
                pass
    for q in (QT - 2, QT - 1):
        qA(q)
    for _ in zip(pair_gen(QT - 2), pair_gen(QT - 1)):
        pass
    free("gvb", "dsc", "c1285", "cm1275", "scr16", "ubuf", "sgn", "vsl",
         "gsl", "vbb", "gbb", "wvsl", "wgsl",
         "xit", "rs1", "ft", "xiT")
    pfree("ptq", "vps2", "gps2", "pt16")
    barrier()

    wo3 = alloc("wo3", [128, HC * D], FP8)
    xb2 = alloc("xb2", [128, 2 * HC * 128], BF16)
    o3sb = alloc("o3sb", [128, D])
    x1b = alloc("x1b", [128, D])
    z4t = alloc("z4t", [128, D])
    obb = alloc("obb", [128, D])
    csb = alloc("csb", [128, D])
    sL.inc(sy.dma_start(out=obb[:], in_=obbt_hbm[:]), 16)
    NW7 = max(1, HC // 16)
    for k7 in range(NW7):
        w7 = HC * D // NW7
        sL.inc(sy.dma_start(out=wo3[:, k7 * w7:(k7 + 1) * w7],
                            in_=woi_sp[:, k7 * w7:(k7 + 1) * w7]), 16)
    te.wait_ge(sL.h, sL.n)
    cs_ps = palloc("cs_ps", [1, D])
    for hc in range(HC):
        for dsl_ in range(DSL):
            sT.inc(te.matmul(cs_ps[:1, dsl_ * SLAB:(dsl_ + 1) * SLAB], ones16[:],
                             wo3[:, hc * D + dsl_ * SLAB: hc * D + (dsl_ + 1) * SLAB],
                             start=(hc == 0), stop=(hc == HC - 1)))
    ve.wait_ge(sT.h, sT.n)
    ve.tensor_scalar(csb[:1, :], cs_ps[:1, :], 128.0, None, op0=ALU.mult)
    sV.inc(ve.drain())
    pfree("cs_ps")
    bc_ps7 = palloc("bc_ps7", [128, 512])
    te.wait_ge(sV.h, sV.n)
    bcast(csb, csb[:1, :], D, bc_ps7)
    pfree("bc_ps7")
    o3ps = palloc("o3ps", [128, 2048])
    te.wait_ge(sA.h, sA.n)
    xu = [0, 0]
    ev7 = [0, 0]
    ld7 = [0, 0]
    ld7[0] = sX[0].inc(sy.dma_start(out=xb2[:, :HC * 128], in_=x2i_sp[0]), 16)
    for qc in range(QT):
        b7 = qc % 2
        xbb = xb2[:, b7 * HC * 128:(b7 + 1) * HC * 128]
        sy.wait_ge(sV.h, sV.n)
        lx7 = sK.inc(sy.dma_start(out=x1b[:], in_=x1_sp[qc * 128:(qc + 1) * 128, :]), 16)
        if qc + 1 < QT:
            b2 = (qc + 1) % 2
            sy.wait_ge(sT.h, xu[b2])
            ld7[b2] = sX[b2].inc(sy.dma_start(
                out=xb2[:, b2 * HC * 128:(b2 + 1) * HC * 128], in_=x2i_sp[qc + 1]), 16)
        te.wait_ge(sX[b7].h, ld7[b7])
        for dg in range((DSL + 1) // 2):
            dss = range(2 * dg, min(2 * dg + 2, DSL))
            te.wait_ge(sV.h, ev7[dg % 2])
            for hc in range(HC):
                for ds in dss:
                    sT.inc(te.matmul(o3ps[:, ds * SLAB:(ds + 1) * SLAB],
                                     xbb[:, hc * 128:(hc + 1) * 128],
                                     wo3[:, hc * D + ds * SLAB: hc * D + (ds + 1) * SLAB],
                                     start=(hc == 0), stop=(hc == HC - 1)))
            xu[b7] = sT.n
            ve.wait_ge(sT.h, sT.n)
            ve.wait_ge(sA.h, sA.n)
            for ds in dss:
                ve.tensor_tensor(o3sb[:, ds * SLAB:(ds + 1) * SLAB],
                                 o3ps[:, ds * SLAB:(ds + 1) * SLAB],
                                 csb[:, ds * SLAB:(ds + 1) * SLAB], op=ALU.subtract)
                ve.drain()
                ve.scalar_tensor_tensor(o3sb[:, ds * SLAB:(ds + 1) * SLAB],
                                        o3sb[:, ds * SLAB:(ds + 1) * SLAB],
                                        sc3[:, qc:qc + 1], obb[:, ds * SLAB:(ds + 1) * SLAB],
                                        op0=ALU.mult, op1=ALU.add)
                ve.drain()
            ev7[dg % 2] = sV.inc(ve.drain())
        ln_stats(o3sb[:], D)
        ln_rstd()
        ac.wait_ge(sO.h, sO.n)
        ln_apply(z4t[:], o3sb[:])
        ve.wait_ge(sA.h, sA.n)
        ve.wait_ge(sK.h, lx7)
        ve.tensor_tensor(z4t[:], z4t[:], x1b[:], op=ALU.add)
        sV.inc(ve.drain())
        sy.wait_ge(sV.h, sV.n)
        sO.inc(sy.dma_start(out=out_hbm[qc * 128:(qc + 1) * 128, :], in_=z4t[:]), 16)
    sy.wait_ge(sO.h, sO.n)
    barrier()

    for cm in reversed(list(live.values())):
        cm.__exit__(None, None, None)
    for cm in reversed(list(psum_live.values())):
        cm.__exit__(None, None, None)
    for cm in reversed(sem_cms):
        cm.__exit__(None, None, None)
    return nc


def _bf16():
    import ml_dtypes
    return ml_dtypes.bfloat16


def reshape_blocked(w, out_blk=128):
    Dd, N = w.shape
    dcn = Dd // 128
    nb = N // out_blk
    return np.ascontiguousarray(
        w.reshape(dcn, 128, nb, out_blk).transpose(1, 2, 0, 3).reshape(128, -1))


def make_core_inputs(cfg, x_b, parity, shared):
    own = np.ascontiguousarray(x_b[parity::2])
    peer = np.ascontiguousarray(x_b[1 - parity::2])
    xr = np.concatenate([own, peer], 0)
    ii = np.arange(128)[:, None]
    jj = np.arange(128)[None, :]
    mask_own = np.where(ii <= jj, 0.0, NEG_MASK).astype(np.float32)
    if parity == 0:
        mask_peer = np.where(ii < jj, 0.0, NEG_MASK).astype(np.float32)
    else:
        mask_peer = np.where(ii <= jj, 0.0, NEG_MASK).astype(np.float32)
    d = {"x": xr, "mask_own": mask_own, "mask_peer": mask_peer}
    d.update(shared)
    return d


def kernel(L0_attn_x, L0_attn_qkv, L0_attn_o_proj, L0_attn_gamma, L0_attn_beta,
           L0_ffn_gate_w, L0_ffn_gate_b, L0_ffn_val_w, L0_ffn_val_b,
           L0_ffn_out_w, L0_ffn_out_b):
    cfg = Cfg(D=2048, S=2048)
    x = np.asarray(L0_attn_x, dtype=np.float32)
    qkv_f = np.asarray(L0_attn_qkv, dtype=np.float32)
    wo_f = np.asarray(L0_attn_o_proj, dtype=np.float32)
    shared = {
        "qkv": np.ascontiguousarray(qkv_f),
        "qkvc": reshape_blocked(qkv_f, 128),
        "wob": reshape_blocked(wo_f, 512),
        "gamma": np.asarray(L0_attn_gamma, dtype=np.float32),
        "beta": np.asarray(L0_attn_beta, dtype=np.float32),
        "gwT": np.ascontiguousarray(np.asarray(L0_ffn_gate_w, dtype=np.float32).T),
        "vwT": np.ascontiguousarray(np.asarray(L0_ffn_val_w, dtype=np.float32).T),
        "owT": np.ascontiguousarray(np.asarray(L0_ffn_out_w, dtype=np.float32).T),
        "gate_b": np.asarray(L0_ffn_gate_b, dtype=np.float32),
        "gbbt": np.ascontiguousarray(np.broadcast_to(
            np.asarray(L0_ffn_gate_b, dtype=np.float32), (128, len(L0_ffn_gate_b)))),
        "vbbt": np.ascontiguousarray(np.broadcast_to(
            np.asarray(L0_ffn_val_b, dtype=np.float32), (128, len(L0_ffn_val_b)))),
        "obbt": np.ascontiguousarray(np.broadcast_to(
            np.asarray(L0_ffn_out_b, dtype=np.float32), (128, len(L0_ffn_out_b)))),
        "val_b": np.asarray(L0_ffn_val_b, dtype=np.float32),
        "out_b": np.asarray(L0_ffn_out_b, dtype=np.float32),
        "eye32": np.eye(128, dtype=np.float32),
        "eye16": np.eye(128, dtype=np.float32).astype(_bf16()),
    }
    nc = build_program(cfg)
    in_maps = []
    for core in range(8):
        b, p = divmod(core, 2)
        in_maps.append(make_core_inputs(cfg, x[b], p, shared))
    res = run_bass_kernel_spmd(nc, in_maps, core_ids=list(range(8)))
    out = np.empty((4, cfg.S, cfg.D), dtype=np.float32)
    for core in range(8):
        b, p = divmod(core, 2)
        out[b, p::2] = res.results[core]["out"]
    return out
